# version 03
# baseline (speedup 1.0000x reference)
import numpy as np
import contextlib
import concourse.bass as bass
import concourse.mybir as mybir
from concourse.bass_utils import run_bass_kernel_spmd

F32 = mybir.dt.float32
BF16 = mybir.dt.bfloat16
AF = mybir.ActivationFunctionType
ALU = mybir.AluOpType
AX = mybir.AxisListType

P = 128
SPC = 2
SEQ = 2048
DM = 1024
NDC = DM // P
INNER = 4096
NIC = INNER // P
TA = 128
NCH = SEQ // TA
TS = 64
NSUB = TA // TS
NST = 16
DTR = 64
EPS = 1e-5
C32 = 1.5 * 2.0**23
C64 = 1.5 * 2.0**52
KMUL = float(np.float32((C64 - C32) * (2.0**40)))
NTT = SPC * NCH


class Buf:
    __slots__ = ("ap", "writer", "readers")

    def __init__(self, ap):
        self.ap = ap
        self.writer = None
        self.readers = []


NDSEM = 32


class Sched:
    ENG = ("t", "s", "v", "g")
    STREAMS = ("d", "t", "s", "v", "g")

    def __init__(self):
        self.ops = {e: [] for e in self.STREAMS}
        self.cnt = {e: 0 for e in self.ENG}
        self.dcnt = [0] * NDSEM
        self.dnext = 0
        self.vc = {e: {} for e in self.STREAMS}
        self.snap = {}

    def _deps(self, reads, writes):
        deps = []
        for b in reads:
            if b.writer is not None:
                deps.append(b.writer)
        for b in writes:
            for tok in b.readers:
                deps.append(tok)
            if b.writer is not None:
                deps.append(b.writer)
        return deps

    def _mkwaits(self, ek, deps):
        vc = self.vc[ek]
        waits = []
        for (dk, val) in deps:
            if vc.get(dk, 0) >= val:
                continue
            waits.append((dk, val))
            psnap = self.snap.get((dk, val))
            if psnap:
                for k2, v2 in psnap.items():
                    if v2 > vc.get(k2, 0):
                        vc[k2] = v2
            if val > vc.get(dk, 0):
                vc[dk] = val
        return waits

    def add(self, ek, buildfn, reads=(), writes=(), inc=True):
        deps = self._deps(reads, writes)
        waits = self._mkwaits(ek, deps)
        if inc:
            self.cnt[ek] += 1
        mytok = (ek, self.cnt[ek])
        if inc:
            s = dict(self.vc[ek])
            s[ek] = self.cnt[ek]
            self.snap[mytok] = s
        self.ops[ek].append((waits, buildfn, inc, None))
        for b in reads:
            b.readers.append(mytok)
        for b in writes:
            if inc:
                b.writer = mytok
                b.readers = []
        return mytok

    def add_dma(self, buildfn, reads=(), writes=()):
        deps = self._deps(reads, writes)
        j = self.dnext
        self.dnext = (j + 1) % NDSEM
        key = f"d{j}"
        if self.dcnt[j] > 0:
            deps.append((key, 16 * self.dcnt[j]))
        waits = self._mkwaits("d", deps)
        self.dcnt[j] += 1
        mytok = (key, 16 * self.dcnt[j])
        s = dict(self.vc["d"])
        s[key] = 16 * self.dcnt[j]
        self.snap[mytok] = s
        self.ops["d"].append((waits, buildfn, True, key))
        for b in reads:
            b.readers.append(mytok)
        for b in writes:
            b.writer = mytok
            b.readers = []
        return mytok

    def _snapshot(self):
        snap = {e: self.cnt[e] for e in self.ENG}
        for j in range(NDSEM):
            if self.dcnt[j]:
                snap[f"d{j}"] = 16 * self.dcnt[j]
        return snap

    def barrier(self):
        snap = self._snapshot()
        for ek in self.STREAMS:
            deps = [(dk, val) for dk, val in snap.items() if dk != ek]
            waits = self._mkwaits(ek, deps)
            if waits:
                self.ops[ek].append((waits, None, False, None))

    def emit(self, block, sems):
        for ek, attr in (("d", "sync"), ("t", "tensor"), ("s", "scalar"),
                         ("v", "vector"), ("g", "gpsimd")):
            oplist = self.ops[ek]
            if not oplist:
                continue

            def section(eng, _ops=oplist, _ek=ek):
                for (waits, buildfn, inc, dkey) in _ops:
                    for (dk, val) in waits:
                        eng.wait_ge(sems[dk], val)
                    if buildfn is None:
                        continue
                    inst = buildfn(eng)
                    if inc:
                        if dkey is not None:
                            inst.then_inc(sems[dkey], 16)
                        else:
                            inst.then_inc(sems[_ek], 1)

            getattr(block, attr)(section)
            self.ops[ek] = []


def ln_stats(sc, stat, n, epsc=None):
    inv = 1.0 / n
    sc.add("v", lambda e: e.tensor_scalar(stat.ap[:, 2:3], stat.ap[:, 0:1],
                                          -inv, None, ALU.mult),
           reads=[stat], writes=[stat])
    sc.add("g", lambda e: e.tensor_tensor(stat.ap[:, 3:4], stat.ap[:, 2:3],
                                          stat.ap[:, 2:3], ALU.mult),
           reads=[stat], writes=[stat])
    sc.add("g", lambda e: e.tensor_scalar(stat.ap[:, 1:2], stat.ap[:, 1:2],
                                          inv, 0.0, ALU.mult, ALU.add),
           reads=[stat], writes=[stat])
    sc.add("g", lambda e: e.tensor_tensor(stat.ap[:, 3:4], stat.ap[:, 1:2],
                                          stat.ap[:, 3:4], ALU.subtract),
           reads=[stat], writes=[stat])
    sc.add("s", lambda e: e.activation(stat.ap[:, 3:4], stat.ap[:, 3:4], AF.Sqrt,
                                       bias=epsc.ap[:, 0:1], scale=1.0),
           reads=[stat, epsc], writes=[stat])
    sc.add("v", lambda e: e.reciprocal(stat.ap[:, 4:5], stat.ap[:, 3:4]),
           reads=[stat], writes=[stat])


def softplus(sc, xb, spb):
    w = xb.ap.shape[1]
    sp = spb.ap[:, 0:w]
    sc.add("s", lambda e: e.activation(sp, xb.ap, AF.Abs, bias=0.0, scale=1.0),
           reads=[xb], writes=[spb])
    sc.add("s", lambda e: e.activation(sp, sp, AF.Exp, bias=0.0, scale=-1.0),
           reads=[spb], writes=[spb])
    sc.add("s", lambda e: e.activation(sp, sp, AF.Ln, bias=1.0, scale=1.0),
           reads=[spb], writes=[spb])
    sc.add("s", lambda e: e.activation(xb.ap, xb.ap, AF.Relu, bias=0.0, scale=1.0),
           reads=[xb], writes=[xb])
    sc.add("g", lambda e: e.tensor_tensor(xb.ap, xb.ap, sp, ALU.add),
           reads=[xb, spb], writes=[xb])


def build_nc():
    nc = bass.Bass("TRN2", target_bir_lowering=False)
    h = {}
    h["x"] = nc.dram_tensor("x", [SPC, SEQ, DM], F32, kind="ExternalInput").ap()
    for nm, shape in [("wpT", [DM, DM]), ("wfT", [DM, DM]), ("wbT", [DM, DM]),
                      ("wdbcT", [DM, 96]), ("wdtT", [DTR, DM]),
                      ("wgT", [DM, INNER]), ("wuT", [DM, INNER]), ("wdT", [INNER, DM])]:
        h[nm] = nc.dram_tensor(nm, shape, F32, kind="ExternalInput").ap()
    for nm in ("A_log8", "gamma8", "beta8", "bp8", "bf8", "bb8", "dtb8", "D8"):
        cols = NDC * NST if nm == "A_log8" else NDC
        h[nm] = nc.dram_tensor(nm, [P, cols], F32, kind="ExternalInput").ap()
    h["ident"] = nc.dram_tensor("ident", [P, P], F32, kind="ExternalInput").ap()
    h["identb"] = nc.dram_tensor("identb", [P, P], BF16, kind="ExternalInput").ap()
    h["ones"] = nc.dram_tensor("ones", [P, P], F32, kind="ExternalInput").ap()
    h["out"] = nc.dram_tensor("out", [SPC * SEQ, DM], F32, kind="ExternalOutput").ap()
    h["x2scr"] = nc.dram_tensor("x2scr", [SPC * SEQ, DM], F32, kind="Internal").ap()
    h["xq1scr"] = nc.dram_tensor("xq1scr", [DM, SPC * SEQ], BF16, kind="Internal").ap()
    h["xq2scr"] = nc.dram_tensor("xq2scr", [INNER, SPC * SEQ], BF16, kind="Internal").ap()
    bscr = [Buf(nc.dram_tensor(f"bscr{i}", [NST, TS], F32, kind="Internal").ap())
            for i in range(2)]
    cscr = [Buf(nc.dram_tensor(f"cscr{i}", [NST, TS], F32, kind="Internal").ap())
            for i in range(2)]
    x2scrB = Buf(h["x2scr"])
    xq1scrB = Buf(h["xq1scr"])
    xq2scrB = Buf(h["xq2scr"])
    outB = Buf(h["out"])

    sc = Sched()

    with contextlib.ExitStack() as main_stk:
        sems = {ek: main_stk.enter_context(nc.semaphore(f"S{ek}")) for ek in Sched.ENG}
        for j in range(NDSEM):
            sems[f"d{j}"] = main_stk.enter_context(nc.semaphore(f"Sd{j}"))

        def sb(stk, name, shape, dt=F32):
            return Buf(stk.enter_context(nc.sbuf_tensor(name, shape, dt)).ap())

        def psb(stk, name, shape, dt=F32):
            return Buf(stk.enter_context(nc.psum_tensor(name, shape, dt)).ap())

        def dma(dst, src, reads=(), writes=()):
            return sc.add_dma(lambda e, d=dst, s=src: e.dma_start(out=d, in_=s),
                              reads=reads, writes=writes)

        ident = sb(main_stk, "identt", [P, P])
        identb = sb(main_stk, "identbt", [P, P], BF16)
        ones = sb(main_stk, "onest", [P, P])
        A_sb = sb(main_stk, "A_sb", [P, NDC * NST])
        D_sb = sb(main_stk, "D_sb", [P, NDC])
        dtb = sb(main_stk, "dtb", [P, NDC])
        bp = sb(main_stk, "bp", [P, NDC])
        bfc = sb(main_stk, "bfc", [P, NDC])
        bbc = sb(main_stk, "bbc", [P, NDC])
        gam = sb(main_stk, "gam", [P, NDC])
        bet = sb(main_stk, "bet", [P, NDC])
        bfold = sb(main_stk, "bfold", [P, NDC])
        ceff = sb(main_stk, "ceff", [P, 2])
        tmpc = sb(main_stk, "tmpc", [P, 2])
        mx1c = sb(main_stk, "mx1c", [P, NTT])
        mx2c = sb(main_stk, "mx2c", [P, NTT])
        stat = sb(main_stk, "stat", [P, 8])
        sc1 = sb(main_stk, "sc1", [P, NTT])
        mgs = sb(main_stk, "mgs", [P, 4])
        colt = sb(main_stk, "colt", [P, 4])
        epsc = sb(main_stk, "epsc", [P, 1])

        def weight_quant(wh, wq, wtmp, wqp, mcol, nkc, width, psw):
            nelem = float(nkc * P * width)
            for kc in range(nkc):
                wt = wtmp[kc % 2]
                dma(wt.ap, wh[kc * P:(kc + 1) * P, :], writes=[wt])
                sc.add("v", lambda e, wt=wt: e.tensor_reduce(
                    colt.ap[:, 0:1], wt.ap, AX.X, ALU.add, apply_absolute_value=True),
                    reads=[wt], writes=[colt])
                if kc == 0:
                    sc.add("g", lambda e: e.tensor_copy(colt.ap[:, 1:2], colt.ap[:, 0:1]),
                           reads=[colt], writes=[colt])
                else:
                    sc.add("g", lambda e: e.tensor_tensor(
                        colt.ap[:, 1:2], colt.ap[:, 1:2], colt.ap[:, 0:1], ALU.add),
                        reads=[colt], writes=[colt])
            sc.add("t", lambda e: e.matmul(psw.ap[:, 0:1], ones.ap, colt.ap[:, 1:2],
                                           start=True, stop=True),
                   reads=[ones, colt], writes=[psw])
            sc.add("s", lambda e: e.copy(colt.ap[:, 2:3], psw.ap[:, 0:1]),
                   reads=[psw], writes=[colt])
            sc.add("g", lambda e: e.tensor_scalar(colt.ap[:, 2:3], colt.ap[:, 2:3],
                                                  1.0 / nelem, 0.0, ALU.mult, ALU.add),
                   reads=[colt], writes=[colt])
            sc.add("g", lambda e: e.tensor_scalar(colt.ap[:, 3:4], colt.ap[:, 2:3],
                                                  3.0e38, EPS, ALU.min, ALU.max),
                   reads=[colt], writes=[colt])
            sc.add("g", lambda e: e.tensor_scalar(
                mgs.ap[:, mcol:mcol + 1], colt.ap[:, 3:4], 1.0 / 127.0, 0.0,
                ALU.mult, ALU.add), reads=[colt], writes=[mgs])
            sc.add("v", lambda e: e.reciprocal(colt.ap[:, 0:1], colt.ap[:, 3:4]),
                   reads=[colt], writes=[colt])
            for kc in range(nkc):
                wt = wtmp[kc % 2]
                dma(wt.ap, wh[kc * P:(kc + 1) * P, :], writes=[wt])
                sc.add("s", lambda e, wt=wt: e.activation(
                    wqp.ap, wt.ap, AF.Identity, bias=0.0, scale=colt.ap[:, 0:1]),
                    reads=[wt, colt], writes=[wqp])
                sc.add("v", lambda e: e.tensor_scalar(
                    wqp.ap, wqp.ap, ceff.ap[:, 0:1], ceff.ap[:, 0:1],
                    ALU.add, ALU.subtract), reads=[wqp, ceff], writes=[wqp])
                sc.add("g", lambda e, kc=kc: e.tensor_scalar(
                    wq.ap[:, kc * width:(kc + 1) * width], wqp.ap, 1.0, -1.0,
                    ALU.min, ALU.max), reads=[wqp], writes=[wq])

        with contextlib.ExitStack() as stk0:
            wpT = sb(stk0, "wpT_s", [P, NDC * DM])
            wfT = sb(stk0, "wfT_s", [P, NDC * DM])
            wbT = sb(stk0, "wbT_s", [P, NDC * DM])
            wdbcT = sb(stk0, "wdbcT_s", [P, NDC * 96])
            wdtT = sb(stk0, "wdtT_s", [DTR, NDC * P])

            for nm, b in [("ident", ident), ("identb", identb), ("ones", ones),
                          ("A_log8", A_sb), ("D8", D_sb), ("dtb8", dtb),
                          ("bp8", bp), ("bf8", bfc), ("bb8", bbc),
                          ("gamma8", gam), ("beta8", bet)]:
                dma(b.ap, h[nm], writes=[b])
            for dc in range(NDC):
                dma(wpT.ap[:, dc * DM:(dc + 1) * DM], h["wpT"][dc * P:(dc + 1) * P, :], writes=[wpT])
                dma(wfT.ap[:, dc * DM:(dc + 1) * DM], h["wfT"][dc * P:(dc + 1) * P, :], writes=[wfT])
                dma(wbT.ap[:, dc * DM:(dc + 1) * DM], h["wbT"][dc * P:(dc + 1) * P, :], writes=[wbT])
                dma(wdbcT.ap[:, dc * 96:(dc + 1) * 96], h["wdbcT"][dc * P:(dc + 1) * P, :], writes=[wdbcT])
                dma(wdtT.ap[:, dc * P:(dc + 1) * P], h["wdtT"][:, dc * P:(dc + 1) * P], writes=[wdtT])

            sc.add("v", lambda e: e.tensor_scalar(tmpc.ap, ones.ap[:, 0:2], 1.0, None,
                                                  ALU.mult), reads=[ones], writes=[tmpc])
            sc.add("g", lambda e: e.tensor_scalar(tmpc.ap, tmpc.ap, 2.0**-40, 1.0,
                                                  ALU.add, ALU.mult),
                   reads=[tmpc], writes=[tmpc])
            sc.add("v", lambda e: e.tensor_scalar(tmpc.ap, tmpc.ap, -1.0, None, ALU.add),
                   reads=[tmpc], writes=[tmpc])
            sc.add("g", lambda e: e.tensor_scalar(ceff.ap, tmpc.ap, KMUL, C32,
                                                  ALU.mult, ALU.add),
                   reads=[tmpc], writes=[ceff])

            sc.add("g", lambda e: e.tensor_scalar(epsc.ap, ones.ap[:, 0:1], EPS, 0.0,
                                                  ALU.mult, ALU.add),
                   reads=[ones], writes=[epsc])
            sc.add("s", lambda e: e.activation(A_sb.ap, A_sb.ap, AF.Exp,
                                               bias=0.0, scale=1.0),
                   reads=[A_sb], writes=[A_sb])
            sc.add("g", lambda e: e.tensor_scalar(A_sb.ap, A_sb.ap, -1.0, 0.0,
                                                  ALU.mult, ALU.add),
                   reads=[A_sb], writes=[A_sb])

            with contextlib.ExitStack() as stkA:
                x_sb = sb(stkA, "x_sb", [P, DM])
                xn_sb = sb(stkA, "xn_sb", [P, DM])
                xpT = sb(stkA, "xpT", [P, NDC * TA])
                uT = sb(stkA, "uT", [P, NDC * TA])
                deltaT = sb(stkA, "deltaT", [P, NDC * TA])
                duT = sb(stkA, "duT", [P, NDC * TA])
                ysum = sb(stkA, "ysum", [P, NDC * TA])
                sp = sb(stkA, "sp", [P, DM])
                dbc = sb(stkA, "dbc", [96, TA])
                brep = [sb(stkA, f"brep{i}", [P, NST * TS]) for i in range(2)]
                crep = [sb(stkA, f"crep{i}", [P, NST * TS]) for i in range(2)]
                ebuf = [sb(stkA, f"ebuf{i}", [P, NST * (TS + 1)]) for i in range(2)]
                bbuf = [sb(stkA, f"bbuf{i}", [P, NST * (TS + 1)]) for i in range(2)]
                hbuf = [sb(stkA, f"hbuf{i}", [P, NST * (TS + 1)]) for i in range(2)]
                hcar = sb(stkA, "hcar", [P, 2 * NDC * NST])
                ysc = sb(stkA, "ysc", [P, TS])
                xq1b = sb(stkA, "xq1b", [P, DM], BF16)
                xq1T = sb(stkA, "xq1T", [P, NDC * TA], BF16)
                psA = psb(stkA, "psA", [P, 512])
                psA2 = psb(stkA, "psA2", [P, 512])
                psDT = psb(stkA, "psDT", [P, 512])
                psT0 = psb(stkA, "psT0", [P, 512])
                psT1 = psb(stkA, "psT1", [P, 512])
                psDB = psb(stkA, "psDB", [96, TA])
                psTB0 = psb(stkA, "psTB0", [P, 512], BF16)
                psTB1 = psb(stkA, "psTB1", [P, 512], BF16)
                psmm = [psA, psA2]

                def e3(b):
                    return b.ap.rearrange("p (n t) -> p n t", n=NST)

                for eb in ebuf:
                    sc.add("g", lambda e, eb=eb: e.memset(e3(eb)[:, :, 0:1], 0.0),
                           writes=[eb])

                for oc in range(NDC):
                    for dc in range(NDC):
                        sc.add("t", lambda e, oc=oc, dc=dc: e.matmul(
                            psA.ap[:, 0:1],
                            wpT.ap[:, dc * DM + oc * P: dc * DM + (oc + 1) * P],
                            bet.ap[:, dc:dc + 1],
                            start=(dc == 0), stop=(dc == NDC - 1)),
                            reads=[wpT, bet], writes=[psA], inc=(dc == NDC - 1))
                    sc.add("s", lambda e, oc=oc: e.activation(
                        bfold.ap[:, oc:oc + 1], psA.ap[:, 0:1], AF.Identity,
                        bias=bp.ap[:, oc:oc + 1], scale=1.0),
                        reads=[psA, bp], writes=[bfold])
                for dc in range(NDC):
                    sc.add("v", lambda e, dc=dc: e.tensor_scalar(
                        wpT.ap[:, dc * DM:(dc + 1) * DM], wpT.ap[:, dc * DM:(dc + 1) * DM],
                        gam.ap[:, dc:dc + 1], 1.0, ALU.mult, ALU.mult),
                        reads=[wpT, gam], writes=[wpT])

                for s in range(SPC):
                    sc.add("g", lambda e: e.memset(hcar.ap, 0.0), writes=[hcar])
                    for c in range(NCH):
                        ti = s * NCH + c
                        t0 = c * TA
                        dma(x_sb.ap, h["x"][s, t0:t0 + TA, :], writes=[x_sb])
                        sc.add("s", lambda e: e.activation(
                            sp.ap, x_sb.ap, AF.Identity, bias=0.0, scale=1.0,
                            accum_out=stat.ap[:, 0:1]), reads=[x_sb], writes=[sp, stat])
                        sc.add("s", lambda e: e.activation(
                            sp.ap, x_sb.ap, AF.Square, bias=0.0, scale=1.0,
                            accum_out=stat.ap[:, 1:2]), reads=[x_sb], writes=[sp, stat])
                        ln_stats(sc, stat, DM, epsc)
                        sc.add("g", lambda e: e.tensor_scalar(
                            xn_sb.ap, x_sb.ap, stat.ap[:, 2:3], stat.ap[:, 4:5],
                            ALU.add, ALU.mult), reads=[x_sb, stat], writes=[xn_sb])
                        for half in range(2):
                            pst = [psT0, psT1][half]
                            for j in range(4):
                                dc = half * 4 + j
                                sc.add("t", lambda e, dc=dc, j=j, pst=pst: e.transpose(
                                    pst.ap[:, j * P:(j + 1) * P],
                                    xn_sb.ap[:, dc * P:(dc + 1) * P], ident.ap),
                                    reads=[xn_sb, ident], writes=[pst], inc=(j == 3))
                            sc.add("s", lambda e, half=half, pst=pst: e.copy(
                                sp.ap[:, half * 512:(half + 1) * 512], pst.ap),
                                reads=[pst], writes=[sp])
                        for oc in range(NDC):
                            pm = psmm[oc % 2]
                            for dc in range(NDC):
                                sc.add("t", lambda e, oc=oc, dc=dc, pm=pm: e.matmul(
                                    pm.ap[:, 0:TA],
                                    wpT.ap[:, dc * DM + oc * P: dc * DM + (oc + 1) * P],
                                    sp.ap[:, dc * P:(dc + 1) * P],
                                    start=(dc == 0), stop=(dc == NDC - 1)),
                                    reads=[wpT, sp], writes=[pm], inc=(dc == NDC - 1))
                            sc.add("s", lambda e, oc=oc, pm=pm: e.activation(
                                xpT.ap[:, oc * TA:(oc + 1) * TA], pm.ap[:, 0:TA],
                                AF.Identity, bias=bfold.ap[:, oc:oc + 1], scale=1.0),
                                reads=[pm, bfold], writes=[xpT])

                        for br, (wb, bb_) in enumerate([(wfT, bfc), (wbT, bbc)]):
                            for oc in range(NDC):
                                pm = psmm[oc % 2]
                                for dc in range(NDC):
                                    sc.add("t", lambda e, oc=oc, dc=dc, pm=pm, wb=wb: e.matmul(
                                        pm.ap[:, 0:TA],
                                        wb.ap[:, dc * DM + oc * P: dc * DM + (oc + 1) * P],
                                        xpT.ap[:, dc * TA:(dc + 1) * TA],
                                        start=(dc == 0), stop=(dc == NDC - 1)),
                                        reads=[wb, xpT], writes=[pm], inc=(dc == NDC - 1))
                                sc.add("s", lambda e, oc=oc, pm=pm, bb_=bb_: e.activation(
                                    uT.ap[:, oc * TA:(oc + 1) * TA], pm.ap[:, 0:TA],
                                    AF.Identity, bias=bb_.ap[:, oc:oc + 1], scale=1.0),
                                    reads=[pm, bb_], writes=[uT])
                            softplus(sc, uT, sp)
                            for dc in range(NDC):
                                sc.add("t", lambda e, dc=dc: e.matmul(
                                    psDB.ap,
                                    wdbcT.ap[:, dc * 96:(dc + 1) * 96],
                                    uT.ap[:, dc * TA:(dc + 1) * TA],
                                    start=(dc == 0), stop=(dc == NDC - 1)),
                                    reads=[wdbcT, uT], writes=[psDB], inc=(dc == NDC - 1))
                            sc.add("s", lambda e: e.copy(dbc.ap, psDB.ap),
                                   reads=[psDB], writes=[dbc])
                            for g4 in range(2):
                                for j in range(4):
                                    dc = g4 * 4 + j
                                    sc.add("t", lambda e, dc=dc, j=j: e.matmul(
                                        psDT.ap[:, j * TA:(j + 1) * TA],
                                        wdtT.ap[:, dc * P:(dc + 1) * P],
                                        dbc.ap[0:DTR, :],
                                        start=True, stop=True),
                                        reads=[wdtT, dbc], writes=[psDT], inc=(j == 3))
                                for j in range(4):
                                    dc = g4 * 4 + j
                                    sc.add("s", lambda e, dc=dc, j=j: e.activation(
                                        deltaT.ap[:, dc * TA:(dc + 1) * TA],
                                        psDT.ap[:, j * TA:(j + 1) * TA],
                                        AF.Identity, bias=dtb.ap[:, dc:dc + 1], scale=1.0),
                                        reads=[psDT, dtb], writes=[deltaT])
                            softplus(sc, deltaT, sp)
                            sc.add("g", lambda e: e.tensor_tensor(
                                duT.ap, deltaT.ap, uT.ap, ALU.mult),
                                reads=[deltaT, uT], writes=[duT])
                            if br == 0:
                                for dc in range(NDC):
                                    sc.add("g", lambda e, dc=dc: e.tensor_scalar(
                                        ysum.ap[:, dc * TA:(dc + 1) * TA],
                                        uT.ap[:, dc * TA:(dc + 1) * TA],
                                        D_sb.ap[:, dc:dc + 1], 0.0, ALU.mult, ALU.add),
                                        reads=[uT, D_sb], writes=[ysum])
                            else:
                                for dc in range(NDC):
                                    sc.add("g", lambda e, dc=dc: e.tensor_scalar(
                                        sp.ap[:, dc * P:(dc + 1) * P],
                                        uT.ap[:, dc * TA:(dc + 1) * TA],
                                        D_sb.ap[:, dc:dc + 1], 0.0, ALU.mult, ALU.add),
                                        reads=[uT, D_sb], writes=[sp])
                                sc.add("g", lambda e: e.tensor_tensor(
                                    ysum.ap, ysum.ap, sp.ap, ALU.add),
                                    reads=[ysum, sp], writes=[ysum])
                            for sub in range(NSUB):
                                pb = (c * NSUB + sub) % 2
                                ts0 = sub * TS
                                dma(bscr[pb].ap, dbc.ap[DTR:DTR + NST, ts0:ts0 + TS],
                                    reads=[dbc], writes=[bscr[pb]])
                                dma(cscr[pb].ap, dbc.ap[DTR + NST:96, ts0:ts0 + TS],
                                    reads=[dbc], writes=[cscr[pb]])
                                bflat = bscr[pb].ap.rearrange("n t -> (n t)") \
                                    .unsqueeze(0).broadcast_to([P, NST * TS])
                                cflat = cscr[pb].ap.rearrange("n t -> (n t)") \
                                    .unsqueeze(0).broadcast_to([P, NST * TS])
                                dma(brep[pb].ap, bflat, reads=[bscr[pb]], writes=[brep[pb]])
                                dma(crep[pb].ap, cflat, reads=[cscr[pb]], writes=[crep[pb]])
                                for dc in range(NDC):
                                    eb = ebuf[(dc + sub) % 2]
                                    bbf = bbuf[(dc + sub) % 2]
                                    hb = hbuf[(dc + sub) % 2]
                                    col0 = dc * TA + ts0
                                    hsl = slice((br * NDC + dc) * NST,
                                                (br * NDC + dc + 1) * NST)
                                    sc.add("g", lambda e, eb=eb, dc=dc, col0=col0: e.tensor_tensor(
                                        e3(eb)[:, :, 1:],
                                        deltaT.ap[:, col0:col0 + TS]
                                        .unsqueeze(1).broadcast_to([P, NST, TS]),
                                        A_sb.ap[:, dc * NST:(dc + 1) * NST]
                                        .unsqueeze(2).broadcast_to([P, NST, TS]),
                                        ALU.mult), reads=[deltaT, A_sb], writes=[eb])
                                    sc.add("s", lambda e, eb=eb: e.activation(
                                        e3(eb)[:, :, 1:], e3(eb)[:, :, 1:], AF.Exp,
                                        bias=0.0, scale=1.0), reads=[eb], writes=[eb])
                                    sc.add("g", lambda e, bbf=bbf, hsl=hsl: e.tensor_copy(
                                        e3(bbf)[:, :, 0:1], hcar.ap[:, hsl].unsqueeze(2)),
                                        reads=[hcar], writes=[bbf])
                                    sc.add("g", lambda e, bbf=bbf, col0=col0, pb=pb: e.tensor_tensor(
                                        e3(bbf)[:, :, 1:],
                                        duT.ap[:, col0:col0 + TS]
                                        .unsqueeze(1).broadcast_to([P, NST, TS]),
                                        brep[pb].ap.rearrange("p (n t) -> p n t", n=NST),
                                        ALU.mult), reads=[duT, brep[pb]], writes=[bbf])
                                    sc.add("v", lambda e, hb=hb, eb=eb, bbf=bbf: e.tensor_tensor_scan(
                                        hb.ap, eb.ap, bbf.ap, 0.0, ALU.mult, ALU.add),
                                        reads=[eb, bbf], writes=[hb])
                                    sc.add("g", lambda e, hb=hb, hsl=hsl: e.tensor_copy(
                                        hcar.ap[:, hsl], e3(hb)[:, :, TS:TS + 1].squeeze(2)),
                                        reads=[hb], writes=[hcar])
                                    sc.add("g", lambda e, eb=eb, hb=hb, pb=pb: e.tensor_tensor(
                                        e3(eb)[:, :, 1:], e3(hb)[:, :, 1:],
                                        crep[pb].ap.rearrange("p (n t) -> p n t", n=NST),
                                        ALU.mult), reads=[hb, crep[pb]], writes=[eb])
                                    sc.add("v", lambda e, eb=eb: e.tensor_reduce(
                                        ysc.ap, e3(eb)[:, :, 1:].transpose([0, 2, 1]),
                                        AX.X, ALU.add), reads=[eb], writes=[ysc])
                                    sc.add("g", lambda e, col0=col0: e.tensor_tensor(
                                        ysum.ap[:, col0:col0 + TS],
                                        ysum.ap[:, col0:col0 + TS],
                                        ysc.ap, ALU.add), reads=[ysc, ysum], writes=[ysum])
                        sc.add("s", lambda e: e.activation(xpT.ap, xpT.ap, AF.Silu,
                                                           bias=0.0, scale=1.0),
                               reads=[xpT], writes=[xpT])
                        sc.add("g", lambda e: e.tensor_tensor(
                            ysum.ap, ysum.ap, xpT.ap, ALU.mult),
                            reads=[ysum, xpT], writes=[ysum])
                        for half in range(2):
                            pst = [psT0, psT1][half]
                            for j in range(4):
                                dc = half * 4 + j
                                sc.add("t", lambda e, dc=dc, j=j, pst=pst: e.transpose(
                                    pst.ap[:, j * P:(j + 1) * P],
                                    ysum.ap[:, dc * TA:(dc + 1) * TA], ident.ap),
                                    reads=[ysum, ident], writes=[pst], inc=(j == 3))
                            sc.add("v", lambda e, half=half, pst=pst: e.scalar_tensor_tensor(
                                x_sb.ap[:, half * 512:(half + 1) * 512],
                                x_sb.ap[:, half * 512:(half + 1) * 512], 2.0, pst.ap,
                                ALU.mult, ALU.add), reads=[x_sb, pst], writes=[x_sb])
                        sc.add("s", lambda e: e.activation(
                            sp.ap, x_sb.ap, AF.Identity, bias=0.0, scale=1.0,
                            accum_out=stat.ap[:, 0:1]), reads=[x_sb], writes=[sp, stat])
                        sc.add("s", lambda e: e.activation(
                            sp.ap, x_sb.ap, AF.Square, bias=0.0, scale=1.0,
                            accum_out=stat.ap[:, 1:2]), reads=[x_sb], writes=[sp, stat])
                        ln_stats(sc, stat, DM, epsc)
                        sc.add("g", lambda e: e.tensor_scalar(
                            x_sb.ap, x_sb.ap, stat.ap[:, 2:3], stat.ap[:, 4:5],
                            ALU.add, ALU.mult), reads=[x_sb, stat], writes=[x_sb])
                        dma(h["x2scr"][ti * TA:(ti + 1) * TA, :], x_sb.ap,
                            reads=[x_sb], writes=[x2scrB])
                        sc.add("v", lambda e: e.tensor_reduce(
                            stat.ap[:, 5:6], x_sb.ap, AX.X, ALU.max,
                            apply_absolute_value=True), reads=[x_sb], writes=[stat])
                        sc.add("g", lambda e, ti=ti: e.tensor_scalar(
                            mx1c.ap[:, ti:ti + 1], stat.ap[:, 5:6], 3.0e38, EPS,
                            ALU.min, ALU.max), reads=[stat], writes=[mx1c])
                        sc.add("v", lambda e, ti=ti: e.reciprocal(
                            stat.ap[:, 6:7], mx1c.ap[:, ti:ti + 1]),
                            reads=[mx1c], writes=[stat])
                        sc.add("g", lambda e, ti=ti: e.tensor_scalar(
                            sc1.ap[:, ti:ti + 1], stat.ap[:, 6:7], 127.0, 0.0,
                            ALU.mult, ALU.add), reads=[stat], writes=[sc1])
                        sc.add("s", lambda e, ti=ti: e.activation(
                            sp.ap, x_sb.ap, AF.Identity, bias=0.0,
                            scale=sc1.ap[:, ti:ti + 1]), reads=[x_sb, sc1], writes=[sp])
                        sc.add("v", lambda e: e.tensor_scalar(
                            sp.ap, sp.ap, ceff.ap[:, 0:1], ceff.ap[:, 0:1],
                            ALU.add, ALU.subtract), reads=[sp, ceff], writes=[sp])
                        sc.add("g", lambda e: e.tensor_scalar(
                            xq1b.ap, sp.ap, 127.0, -128.0, ALU.min, ALU.max),
                            reads=[sp], writes=[xq1b])
                        for half in range(2):
                            pst = [psTB0, psTB1][half]
                            for j in range(4):
                                dc = half * 4 + j
                                sc.add("t", lambda e, dc=dc, j=j, pst=pst: e.transpose(
                                    pst.ap[:, j * P:(j + 1) * P],
                                    xq1b.ap[:, dc * P:(dc + 1) * P], identb.ap),
                                    reads=[xq1b, identb], writes=[pst], inc=(j == 3))
                            sc.add("s", lambda e, half=half, pst=pst: e.copy(
                                xq1T.ap[:, half * 512:(half + 1) * 512], pst.ap),
                                reads=[pst], writes=[xq1T])
                        dma(h["xq1scr"].rearrange("(dc p) t -> p dc t", p=P)
                            [:, :, ti * TA:(ti + 1) * TA],
                            xq1T.ap.rearrange("p (dc t) -> p dc t", dc=NDC),
                            reads=[xq1T], writes=[xq1scrB])

            sc.barrier()

        with contextlib.ExitStack() as stkB1:
            wgq = sb(stkB1, "wgq", [P, NDC * INNER], BF16)
            wuq = sb(stkB1, "wuq", [P, NDC * INNER], BF16)
            with contextlib.ExitStack() as stkQ:
                wtmp = [sb(stkQ, f"wtmp{i}", [P, INNER]) for i in range(2)]
                wqp = sb(stkQ, "wqp", [P, INNER])
                psW = psb(stkQ, "psW", [P, 8])
                weight_quant(h["wgT"], wgq, wtmp, wqp, 0, NDC, INNER, psW)
                weight_quant(h["wuT"], wuq, wtmp, wqp, 1, NDC, INNER, psW)
                sc.barrier()

            xq1c = [sb(stkB1, f"xq1c{i}", [P, NDC * TA], BF16) for i in range(2)]
            gate = sb(stkB1, "gate", [P, INNER])
            prod = sb(stkB1, "prod", [P, INNER])
            xq2b = sb(stkB1, "xq2b", [P, INNER], BF16)
            xq2T = sb(stkB1, "xq2T", [P, NIC * TA], BF16)
            psG = psb(stkB1, "psG", [P, 512])
            psU = psb(stkB1, "psU", [P, 512])
            psT0b = psb(stkB1, "psT0b", [P, 512], BF16)
            psT1b = psb(stkB1, "psT1b", [P, 512], BF16)
            sgc = sb(stkB1, "sgc", [P, 2])

            for ti in range(NTT):
                xc = xq1c[ti % 2]
                dma(xc.ap.rearrange("p (dc t) -> p dc t", dc=NDC),
                    h["xq1scr"].rearrange("(dc p) t -> p dc t", p=P)
                    [:, :, ti * TA:(ti + 1) * TA],
                    reads=[xq1scrB], writes=[xc])
                sc.add("g", lambda e, ti=ti: e.tensor_tensor(
                    sgc.ap[:, 0:1], mx1c.ap[:, ti:ti + 1], mgs.ap[:, 0:1], ALU.mult),
                    reads=[mx1c, mgs], writes=[sgc])
                sc.add("g", lambda e, ti=ti: e.tensor_tensor(
                    sgc.ap[:, 1:2], mx1c.ap[:, ti:ti + 1], mgs.ap[:, 1:2], ALU.mult),
                    reads=[mx1c, mgs], writes=[sgc])
                for oc in range(8):
                    for dc in range(NDC):
                        sc.add("t", lambda e, oc=oc, dc=dc, xc=xc: e.matmul(
                            psG.ap, xc.ap[:, dc * TA:(dc + 1) * TA],
                            wgq.ap[:, dc * INNER + oc * 512:dc * INNER + (oc + 1) * 512],
                            start=(dc == 0), stop=(dc == NDC - 1)),
                            reads=[xc, wgq], writes=[psG], inc=(dc == NDC - 1))
                    sc.add("s", lambda e, oc=oc: e.activation(
                        gate.ap[:, oc * 512:(oc + 1) * 512], psG.ap, AF.Silu,
                        bias=0.0, scale=sgc.ap[:, 0:1]),
                        reads=[psG, sgc], writes=[gate])
                    for dc in range(NDC):
                        sc.add("t", lambda e, oc=oc, dc=dc, xc=xc: e.matmul(
                            psU.ap, xc.ap[:, dc * TA:(dc + 1) * TA],
                            wuq.ap[:, dc * INNER + oc * 512:dc * INNER + (oc + 1) * 512],
                            start=(dc == 0), stop=(dc == NDC - 1)),
                            reads=[xc, wuq], writes=[psU], inc=(dc == NDC - 1))
                    sc.add("v", lambda e, oc=oc: e.scalar_tensor_tensor(
                        prod.ap[:, oc * 512:(oc + 1) * 512], psU.ap, sgc.ap[:, 1:2],
                        gate.ap[:, oc * 512:(oc + 1) * 512], ALU.mult, ALU.mult),
                        reads=[psU, sgc, gate], writes=[prod])
                sc.add("v", lambda e: e.tensor_reduce(
                    stat.ap[:, 5:6], prod.ap, AX.X, ALU.max,
                    apply_absolute_value=True), reads=[prod], writes=[stat])
                sc.add("g", lambda e, ti=ti: e.tensor_scalar(
                    mx2c.ap[:, ti:ti + 1], stat.ap[:, 5:6], 3.0e38, EPS,
                    ALU.min, ALU.max), reads=[stat], writes=[mx2c])
                sc.add("v", lambda e, ti=ti: e.reciprocal(
                    stat.ap[:, 6:7], mx2c.ap[:, ti:ti + 1]), reads=[mx2c], writes=[stat])
                sc.add("g", lambda e: e.tensor_scalar(
                    stat.ap[:, 7:8], stat.ap[:, 6:7], 127.0, 0.0, ALU.mult, ALU.add),
                    reads=[stat], writes=[stat])
                sc.add("s", lambda e: e.activation(
                    gate.ap, prod.ap, AF.Identity, bias=0.0, scale=stat.ap[:, 7:8]),
                    reads=[prod, stat], writes=[gate])
                sc.add("v", lambda e: e.tensor_scalar(
                    gate.ap, gate.ap, ceff.ap[:, 0:1], ceff.ap[:, 0:1],
                    ALU.add, ALU.subtract), reads=[gate, ceff], writes=[gate])
                sc.add("g", lambda e: e.tensor_scalar(
                    xq2b.ap, gate.ap, 127.0, -128.0, ALU.min, ALU.max),
                    reads=[gate], writes=[xq2b])
                for q8 in range(8):
                    pst = [psT0b, psT1b][q8 % 2]
                    for j in range(4):
                        ocb = q8 * 4 + j
                        sc.add("t", lambda e, ocb=ocb, j=j, pst=pst: e.transpose(
                            pst.ap[:, j * P:(j + 1) * P],
                            xq2b.ap[:, ocb * P:(ocb + 1) * P], identb.ap),
                            reads=[xq2b, identb], writes=[pst], inc=(j == 3))
                    sc.add("s", lambda e, q8=q8, pst=pst: e.copy(
                        xq2T.ap[:, q8 * 512:(q8 + 1) * 512], pst.ap),
                        reads=[pst], writes=[xq2T])
                dma(h["xq2scr"].rearrange("(kc p) t -> p kc t", p=P)
                    [:, :, ti * TA:(ti + 1) * TA],
                    xq2T.ap.rearrange("p (kc t) -> p kc t", kc=NIC),
                    reads=[xq2T], writes=[xq2scrB])
            sc.barrier()

        with contextlib.ExitStack() as stkB2:
            wdq = sb(stkB2, "wdq", [P, NIC * DM], BF16)
            with contextlib.ExitStack() as stkQ2:
                wtmp2 = [sb(stkQ2, f"wtmp2_{i}", [P, DM]) for i in range(2)]
                wqp2 = sb(stkQ2, "wqp2", [P, DM])
                psW2 = psb(stkQ2, "psW2", [P, 8])
                weight_quant(h["wdT"], wdq, wtmp2, wqp2, 2, NIC, DM, psW2)
                sc.barrier()

            xq2c = [sb(stkB2, f"xq2c{i}", [P, NIC * TA], BF16) for i in range(2)]
            x2c = [sb(stkB2, f"x2c{i}", [P, DM]) for i in range(2)]
            outb = sb(stkB2, "outb", [P, DM])
            psD0 = psb(stkB2, "psD0", [P, 512])
            psD1 = psb(stkB2, "psD1", [P, 512])
            sdc = sb(stkB2, "sdc", [P, 1])

            for ti in range(NTT):
                xc = xq2c[ti % 2]
                x2 = x2c[ti % 2]
                dma(xc.ap.rearrange("p (kc t) -> p kc t", kc=NIC),
                    h["xq2scr"].rearrange("(kc p) t -> p kc t", p=P)
                    [:, :, ti * TA:(ti + 1) * TA],
                    reads=[xq2scrB], writes=[xc])
                dma(x2.ap, h["x2scr"][ti * TA:(ti + 1) * TA, :],
                    reads=[x2scrB], writes=[x2])
                sc.add("g", lambda e, ti=ti: e.tensor_tensor(
                    sdc.ap, mx2c.ap[:, ti:ti + 1], mgs.ap[:, 2:3], ALU.mult),
                    reads=[mx2c, mgs], writes=[sdc])
                for oc2 in range(2):
                    pm = [psD0, psD1][oc2]
                    for kc in range(NIC):
                        sc.add("t", lambda e, oc2=oc2, kc=kc, xc=xc, pm=pm: e.matmul(
                            pm.ap, xc.ap[:, kc * TA:(kc + 1) * TA],
                            wdq.ap[:, kc * DM + oc2 * 512:kc * DM + (oc2 + 1) * 512],
                            start=(kc == 0), stop=(kc == NIC - 1)),
                            reads=[xc, wdq], writes=[pm], inc=(kc == NIC - 1))
                    sc.add("v", lambda e, oc2=oc2, x2=x2, pm=pm: e.scalar_tensor_tensor(
                        x2.ap[:, oc2 * 512:(oc2 + 1) * 512], pm.ap, sdc.ap[:, 0:1],
                        x2.ap[:, oc2 * 512:(oc2 + 1) * 512], ALU.mult, ALU.add),
                        reads=[pm, sdc, x2], writes=[x2])
                sc.add("s", lambda e, x2=x2: e.activation(
                    outb.ap, x2.ap, AF.Identity, bias=0.0, scale=1.0,
                    accum_out=stat.ap[:, 0:1]), reads=[x2], writes=[outb, stat])
                sc.add("s", lambda e, x2=x2: e.activation(
                    outb.ap, x2.ap, AF.Square, bias=0.0, scale=1.0,
                    accum_out=stat.ap[:, 1:2]), reads=[x2], writes=[outb, stat])
                ln_stats(sc, stat, DM, epsc)
                sc.add("g", lambda e, x2=x2: e.tensor_scalar(
                    outb.ap, x2.ap, stat.ap[:, 2:3], stat.ap[:, 4:5],
                    ALU.add, ALU.mult), reads=[x2, stat], writes=[outb])
                dma(h["out"][ti * TA:(ti + 1) * TA, :], outb.ap,
                    reads=[outb], writes=[outB])

        finwaits = []
        for j in range(NDSEM):
            if sc.dcnt[j]:
                finwaits.append((f"d{j}", 16 * sc.dcnt[j]))
        sc.ops["d"].append((finwaits, None, False, None))

        with nc.Block() as block:
            sc.emit(block, sems)
    return nc


_NC_CACHE = [None]


def kernel(**inputs):
    import ml_dtypes
    f32 = np.float32
    x = np.ascontiguousarray(np.asarray(inputs["L0_attn_x"], dtype=f32))

    def t(a):
        return np.ascontiguousarray(np.asarray(a, dtype=f32).T)

    def col8(v):
        return np.ascontiguousarray(np.asarray(v, dtype=f32).reshape(NDC, P).T)

    A8 = np.ascontiguousarray(
        np.asarray(inputs["L0_attn_A_log"], dtype=f32).reshape(NDC, P, NST)
        .transpose(1, 0, 2).reshape(P, NDC * NST))
    base = dict(
        wpT=t(inputs["L0_attn_proj_W"]), wfT=t(inputs["L0_attn_fconv_W"]),
        wbT=t(inputs["L0_attn_bconv_W"]), wdbcT=t(inputs["L0_attn_deltaBC_W"]),
        wdtT=t(inputs["L0_attn_dt_W"]), wgT=t(inputs["L0_ffn_w_gate"]),
        wuT=t(inputs["L0_ffn_w_up"]), wdT=t(inputs["L0_ffn_w_down"]),
        A_log8=A8,
        gamma8=col8(inputs["L0_attn_ln_w"]), beta8=col8(inputs["L0_attn_ln_b"]),
        bp8=col8(inputs["L0_attn_proj_b"]), bf8=col8(inputs["L0_attn_fconv_b"]),
        bb8=col8(inputs["L0_attn_bconv_b"]), dtb8=col8(inputs["L0_attn_dt_b"]),
        D8=col8(inputs["L0_attn_D"]),
        ident=np.eye(P, dtype=f32),
        identb=np.eye(P, dtype=ml_dtypes.bfloat16),
        ones=np.ones((P, P), dtype=f32),
    )
    in_maps = []
    for core in range(8):
        m = dict(base)
        m["x"] = np.ascontiguousarray(x[core * SPC:(core + 1) * SPC])
        in_maps.append(m)

    if _NC_CACHE[0] is None:
        _NC_CACHE[0] = build_nc()
    res = run_bass_kernel_spmd(_NC_CACHE[0], in_maps, core_ids=list(range(8)))
    out = np.concatenate([r["out"].reshape(SPC, SEQ, DM) for r in res.results], axis=0)
    return np.ascontiguousarray(out.astype(np.float32))


# version 06
# speedup vs baseline: 1.0553x; 1.0553x over previous
import numpy as np
import contextlib
import concourse.bass as bass
import concourse.mybir as mybir
from concourse.bass_utils import run_bass_kernel_spmd

F32 = mybir.dt.float32
BF16 = mybir.dt.bfloat16
AF = mybir.ActivationFunctionType
ALU = mybir.AluOpType
AX = mybir.AxisListType

P = 128
SPC = 2
SEQ = 2048
DM = 1024
NDC = DM // P
INNER = 4096
NIC = INNER // P
TA = 128
NCH = SEQ // TA
TS = 64
NSUB = TA // TS
NST = 16
DTR = 64
EPS = 1e-5
C32 = 1.5 * 2.0**23
C64 = 1.5 * 2.0**52
KMUL = float(np.float32((C64 - C32) * (2.0**40)))
NTT = SPC * NCH


class Buf:
    __slots__ = ("ap", "writer", "readers")

    def __init__(self, ap):
        self.ap = ap
        self.writer = None
        self.readers = []


NDSEM = 32


class Sched:
    ENG = ("t", "s", "v", "g")
    STREAMS = ("d", "t", "s", "v", "g")

    def __init__(self):
        self.ops = {e: [] for e in self.STREAMS}
        self.cnt = {e: 0 for e in self.ENG}
        self.dcnt = [0] * NDSEM
        self.dnext = 0
        self.vc = {e: {} for e in self.STREAMS}
        self.snap = {}

    def _deps(self, reads, writes):
        deps = []
        for b in reads:
            if b.writer is not None:
                deps.append(b.writer)
        for b in writes:
            for tok in b.readers:
                deps.append(tok)
            if b.writer is not None:
                deps.append(b.writer)
        return deps

    def _mkwaits(self, ek, deps):
        vc = self.vc[ek]
        waits = []
        for (dk, val) in deps:
            if vc.get(dk, 0) >= val:
                continue
            waits.append((dk, val))
            psnap = self.snap.get((dk, val))
            if psnap:
                for k2, v2 in psnap.items():
                    if v2 > vc.get(k2, 0):
                        vc[k2] = v2
            if val > vc.get(dk, 0):
                vc[dk] = val
        return waits

    def add(self, ek, buildfn, reads=(), writes=(), inc=True):
        deps = self._deps(reads, writes)
        waits = self._mkwaits(ek, deps)
        if inc:
            self.cnt[ek] += 1
        mytok = (ek, self.cnt[ek])
        if inc:
            s = dict(self.vc[ek])
            s[ek] = self.cnt[ek]
            self.snap[mytok] = s
        self.ops[ek].append((waits, buildfn, inc, None))
        for b in reads:
            b.readers.append(mytok)
        for b in writes:
            if inc:
                b.writer = mytok
                b.readers = []
        return mytok

    def add_dma(self, buildfn, reads=(), writes=()):
        deps = self._deps(reads, writes)
        j = self.dnext
        self.dnext = (j + 1) % NDSEM
        key = f"d{j}"
        if self.dcnt[j] > 0:
            deps.append((key, 16 * self.dcnt[j]))
        waits = self._mkwaits("d", deps)
        self.dcnt[j] += 1
        mytok = (key, 16 * self.dcnt[j])
        s = dict(self.vc["d"])
        s[key] = 16 * self.dcnt[j]
        self.snap[mytok] = s
        self.ops["d"].append((waits, buildfn, True, key))
        for b in reads:
            b.readers.append(mytok)
        for b in writes:
            b.writer = mytok
            b.readers = []
        return mytok

    def _snapshot(self):
        snap = {e: self.cnt[e] for e in self.ENG}
        for j in range(NDSEM):
            if self.dcnt[j]:
                snap[f"d{j}"] = 16 * self.dcnt[j]
        return snap

    def barrier(self):
        snap = self._snapshot()
        for ek in self.STREAMS:
            deps = [(dk, val) for dk, val in snap.items() if dk != ek]
            waits = self._mkwaits(ek, deps)
            if waits:
                self.ops[ek].append((waits, None, False, None))

    def emit(self, block, sems):
        for ek, attr in (("d", "sync"), ("t", "tensor"), ("s", "scalar"),
                         ("v", "vector"), ("g", "gpsimd")):
            oplist = self.ops[ek]
            if not oplist:
                continue

            def section(eng, _ops=oplist, _ek=ek):
                for (waits, buildfn, inc, dkey) in _ops:
                    for (dk, val) in waits:
                        eng.wait_ge(sems[dk], val)
                    if buildfn is None:
                        continue
                    inst = buildfn(eng)
                    if inc:
                        if dkey is not None:
                            inst.then_inc(sems[dkey], 16)
                        else:
                            inst.then_inc(sems[_ek], 1)

            getattr(block, attr)(section)
            self.ops[ek] = []


def ln_stats(sc, stat, n, epsc=None):
    inv = 1.0 / n
    sc.add("v", lambda e: e.tensor_scalar(stat.ap[:, 2:3], stat.ap[:, 0:1],
                                          -inv, None, ALU.mult),
           reads=[stat], writes=[stat])
    sc.add("g", lambda e: e.tensor_tensor(stat.ap[:, 3:4], stat.ap[:, 2:3],
                                          stat.ap[:, 2:3], ALU.mult),
           reads=[stat], writes=[stat])
    sc.add("g", lambda e: e.tensor_scalar(stat.ap[:, 1:2], stat.ap[:, 1:2],
                                          inv, 0.0, ALU.mult, ALU.add),
           reads=[stat], writes=[stat])
    sc.add("g", lambda e: e.tensor_tensor(stat.ap[:, 3:4], stat.ap[:, 1:2],
                                          stat.ap[:, 3:4], ALU.subtract),
           reads=[stat], writes=[stat])
    sc.add("s", lambda e: e.activation(stat.ap[:, 3:4], stat.ap[:, 3:4], AF.Sqrt,
                                       bias=epsc.ap[:, 0:1], scale=1.0),
           reads=[stat, epsc], writes=[stat])
    sc.add("v", lambda e: e.reciprocal(stat.ap[:, 4:5], stat.ap[:, 3:4]),
           reads=[stat], writes=[stat])


def softplus(sc, xb, spb):
    w = xb.ap.shape[1]
    sp = spb.ap[:, 0:w]
    sc.add("s", lambda e: e.activation(sp, xb.ap, AF.Abs, bias=0.0, scale=1.0),
           reads=[xb], writes=[spb])
    sc.add("s", lambda e: e.activation(sp, sp, AF.Exp, bias=0.0, scale=-1.0),
           reads=[spb], writes=[spb])
    sc.add("s", lambda e: e.activation(sp, sp, AF.Ln, bias=1.0, scale=1.0),
           reads=[spb], writes=[spb])
    sc.add("s", lambda e: e.activation(xb.ap, xb.ap, AF.Relu, bias=0.0, scale=1.0),
           reads=[xb], writes=[xb])
    sc.add("g", lambda e: e.tensor_tensor(xb.ap, xb.ap, sp, ALU.add),
           reads=[xb, spb], writes=[xb])


def build_nc():
    nc = bass.Bass("TRN2", target_bir_lowering=False)
    h = {}
    h["x"] = nc.dram_tensor("x", [SPC, SEQ, DM], F32, kind="ExternalInput").ap()
    for nm, shape in [("wpT", [DM, DM]), ("wfT", [DM, DM]), ("wbT", [DM, DM]),
                      ("wdbcT", [DM, 96]), ("wdtT", [DTR, DM]),
                      ("wgT", [DM, INNER]), ("wuT", [DM, INNER]), ("wdT", [INNER, DM])]:
        h[nm] = nc.dram_tensor(nm, shape, F32, kind="ExternalInput").ap()
    for nm in ("A_log8", "gamma8", "beta8", "bp8", "bf8", "bb8", "dtb8", "D8"):
        cols = NDC * NST if nm == "A_log8" else NDC
        h[nm] = nc.dram_tensor(nm, [P, cols], F32, kind="ExternalInput").ap()
    h["ident"] = nc.dram_tensor("ident", [P, P], F32, kind="ExternalInput").ap()
    h["identb"] = nc.dram_tensor("identb", [P, P], BF16, kind="ExternalInput").ap()
    h["ones"] = nc.dram_tensor("ones", [P, P], F32, kind="ExternalInput").ap()
    h["out"] = nc.dram_tensor("out", [SPC * SEQ, DM], F32, kind="ExternalOutput").ap()
    h["x2scr"] = nc.dram_tensor("x2scr", [SPC * SEQ, DM], F32, kind="Internal").ap()
    h["xq1scr"] = nc.dram_tensor("xq1scr", [DM, SPC * SEQ], BF16, kind="Internal").ap()
    h["xq2scr"] = nc.dram_tensor("xq2scr", [INNER, SPC * SEQ], BF16, kind="Internal").ap()
    bscr = [Buf(nc.dram_tensor(f"bscr{i}", [NST, TS], F32, kind="Internal").ap())
            for i in range(2)]
    cscr = [Buf(nc.dram_tensor(f"cscr{i}", [NST, TS], F32, kind="Internal").ap())
            for i in range(2)]
    x2scrB = Buf(h["x2scr"])
    xq1scrB = Buf(h["xq1scr"])
    xq2scrB = Buf(h["xq2scr"])
    outB = Buf(h["out"])

    sc = Sched()

    with contextlib.ExitStack() as main_stk:
        sems = {ek: main_stk.enter_context(nc.semaphore(f"S{ek}")) for ek in Sched.ENG}
        for j in range(NDSEM):
            sems[f"d{j}"] = main_stk.enter_context(nc.semaphore(f"Sd{j}"))

        def sb(stk, name, shape, dt=F32):
            return Buf(stk.enter_context(nc.sbuf_tensor(name, shape, dt)).ap())

        def psb(stk, name, shape, dt=F32):
            return Buf(stk.enter_context(nc.psum_tensor(name, shape, dt)).ap())

        def dma(dst, src, reads=(), writes=()):
            return sc.add_dma(lambda e, d=dst, s=src: e.dma_start(out=d, in_=s),
                              reads=reads, writes=writes)

        ident = sb(main_stk, "identt", [P, P])
        identb = sb(main_stk, "identbt", [P, P], BF16)
        ones = sb(main_stk, "onest", [P, P])
        A_sb = sb(main_stk, "A_sb", [P, NDC * NST])
        D_sb = sb(main_stk, "D_sb", [P, NDC])
        dtb = sb(main_stk, "dtb", [P, NDC])
        bp = sb(main_stk, "bp", [P, NDC])
        bfc = sb(main_stk, "bfc", [P, NDC])
        bbc = sb(main_stk, "bbc", [P, NDC])
        gam = sb(main_stk, "gam", [P, NDC])
        bet = sb(main_stk, "bet", [P, NDC])
        bfold = sb(main_stk, "bfold", [P, NDC])
        ceff = sb(main_stk, "ceff", [P, 2])
        tmpc = sb(main_stk, "tmpc", [P, 2])
        mx1c = sb(main_stk, "mx1c", [P, NTT])
        mx2c = sb(main_stk, "mx2c", [P, NTT])
        stat = sb(main_stk, "stat", [P, 8])
        sc1 = sb(main_stk, "sc1", [P, NTT])
        mgs = sb(main_stk, "mgs", [P, 4])
        colt = sb(main_stk, "colt", [P, 4])
        epsc = sb(main_stk, "epsc", [P, 1])

        def weight_quant(wh, wq, wtmp, wqp, mcol, nkc, width, psw):
            nelem = float(nkc * P * width)
            for kc in range(nkc):
                wt = wtmp[kc % 2]
                dma(wt.ap, wh[kc * P:(kc + 1) * P, :], writes=[wt])
                sc.add("v", lambda e, wt=wt: e.tensor_reduce(
                    colt.ap[:, 0:1], wt.ap, AX.X, ALU.add, apply_absolute_value=True),
                    reads=[wt], writes=[colt])
                if kc == 0:
                    sc.add("g", lambda e: e.tensor_copy(colt.ap[:, 1:2], colt.ap[:, 0:1]),
                           reads=[colt], writes=[colt])
                else:
                    sc.add("g", lambda e: e.tensor_tensor(
                        colt.ap[:, 1:2], colt.ap[:, 1:2], colt.ap[:, 0:1], ALU.add),
                        reads=[colt], writes=[colt])
            sc.add("t", lambda e: e.matmul(psw.ap[:, 0:1], ones.ap, colt.ap[:, 1:2],
                                           start=True, stop=True),
                   reads=[ones, colt], writes=[psw])
            sc.add("s", lambda e: e.copy(colt.ap[:, 2:3], psw.ap[:, 0:1]),
                   reads=[psw], writes=[colt])
            sc.add("g", lambda e: e.tensor_scalar(colt.ap[:, 2:3], colt.ap[:, 2:3],
                                                  1.0 / nelem, 0.0, ALU.mult, ALU.add),
                   reads=[colt], writes=[colt])
            sc.add("g", lambda e: e.tensor_scalar(colt.ap[:, 3:4], colt.ap[:, 2:3],
                                                  3.0e38, EPS, ALU.min, ALU.max),
                   reads=[colt], writes=[colt])
            sc.add("g", lambda e: e.tensor_scalar(
                mgs.ap[:, mcol:mcol + 1], colt.ap[:, 3:4], 1.0 / 127.0, 0.0,
                ALU.mult, ALU.add), reads=[colt], writes=[mgs])
            sc.add("v", lambda e: e.reciprocal(colt.ap[:, 0:1], colt.ap[:, 3:4]),
                   reads=[colt], writes=[colt])
            for kc in range(nkc):
                wt = wtmp[kc % 2]
                dma(wt.ap, wh[kc * P:(kc + 1) * P, :], writes=[wt])
                sc.add("s", lambda e, wt=wt: e.activation(
                    wqp.ap, wt.ap, AF.Identity, bias=0.0, scale=colt.ap[:, 0:1]),
                    reads=[wt, colt], writes=[wqp])
                sc.add("v", lambda e: e.tensor_scalar(
                    wqp.ap, wqp.ap, ceff.ap[:, 0:1], ceff.ap[:, 0:1],
                    ALU.add, ALU.subtract), reads=[wqp, ceff], writes=[wqp])
                sc.add("g", lambda e, kc=kc: e.tensor_scalar(
                    wq.ap[:, kc * width:(kc + 1) * width], wqp.ap, 1.0, -1.0,
                    ALU.min, ALU.max), reads=[wqp], writes=[wq])

        with contextlib.ExitStack() as stk0:
            wpT = sb(stk0, "wpT_s", [P, NDC * DM])
            wfT = sb(stk0, "wfT_s", [P, NDC * DM])
            wbT = sb(stk0, "wbT_s", [P, NDC * DM])
            wdbcT = sb(stk0, "wdbcT_s", [P, NDC * 96])
            wdtT = sb(stk0, "wdtT_s", [DTR, NDC * P])

            for nm, b in [("ident", ident), ("identb", identb), ("ones", ones),
                          ("A_log8", A_sb), ("D8", D_sb), ("dtb8", dtb),
                          ("bp8", bp), ("bf8", bfc), ("bb8", bbc),
                          ("gamma8", gam), ("beta8", bet)]:
                dma(b.ap, h[nm], writes=[b])
            for dc in range(NDC):
                dma(wpT.ap[:, dc * DM:(dc + 1) * DM], h["wpT"][dc * P:(dc + 1) * P, :], writes=[wpT])
                dma(wfT.ap[:, dc * DM:(dc + 1) * DM], h["wfT"][dc * P:(dc + 1) * P, :], writes=[wfT])
                dma(wbT.ap[:, dc * DM:(dc + 1) * DM], h["wbT"][dc * P:(dc + 1) * P, :], writes=[wbT])
                dma(wdbcT.ap[:, dc * 96:(dc + 1) * 96], h["wdbcT"][dc * P:(dc + 1) * P, :], writes=[wdbcT])
                dma(wdtT.ap[:, dc * P:(dc + 1) * P], h["wdtT"][:, dc * P:(dc + 1) * P], writes=[wdtT])

            sc.add("v", lambda e: e.tensor_scalar(tmpc.ap, ones.ap[:, 0:2], 1.0, None,
                                                  ALU.mult), reads=[ones], writes=[tmpc])
            sc.add("g", lambda e: e.tensor_scalar(tmpc.ap, tmpc.ap, 2.0**-40, 1.0,
                                                  ALU.add, ALU.mult),
                   reads=[tmpc], writes=[tmpc])
            sc.add("v", lambda e: e.tensor_scalar(tmpc.ap, tmpc.ap, -1.0, None, ALU.add),
                   reads=[tmpc], writes=[tmpc])
            sc.add("g", lambda e: e.tensor_scalar(ceff.ap, tmpc.ap, KMUL, C32,
                                                  ALU.mult, ALU.add),
                   reads=[tmpc], writes=[ceff])

            sc.add("g", lambda e: e.tensor_scalar(epsc.ap, ones.ap[:, 0:1], EPS, 0.0,
                                                  ALU.mult, ALU.add),
                   reads=[ones], writes=[epsc])
            sc.add("s", lambda e: e.activation(A_sb.ap, A_sb.ap, AF.Exp,
                                               bias=0.0, scale=1.0),
                   reads=[A_sb], writes=[A_sb])
            sc.add("g", lambda e: e.tensor_scalar(A_sb.ap, A_sb.ap, -1.0, 0.0,
                                                  ALU.mult, ALU.add),
                   reads=[A_sb], writes=[A_sb])

            with contextlib.ExitStack() as stkA:
                x_sbP = [sb(stkA, f"x_sb{i}", [P, DM]) for i in range(2)]
                xn_sb = sb(stkA, "xn_sb", [P, DM])
                xpTP = [sb(stkA, f"xpT{i}", [P, NDC * TA]) for i in range(2)]
                uT = sb(stkA, "uT", [P, NDC * TA])
                deltaTP = [sb(stkA, f"deltaT{i}", [P, NDC * TA]) for i in range(2)]
                duTP = [sb(stkA, f"duT{i}", [P, NDC * TA]) for i in range(2)]
                ysumP = [sb(stkA, f"ysum{i}", [P, NDC * TA]) for i in range(2)]
                spP = [sb(stkA, f"sp{i}", [P, DM]) for i in range(2)]
                dbcP = [Buf(stkA.enter_context(nc.sbuf_tensor(f"dbc{i}", [96, TA], F32)).ap())
                        for i in range(2)]
                brepP = [sb(stkA, f"brep{i}", [P, NST * TS]) for i in range(2)]
                crepP = [sb(stkA, f"crep{i}", [P, NST * TS]) for i in range(2)]
                ebuf = [sb(stkA, f"ebuf{i}", [P, NST * (TS + 1)]) for i in range(2)]
                bbuf = [sb(stkA, f"bbuf{i}", [P, NST * (TS + 1)]) for i in range(2)]
                hbuf = sb(stkA, "hbuf", [P, NST * (TS + 1)])
                hcar = sb(stkA, "hcar", [P, 2 * NDC * NST])
                yscP = [sb(stkA, f"ysc{i}", [P, TS]) for i in range(2)]
                statP = [sb(stkA, f"stat{i}", [P, 8]) for i in range(2)]
                xq1b = sb(stkA, "xq1b", [P, DM], BF16)
                xq1T = sb(stkA, "xq1T", [P, NDC * TA], BF16)
                psA = psb(stkA, "psA", [P, 512])
                psA2 = psb(stkA, "psA2", [P, 512])
                psDT = psb(stkA, "psDT", [P, 512])
                psT0 = psb(stkA, "psT0", [P, 512])
                psT1 = psb(stkA, "psT1", [P, 512])
                psDB = psb(stkA, "psDB", [96, TA])
                psTB0 = psb(stkA, "psTB0", [P, 512], BF16)
                psTB1 = psb(stkA, "psTB1", [P, 512], BF16)
                psmm = [psA, psA2]

                def e3(b):
                    return b.ap.rearrange("p (n t) -> p n t", n=NST)

                for eb in ebuf:
                    sc.add("g", lambda e, eb=eb: e.memset(e3(eb)[:, :, 0:1], 0.0),
                           writes=[eb])

                for oc in range(NDC):
                    for dc in range(NDC):
                        sc.add("t", lambda e, oc=oc, dc=dc: e.matmul(
                            psA.ap[:, 0:1],
                            wpT.ap[:, dc * DM + oc * P: dc * DM + (oc + 1) * P],
                            bet.ap[:, dc:dc + 1],
                            start=(dc == 0), stop=(dc == NDC - 1)),
                            reads=[wpT, bet], writes=[psA], inc=(dc == NDC - 1))
                    sc.add("s", lambda e, oc=oc: e.activation(
                        bfold.ap[:, oc:oc + 1], psA.ap[:, 0:1], AF.Identity,
                        bias=bp.ap[:, oc:oc + 1], scale=1.0),
                        reads=[psA, bp], writes=[bfold])
                for dc in range(NDC):
                    sc.add("v", lambda e, dc=dc: e.tensor_scalar(
                        wpT.ap[:, dc * DM:(dc + 1) * DM], wpT.ap[:, dc * DM:(dc + 1) * DM],
                        gam.ap[:, dc:dc + 1], 1.0, ALU.mult, ALU.mult),
                        reads=[wpT, gam], writes=[wpT])

                def do_head(s, c, ti, x_sb, xpT, deltaT, duT, ysum, sp, dbc, stat):
                    t0 = c * TA
                    dma(x_sb.ap, h["x"][s, t0:t0 + TA, :], writes=[x_sb])
                    sc.add("s", lambda e: e.activation(
                        x_sb.ap, x_sb.ap, AF.Identity, bias=0.0, scale=1.0,
                        accum_out=stat.ap[:, 0:1]), reads=[x_sb], writes=[x_sb, stat])
                    sc.add("s", lambda e: e.activation(
                        sp.ap, x_sb.ap, AF.Square, bias=0.0, scale=1.0,
                        accum_out=stat.ap[:, 1:2]), reads=[x_sb], writes=[sp, stat])
                    ln_stats(sc, stat, DM, epsc)
                    sc.add("g", lambda e: e.tensor_scalar(
                        xn_sb.ap, x_sb.ap, stat.ap[:, 2:3], stat.ap[:, 4:5],
                        ALU.add, ALU.mult), reads=[x_sb, stat], writes=[xn_sb])
                    for half in range(2):
                        pst = [psT0, psT1][half]
                        for j in range(4):
                            dc = half * 4 + j
                            sc.add("t", lambda e, dc=dc, j=j, pst=pst: e.transpose(
                                pst.ap[:, j * P:(j + 1) * P],
                                xn_sb.ap[:, dc * P:(dc + 1) * P], ident.ap),
                                reads=[xn_sb, ident], writes=[pst], inc=(j == 3))
                        sc.add("s", lambda e, half=half, pst=pst: e.copy(
                            sp.ap[:, half * 512:(half + 1) * 512], pst.ap),
                            reads=[pst], writes=[sp])
                    for oc in range(NDC):
                        pm = psmm[oc % 2]
                        for dc in range(NDC):
                            sc.add("t", lambda e, oc=oc, dc=dc, pm=pm: e.matmul(
                                pm.ap[:, 0:TA],
                                wpT.ap[:, dc * DM + oc * P: dc * DM + (oc + 1) * P],
                                sp.ap[:, dc * P:(dc + 1) * P],
                                start=(dc == 0), stop=(dc == NDC - 1)),
                                reads=[wpT, sp], writes=[pm], inc=(dc == NDC - 1))
                        sc.add("s", lambda e, oc=oc, pm=pm: e.activation(
                            xpT.ap[:, oc * TA:(oc + 1) * TA], pm.ap[:, 0:TA],
                            AF.Identity, bias=bfold.ap[:, oc:oc + 1], scale=1.0),
                            reads=[pm, bfold], writes=[xpT])

                    for br, (wb, bb_) in enumerate([(wfT, bfc), (wbT, bbc)]):
                        for oc in range(NDC):
                            pm = psmm[oc % 2]
                            for dc in range(NDC):
                                sc.add("t", lambda e, oc=oc, dc=dc, pm=pm, wb=wb: e.matmul(
                                    pm.ap[:, 0:TA],
                                    wb.ap[:, dc * DM + oc * P: dc * DM + (oc + 1) * P],
                                    xpT.ap[:, dc * TA:(dc + 1) * TA],
                                    start=(dc == 0), stop=(dc == NDC - 1)),
                                    reads=[wb, xpT], writes=[pm], inc=(dc == NDC - 1))
                            sc.add("s", lambda e, oc=oc, pm=pm, bb_=bb_: e.activation(
                                uT.ap[:, oc * TA:(oc + 1) * TA], pm.ap[:, 0:TA],
                                AF.Identity, bias=bb_.ap[:, oc:oc + 1], scale=1.0),
                                reads=[pm, bb_], writes=[uT])
                        softplus(sc, uT, sp)
                        for dc in range(NDC):
                            sc.add("t", lambda e, dc=dc: e.matmul(
                                psDB.ap,
                                wdbcT.ap[:, dc * 96:(dc + 1) * 96],
                                uT.ap[:, dc * TA:(dc + 1) * TA],
                                start=(dc == 0), stop=(dc == NDC - 1)),
                                reads=[wdbcT, uT], writes=[psDB], inc=(dc == NDC - 1))
                        sc.add("s", lambda e: e.copy(dbc.ap, psDB.ap),
                               reads=[psDB], writes=[dbc])
                        for g4 in range(2):
                            for j in range(4):
                                dc = g4 * 4 + j
                                sc.add("t", lambda e, dc=dc, j=j: e.matmul(
                                    psDT.ap[:, j * TA:(j + 1) * TA],
                                    wdtT.ap[:, dc * P:(dc + 1) * P],
                                    dbc.ap[0:DTR, :],
                                    start=True, stop=True),
                                    reads=[wdtT, dbc], writes=[psDT], inc=(j == 3))
                            for j in range(4):
                                dc = g4 * 4 + j
                                sc.add("s", lambda e, dc=dc, j=j: e.activation(
                                    deltaT.ap[:, dc * TA:(dc + 1) * TA],
                                    psDT.ap[:, j * TA:(j + 1) * TA],
                                    AF.Identity, bias=dtb.ap[:, dc:dc + 1], scale=1.0),
                                    reads=[psDT, dtb], writes=[deltaT])
                        softplus(sc, deltaT, sp)
                        sc.add("g", lambda e: e.tensor_tensor(
                            duT.ap, deltaT.ap, uT.ap, ALU.mult),
                            reads=[deltaT, uT], writes=[duT])
                        if br == 0:
                            for dc in range(NDC):
                                sc.add("g", lambda e, dc=dc: e.tensor_scalar(
                                    ysum.ap[:, dc * TA:(dc + 1) * TA],
                                    uT.ap[:, dc * TA:(dc + 1) * TA],
                                    D_sb.ap[:, dc:dc + 1], 0.0, ALU.mult, ALU.add),
                                    reads=[uT, D_sb], writes=[ysum])
                        else:
                            for dc in range(NDC):
                                sc.add("g", lambda e, dc=dc: e.tensor_scalar(
                                    sp.ap[:, dc * P:(dc + 1) * P],
                                    uT.ap[:, dc * TA:(dc + 1) * TA],
                                    D_sb.ap[:, dc:dc + 1], 0.0, ALU.mult, ALU.add),
                                    reads=[uT, D_sb], writes=[sp])
                            sc.add("g", lambda e: e.tensor_tensor(
                                ysum.ap, ysum.ap, sp.ap, ALU.add),
                                reads=[ysum, sp], writes=[ysum])
                        for sub in range(NSUB):
                            ts0 = sub * TS
                            pb = (c * NSUB + sub) % 2
                            dma(bscr[pb].ap, dbc.ap[DTR:DTR + NST, ts0:ts0 + TS],
                                reads=[dbc], writes=[bscr[pb]])
                            dma(cscr[pb].ap, dbc.ap[DTR + NST:96, ts0:ts0 + TS],
                                reads=[dbc], writes=[cscr[pb]])
                            bflat = bscr[pb].ap.rearrange("n t -> (n t)") \
                                .unsqueeze(0).broadcast_to([P, NST * TS])
                            cflat = cscr[pb].ap.rearrange("n t -> (n t)") \
                                .unsqueeze(0).broadcast_to([P, NST * TS])
                            dma(brepP[pb].ap, bflat, reads=[bscr[pb]], writes=[brepP[pb]])
                            dma(crepP[pb].ap, cflat, reads=[cscr[pb]], writes=[crepP[pb]])
                            brep, crep = brepP[pb], crepP[pb]

                            def emit_e(dc):
                                eb = ebuf[(dc + sub) % 2]
                                col0 = dc * TA + ts0
                                ek = "v" if dc % 2 == 0 else "g"
                                sc.add(ek, lambda e, eb=eb, dc=dc, col0=col0: e.tensor_tensor(
                                    e3(eb)[:, :, 1:],
                                    deltaT.ap[:, col0:col0 + TS]
                                    .unsqueeze(1).broadcast_to([P, NST, TS]),
                                    A_sb.ap[:, dc * NST:(dc + 1) * NST]
                                    .unsqueeze(2).broadcast_to([P, NST, TS]),
                                    ALU.mult), reads=[deltaT, A_sb], writes=[eb])

                            def emit_exp(dc):
                                eb = ebuf[(dc + sub) % 2]
                                sc.add("s", lambda e, eb=eb: e.activation(
                                    e3(eb)[:, :, 1:], e3(eb)[:, :, 1:], AF.Exp,
                                    bias=0.0, scale=1.0), reads=[eb], writes=[eb])

                            def emit_b(dc):
                                bbf = bbuf[(dc + sub) % 2]
                                col0 = dc * TA + ts0
                                hsl = slice((br * NDC + dc) * NST,
                                            (br * NDC + dc + 1) * NST)
                                sc.add("g", lambda e, bbf=bbf, hsl=hsl: e.tensor_copy(
                                    e3(bbf)[:, :, 0:1], hcar.ap[:, hsl].unsqueeze(2)),
                                    reads=[hcar], writes=[bbf])
                                sc.add("g", lambda e, bbf=bbf, col0=col0, brep=brep: e.tensor_tensor(
                                    e3(bbf)[:, :, 1:],
                                    duT.ap[:, col0:col0 + TS]
                                    .unsqueeze(1).broadcast_to([P, NST, TS]),
                                    brep.ap.rearrange("p (n t) -> p n t", n=NST),
                                    ALU.mult), reads=[duT, brep], writes=[bbf])

                            def emit_red(dc):
                                eb = ebuf[(dc + sub) % 2]
                                ysc = yscP[dc % 2]
                                col0 = dc * TA + ts0
                                sc.add("v", lambda e, eb=eb, ysc=ysc: e.tensor_reduce(
                                    ysc.ap, e3(eb)[:, :, 1:].transpose([0, 2, 1]),
                                    AX.X, ALU.add), reads=[eb], writes=[ysc])
                                sc.add("g", lambda e, col0=col0, ysc=ysc: e.tensor_tensor(
                                    ysum.ap[:, col0:col0 + TS],
                                    ysum.ap[:, col0:col0 + TS],
                                    ysc.ap, ALU.add), reads=[ysc, ysum], writes=[ysum])

                            emit_e(0)
                            emit_exp(0)
                            emit_b(0)
                            for dc in range(NDC):
                                eb = ebuf[(dc + sub) % 2]
                                bbf = bbuf[(dc + sub) % 2]
                                hb = hbuf
                                hsl = slice((br * NDC + dc) * NST,
                                            (br * NDC + dc + 1) * NST)
                                sc.add("v", lambda e, hb=hb, eb=eb, bbf=bbf: e.tensor_tensor_scan(
                                    hb.ap, eb.ap, bbf.ap, 0.0, ALU.mult, ALU.add),
                                    reads=[eb, bbf], writes=[hb])
                                if dc > 0:
                                    emit_red(dc - 1)
                                if dc + 1 < NDC:
                                    emit_e(dc + 1)
                                    emit_b(dc + 1)
                                sc.add("g", lambda e, hb=hb, hsl=hsl: e.tensor_copy(
                                    hcar.ap[:, hsl], e3(hb)[:, :, TS:TS + 1].squeeze(2)),
                                    reads=[hb], writes=[hcar])
                                sc.add("g", lambda e, eb=eb, hb=hb, crep=crep: e.tensor_tensor(
                                    e3(eb)[:, :, 1:], e3(hb)[:, :, 1:],
                                    crep.ap.rearrange("p (n t) -> p n t", n=NST),
                                    ALU.mult), reads=[hb, crep], writes=[eb])
                                if dc + 1 < NDC:
                                    emit_exp(dc + 1)
                            emit_red(NDC - 1)

                def do_tail(s, c, ti, x_sb, xpT, deltaT, duT, ysum, sp, dbc, stat):
                    sc.add("s", lambda e: e.activation(xpT.ap, xpT.ap, AF.Silu,
                                                       bias=0.0, scale=1.0),
                           reads=[xpT], writes=[xpT])
                    sc.add("g", lambda e: e.tensor_tensor(
                        ysum.ap, ysum.ap, xpT.ap, ALU.mult),
                        reads=[ysum, xpT], writes=[ysum])
                    for half in range(2):
                        pst = [psT0, psT1][half]
                        for j in range(4):
                            dc = half * 4 + j
                            sc.add("t", lambda e, dc=dc, j=j, pst=pst: e.transpose(
                                pst.ap[:, j * P:(j + 1) * P],
                                ysum.ap[:, dc * TA:(dc + 1) * TA], ident.ap),
                                reads=[ysum, ident], writes=[pst], inc=(j == 3))
                        sc.add("v", lambda e, half=half, pst=pst: e.scalar_tensor_tensor(
                            x_sb.ap[:, half * 512:(half + 1) * 512],
                            x_sb.ap[:, half * 512:(half + 1) * 512], 2.0, pst.ap,
                            ALU.mult, ALU.add), reads=[x_sb, pst], writes=[x_sb])
                    sc.add("s", lambda e: e.activation(
                        x_sb.ap, x_sb.ap, AF.Identity, bias=0.0, scale=1.0,
                        accum_out=stat.ap[:, 0:1]), reads=[x_sb], writes=[x_sb, stat])
                    sc.add("s", lambda e: e.activation(
                        sp.ap, x_sb.ap, AF.Square, bias=0.0, scale=1.0,
                        accum_out=stat.ap[:, 1:2]), reads=[x_sb], writes=[sp, stat])
                    ln_stats(sc, stat, DM, epsc)
                    sc.add("g", lambda e: e.tensor_scalar(
                        x_sb.ap, x_sb.ap, stat.ap[:, 2:3], stat.ap[:, 4:5],
                        ALU.add, ALU.mult), reads=[x_sb, stat], writes=[x_sb])
                    dma(h["x2scr"][ti * TA:(ti + 1) * TA, :], x_sb.ap,
                        reads=[x_sb], writes=[x2scrB])
                    sc.add("v", lambda e: e.tensor_reduce(
                        stat.ap[:, 5:6], x_sb.ap, AX.X, ALU.max,
                        apply_absolute_value=True), reads=[x_sb], writes=[stat])
                    sc.add("g", lambda e: e.tensor_scalar(
                        mx1c.ap[:, ti:ti + 1], stat.ap[:, 5:6], 3.0e38, EPS,
                        ALU.min, ALU.max), reads=[stat], writes=[mx1c])
                    sc.add("v", lambda e: e.reciprocal(
                        stat.ap[:, 6:7], mx1c.ap[:, ti:ti + 1]),
                        reads=[mx1c], writes=[stat])
                    sc.add("g", lambda e: e.tensor_scalar(
                        sc1.ap[:, ti:ti + 1], stat.ap[:, 6:7], 127.0, 0.0,
                        ALU.mult, ALU.add), reads=[stat], writes=[sc1])
                    sc.add("s", lambda e: e.activation(
                        sp.ap, x_sb.ap, AF.Identity, bias=0.0,
                        scale=sc1.ap[:, ti:ti + 1]), reads=[x_sb, sc1], writes=[sp])
                    sc.add("v", lambda e: e.tensor_scalar(
                        sp.ap, sp.ap, ceff.ap[:, 0:1], ceff.ap[:, 0:1],
                        ALU.add, ALU.subtract), reads=[sp, ceff], writes=[sp])
                    sc.add("g", lambda e: e.tensor_scalar(
                        xq1b.ap, sp.ap, 127.0, -128.0, ALU.min, ALU.max),
                        reads=[sp], writes=[xq1b])
                    for half in range(2):
                        pst = [psTB0, psTB1][half]
                        for j in range(4):
                            dc = half * 4 + j
                            sc.add("t", lambda e, dc=dc, j=j, pst=pst: e.transpose(
                                pst.ap[:, j * P:(j + 1) * P],
                                xq1b.ap[:, dc * P:(dc + 1) * P], identb.ap),
                                reads=[xq1b, identb], writes=[pst], inc=(j == 3))
                        sc.add("s", lambda e, half=half, pst=pst: e.copy(
                            xq1T.ap[:, half * 512:(half + 1) * 512], pst.ap),
                            reads=[pst], writes=[xq1T])
                    dma(h["xq1scr"].rearrange("(dc p) t -> p dc t", p=P)
                        [:, :, ti * TA:(ti + 1) * TA],
                        xq1T.ap.rearrange("p (dc t) -> p dc t", dc=NDC),
                        reads=[xq1T], writes=[xq1scrB])

                def args_of(ti):
                    pi = ti % 2
                    return (ti // NCH, ti % NCH, ti, x_sbP[pi], xpTP[pi], deltaTP[pi],
                            duTP[pi], ysumP[pi], spP[pi], dbcP[pi], statP[pi])

                for ti in range(NTT):
                    if ti % NCH == 0:
                        sc.add("g", lambda e: e.memset(hcar.ap, 0.0), writes=[hcar])
                    do_head(*args_of(ti))
                    if ti > 0:
                        do_tail(*args_of(ti - 1))
                do_tail(*args_of(NTT - 1))

            sc.barrier()

        with contextlib.ExitStack() as stkB1:
            wgq = sb(stkB1, "wgq", [P, NDC * INNER], BF16)
            wuq = sb(stkB1, "wuq", [P, NDC * INNER], BF16)
            with contextlib.ExitStack() as stkQ:
                wtmp = [sb(stkQ, f"wtmp{i}", [P, INNER]) for i in range(2)]
                wqp = sb(stkQ, "wqp", [P, INNER])
                psW = psb(stkQ, "psW", [P, 8])
                weight_quant(h["wgT"], wgq, wtmp, wqp, 0, NDC, INNER, psW)
                weight_quant(h["wuT"], wuq, wtmp, wqp, 1, NDC, INNER, psW)
                sc.barrier()

            xq1c = [sb(stkB1, f"xq1c{i}", [P, NDC * TA], BF16) for i in range(2)]
            gate = sb(stkB1, "gate", [P, INNER])
            prod = sb(stkB1, "prod", [P, INNER])
            xq2b = sb(stkB1, "xq2b", [P, INNER], BF16)
            xq2T = sb(stkB1, "xq2T", [P, NIC * TA], BF16)
            psG = psb(stkB1, "psG", [P, 512])
            psU = psb(stkB1, "psU", [P, 512])
            psT0b = psb(stkB1, "psT0b", [P, 512], BF16)
            psT1b = psb(stkB1, "psT1b", [P, 512], BF16)
            sgc = sb(stkB1, "sgc", [P, 2])

            for ti in range(NTT):
                xc = xq1c[ti % 2]
                dma(xc.ap.rearrange("p (dc t) -> p dc t", dc=NDC),
                    h["xq1scr"].rearrange("(dc p) t -> p dc t", p=P)
                    [:, :, ti * TA:(ti + 1) * TA],
                    reads=[xq1scrB], writes=[xc])
                sc.add("g", lambda e, ti=ti: e.tensor_tensor(
                    sgc.ap[:, 0:1], mx1c.ap[:, ti:ti + 1], mgs.ap[:, 0:1], ALU.mult),
                    reads=[mx1c, mgs], writes=[sgc])
                sc.add("g", lambda e, ti=ti: e.tensor_tensor(
                    sgc.ap[:, 1:2], mx1c.ap[:, ti:ti + 1], mgs.ap[:, 1:2], ALU.mult),
                    reads=[mx1c, mgs], writes=[sgc])
                for oc in range(8):
                    for dc in range(NDC):
                        sc.add("t", lambda e, oc=oc, dc=dc, xc=xc: e.matmul(
                            psG.ap, xc.ap[:, dc * TA:(dc + 1) * TA],
                            wgq.ap[:, dc * INNER + oc * 512:dc * INNER + (oc + 1) * 512],
                            start=(dc == 0), stop=(dc == NDC - 1)),
                            reads=[xc, wgq], writes=[psG], inc=(dc == NDC - 1))
                    sc.add("s", lambda e, oc=oc: e.activation(
                        gate.ap[:, oc * 512:(oc + 1) * 512], psG.ap, AF.Silu,
                        bias=0.0, scale=sgc.ap[:, 0:1]),
                        reads=[psG, sgc], writes=[gate])
                    for dc in range(NDC):
                        sc.add("t", lambda e, oc=oc, dc=dc, xc=xc: e.matmul(
                            psU.ap, xc.ap[:, dc * TA:(dc + 1) * TA],
                            wuq.ap[:, dc * INNER + oc * 512:dc * INNER + (oc + 1) * 512],
                            start=(dc == 0), stop=(dc == NDC - 1)),
                            reads=[xc, wuq], writes=[psU], inc=(dc == NDC - 1))
                    sc.add("v", lambda e, oc=oc: e.scalar_tensor_tensor(
                        prod.ap[:, oc * 512:(oc + 1) * 512], psU.ap, sgc.ap[:, 1:2],
                        gate.ap[:, oc * 512:(oc + 1) * 512], ALU.mult, ALU.mult),
                        reads=[psU, sgc, gate], writes=[prod])
                sc.add("v", lambda e: e.tensor_reduce(
                    stat.ap[:, 5:6], prod.ap, AX.X, ALU.max,
                    apply_absolute_value=True), reads=[prod], writes=[stat])
                sc.add("g", lambda e, ti=ti: e.tensor_scalar(
                    mx2c.ap[:, ti:ti + 1], stat.ap[:, 5:6], 3.0e38, EPS,
                    ALU.min, ALU.max), reads=[stat], writes=[mx2c])
                sc.add("v", lambda e, ti=ti: e.reciprocal(
                    stat.ap[:, 6:7], mx2c.ap[:, ti:ti + 1]), reads=[mx2c], writes=[stat])
                sc.add("g", lambda e: e.tensor_scalar(
                    stat.ap[:, 7:8], stat.ap[:, 6:7], 127.0, 0.0, ALU.mult, ALU.add),
                    reads=[stat], writes=[stat])
                sc.add("s", lambda e: e.activation(
                    gate.ap, prod.ap, AF.Identity, bias=0.0, scale=stat.ap[:, 7:8]),
                    reads=[prod, stat], writes=[gate])
                sc.add("v", lambda e: e.tensor_scalar(
                    gate.ap, gate.ap, ceff.ap[:, 0:1], ceff.ap[:, 0:1],
                    ALU.add, ALU.subtract), reads=[gate, ceff], writes=[gate])
                sc.add("g", lambda e: e.tensor_scalar(
                    xq2b.ap, gate.ap, 127.0, -128.0, ALU.min, ALU.max),
                    reads=[gate], writes=[xq2b])
                for q8 in range(8):
                    pst = [psT0b, psT1b][q8 % 2]
                    for j in range(4):
                        ocb = q8 * 4 + j
                        sc.add("t", lambda e, ocb=ocb, j=j, pst=pst: e.transpose(
                            pst.ap[:, j * P:(j + 1) * P],
                            xq2b.ap[:, ocb * P:(ocb + 1) * P], identb.ap),
                            reads=[xq2b, identb], writes=[pst], inc=(j == 3))
                    sc.add("s", lambda e, q8=q8, pst=pst: e.copy(
                        xq2T.ap[:, q8 * 512:(q8 + 1) * 512], pst.ap),
                        reads=[pst], writes=[xq2T])
                dma(h["xq2scr"].rearrange("(kc p) t -> p kc t", p=P)
                    [:, :, ti * TA:(ti + 1) * TA],
                    xq2T.ap.rearrange("p (kc t) -> p kc t", kc=NIC),
                    reads=[xq2T], writes=[xq2scrB])
            sc.barrier()

        with contextlib.ExitStack() as stkB2:
            wdq = sb(stkB2, "wdq", [P, NIC * DM], BF16)
            with contextlib.ExitStack() as stkQ2:
                wtmp2 = [sb(stkQ2, f"wtmp2_{i}", [P, DM]) for i in range(2)]
                wqp2 = sb(stkQ2, "wqp2", [P, DM])
                psW2 = psb(stkQ2, "psW2", [P, 8])
                weight_quant(h["wdT"], wdq, wtmp2, wqp2, 2, NIC, DM, psW2)
                sc.barrier()

            xq2c = [sb(stkB2, f"xq2c{i}", [P, NIC * TA], BF16) for i in range(2)]
            x2c = [sb(stkB2, f"x2c{i}", [P, DM]) for i in range(2)]
            outb = sb(stkB2, "outb", [P, DM])
            psD0 = psb(stkB2, "psD0", [P, 512])
            psD1 = psb(stkB2, "psD1", [P, 512])
            sdc = sb(stkB2, "sdc", [P, 1])

            for ti in range(NTT):
                xc = xq2c[ti % 2]
                x2 = x2c[ti % 2]
                dma(xc.ap.rearrange("p (kc t) -> p kc t", kc=NIC),
                    h["xq2scr"].rearrange("(kc p) t -> p kc t", p=P)
                    [:, :, ti * TA:(ti + 1) * TA],
                    reads=[xq2scrB], writes=[xc])
                dma(x2.ap, h["x2scr"][ti * TA:(ti + 1) * TA, :],
                    reads=[x2scrB], writes=[x2])
                sc.add("g", lambda e, ti=ti: e.tensor_tensor(
                    sdc.ap, mx2c.ap[:, ti:ti + 1], mgs.ap[:, 2:3], ALU.mult),
                    reads=[mx2c, mgs], writes=[sdc])
                for oc2 in range(2):
                    pm = [psD0, psD1][oc2]
                    for kc in range(NIC):
                        sc.add("t", lambda e, oc2=oc2, kc=kc, xc=xc, pm=pm: e.matmul(
                            pm.ap, xc.ap[:, kc * TA:(kc + 1) * TA],
                            wdq.ap[:, kc * DM + oc2 * 512:kc * DM + (oc2 + 1) * 512],
                            start=(kc == 0), stop=(kc == NIC - 1)),
                            reads=[xc, wdq], writes=[pm], inc=(kc == NIC - 1))
                    sc.add("v", lambda e, oc2=oc2, x2=x2, pm=pm: e.scalar_tensor_tensor(
                        x2.ap[:, oc2 * 512:(oc2 + 1) * 512], pm.ap, sdc.ap[:, 0:1],
                        x2.ap[:, oc2 * 512:(oc2 + 1) * 512], ALU.mult, ALU.add),
                        reads=[pm, sdc, x2], writes=[x2])
                sc.add("s", lambda e, x2=x2: e.activation(
                    outb.ap, x2.ap, AF.Identity, bias=0.0, scale=1.0,
                    accum_out=stat.ap[:, 0:1]), reads=[x2], writes=[outb, stat])
                sc.add("s", lambda e, x2=x2: e.activation(
                    outb.ap, x2.ap, AF.Square, bias=0.0, scale=1.0,
                    accum_out=stat.ap[:, 1:2]), reads=[x2], writes=[outb, stat])
                ln_stats(sc, stat, DM, epsc)
                sc.add("g", lambda e, x2=x2: e.tensor_scalar(
                    outb.ap, x2.ap, stat.ap[:, 2:3], stat.ap[:, 4:5],
                    ALU.add, ALU.mult), reads=[x2, stat], writes=[outb])
                dma(h["out"][ti * TA:(ti + 1) * TA, :], outb.ap,
                    reads=[outb], writes=[outB])

        finwaits = []
        for j in range(NDSEM):
            if sc.dcnt[j]:
                finwaits.append((f"d{j}", 16 * sc.dcnt[j]))
        sc.ops["d"].append((finwaits, None, False, None))

        with nc.Block() as block:
            sc.emit(block, sems)
    return nc


_NC_CACHE = [None]


def kernel(**inputs):
    import ml_dtypes
    f32 = np.float32
    x = np.ascontiguousarray(np.asarray(inputs["L0_attn_x"], dtype=f32))

    def t(a):
        return np.ascontiguousarray(np.asarray(a, dtype=f32).T)

    def col8(v):
        return np.ascontiguousarray(np.asarray(v, dtype=f32).reshape(NDC, P).T)

    A8 = np.ascontiguousarray(
        np.asarray(inputs["L0_attn_A_log"], dtype=f32).reshape(NDC, P, NST)
        .transpose(1, 0, 2).reshape(P, NDC * NST))
    base = dict(
        wpT=t(inputs["L0_attn_proj_W"]), wfT=t(inputs["L0_attn_fconv_W"]),
        wbT=t(inputs["L0_attn_bconv_W"]), wdbcT=t(inputs["L0_attn_deltaBC_W"]),
        wdtT=t(inputs["L0_attn_dt_W"]), wgT=t(inputs["L0_ffn_w_gate"]),
        wuT=t(inputs["L0_ffn_w_up"]), wdT=t(inputs["L0_ffn_w_down"]),
        A_log8=A8,
        gamma8=col8(inputs["L0_attn_ln_w"]), beta8=col8(inputs["L0_attn_ln_b"]),
        bp8=col8(inputs["L0_attn_proj_b"]), bf8=col8(inputs["L0_attn_fconv_b"]),
        bb8=col8(inputs["L0_attn_bconv_b"]), dtb8=col8(inputs["L0_attn_dt_b"]),
        D8=col8(inputs["L0_attn_D"]),
        ident=np.eye(P, dtype=f32),
        identb=np.eye(P, dtype=ml_dtypes.bfloat16),
        ones=np.ones((P, P), dtype=f32),
    )
    in_maps = []
    for core in range(8):
        m = dict(base)
        m["x"] = np.ascontiguousarray(x[core * SPC:(core + 1) * SPC])
        in_maps.append(m)

    if _NC_CACHE[0] is None:
        _NC_CACHE[0] = build_nc()
    res = run_bass_kernel_spmd(_NC_CACHE[0], in_maps, core_ids=list(range(8)))
    out = np.concatenate([r["out"].reshape(SPC, SEQ, DM) for r in res.results], axis=0)
    return np.ascontiguousarray(out.astype(np.float32))


# version 07
# speedup vs baseline: 1.4306x; 1.3556x over previous
import numpy as np
import contextlib
import concourse.bass as bass
import concourse.mybir as mybir
from concourse.bass_utils import run_bass_kernel_spmd

F32 = mybir.dt.float32
BF16 = mybir.dt.bfloat16
AF = mybir.ActivationFunctionType
ALU = mybir.AluOpType
AX = mybir.AxisListType

P = 128
SPC = 2
SEQ = 2048
DM = 1024
NDC = DM // P
INNER = 4096
NIC = INNER // P
TA = 128
NCH = SEQ // TA
TS = 64
NSUB = TA // TS
NST = 16
DTR = 64
EPS = 1e-5
C32 = 1.5 * 2.0**23
C64 = 1.5 * 2.0**52
KMUL = float(np.float32((C64 - C32) * (2.0**40)))
NTT = SPC * NCH


class Buf:
    __slots__ = ("ap", "writer", "readers")

    def __init__(self, ap):
        self.ap = ap
        self.writer = None
        self.readers = []


NDSEM = 32


class Sched:
    ENG = ("t", "s", "v", "g")
    STREAMS = ("d", "t", "s", "v", "g")

    def __init__(self):
        self.nodes = []
        self.frontier = None

    def _deps(self, reads, writes):
        deps = []
        for b in reads:
            if b.writer is not None:
                deps.append(b.writer)
        for b in writes:
            deps.extend(b.readers)
            if b.writer is not None:
                deps.append(b.writer)
        if self.frontier is not None:
            deps.append(self.frontier)
        return deps

    def add(self, ek, buildfn, reads=(), writes=(), inc=True, cost=300.0):
        deps = self._deps(reads, writes)
        nid = len(self.nodes)
        self.nodes.append((ek, buildfn, float(cost), deps, False, 0.0))
        for b in reads:
            b.readers.append(nid)
        for b in writes:
            b.writer = nid
            b.readers = []
        return nid

    def add_dma(self, buildfn, reads=(), writes=(), xfer=2000.0):
        deps = self._deps(reads, writes)
        nid = len(self.nodes)
        self.nodes.append(("d", buildfn, 150.0, deps, True, float(xfer)))
        for b in reads:
            b.readers.append(nid)
        for b in writes:
            b.writer = nid
            b.readers = []
        return nid

    def barrier(self):
        deps = list(range(self.frontier + 1 if self.frontier is not None else 0,
                          len(self.nodes)))
        nid = len(self.nodes)
        self.nodes.append((None, None, 0.0, deps, False, 0.0))
        self.frontier = nid
        return nid

    def _schedule(self):
        import heapq
        n = len(self.nodes)
        succ = [[] for _ in range(n)]
        indeg = [0] * n
        for i, (_, _, _, deps, _, _) in enumerate(self.nodes):
            for d in set(deps):
                succ[d].append(i)
                indeg[i] += 1
        prio = [0.0] * n
        for i in range(n - 1, -1, -1):
            ek, _, cost, _, is_dma, xfer = self.nodes[i]
            best = 0.0
            for s_ in succ[i]:
                if prio[s_] > best:
                    best = prio[s_]
            prio[i] = best + cost + (xfer if is_dma else 0.0)
        ready = {e: [] for e in self.STREAMS}
        free = {e: 0.0 for e in self.STREAMS}
        fin = [0.0] * n
        depcnt = list(indeg)
        rdytime = [0.0] * n
        barrier_ready = []
        order = {e: [] for e in self.STREAMS}
        def push(i):
            ek = self.nodes[i][0]
            if ek is None:
                barrier_ready.append(i)
            else:
                heapq.heappush(ready[ek], (-prio[i], i))
        for i in range(n):
            if depcnt[i] == 0:
                push(i)
        ndone = 0
        while ndone < n:
            while barrier_ready:
                i = barrier_ready.pop()
                fin[i] = rdytime[i]
                ndone += 1
                for s_ in succ[i]:
                    depcnt[s_] -= 1
                    rdytime[s_] = max(rdytime[s_], fin[i])
                    if depcnt[s_] == 0:
                        push(s_)
                continue
            bestpick = None
            for e in self.STREAMS:
                while ready[e]:
                    i = ready[e][0][1]
                    start = max(free[e], rdytime[i])
                    if bestpick is None or start < bestpick[0] or \
                       (start == bestpick[0] and -ready[e][0][0] > bestpick[3]):
                        bestpick = (start, e, i, -ready[e][0][0])
                    break
            if bestpick is None:
                if barrier_ready:
                    continue
                raise RuntimeError("deadlock in scheduler")
            start, e, i, _ = bestpick
            import heapq as hq
            hq.heappop(ready[e])
            ek, _, cost, _, is_dma, xfer = self.nodes[i]
            end_issue = start + cost
            free[e] = end_issue
            fin[i] = end_issue + (xfer if is_dma else 0.0)
            order[e].append(i)
            ndone += 1
            for s_ in succ[i]:
                depcnt[s_] -= 1
                rdytime[s_] = max(rdytime[s_], fin[i])
                if depcnt[s_] == 0:
                    push(s_)
        self.est_span = max(fin) if n else 0.0
        return order

    def emit(self, block, sems):
        order = self._schedule()
        semid = [None] * len(self.nodes)
        cnt = {e: 0 for e in self.ENG}
        dcnt = [0] * NDSEM
        dmagate = [None] * len(self.nodes)
        dnext = 0
        for e in self.ENG:
            for i in order[e]:
                cnt[e] += 1
                semid[i] = (e, cnt[e])
        for i in order["d"]:
            j = dnext
            dnext = (j + 1) % NDSEM
            if dcnt[j] > 0:
                dmagate[i] = (f"d{j}", 16 * dcnt[j])
            dcnt[j] += 1
            semid[i] = (f"d{j}", 16 * dcnt[j])
        self.final_dma = [(f"d{j}", 16 * dcnt[j]) for j in range(NDSEM) if dcnt[j]]
        vc = {e: {} for e in self.STREAMS}
        snap = {}
        emitted = {e: [] for e in self.STREAMS}
        pos = {e: 0 for e in self.STREAMS}
        nexti = {e: 0 for e in self.STREAMS}
        done = [False] * len(self.nodes)
        def deps_done(i):
            return all(done[d] or self.nodes[d][0] is None and _bar_done(d)
                       for d in self.nodes[i][3])
        def _bar_done(b):
            return all(done[d] for d in self.nodes[b][3])
        progress = True
        total = sum(len(order[e]) for e in self.STREAMS)
        processed = 0
        while processed < total:
            progress = False
            for e in self.STREAMS:
                while nexti[e] < len(order[e]):
                    i = order[e][nexti[e]]
                    if not deps_done(i):
                        break
                    waits = []
                    mydeps = []
                    for d in set(self.nodes[i][3]):
                        if self.nodes[d][0] is None:
                            mydeps.extend(self.nodes[d][3])
                        else:
                            mydeps.append(d)
                    for d in set(mydeps):
                        dk, val = semid[d]
                        if vc[e].get(dk, 0) >= val:
                            continue
                        waits.append((dk, val))
                        ps = snap.get((dk, val))
                        if ps:
                            for k2, v2 in ps.items():
                                if v2 > vc[e].get(k2, 0):
                                    vc[e][k2] = v2
                        if val > vc[e].get(dk, 0):
                            vc[e][dk] = val
                    if dmagate[i] is not None:
                        gk, gv = dmagate[i]
                        if vc[e].get(gk, 0) < gv:
                            waits.append((gk, gv))
                            vc[e][gk] = gv
                    key, val = semid[i]
                    s_ = dict(vc[e])
                    s_[key] = val
                    snap[(key, val)] = s_
                    emitted[e].append((waits, self.nodes[i][1], key))
                    done[i] = True
                    nexti[e] += 1
                    processed += 1
                    progress = True
            if not progress:
                raise RuntimeError("wait-computation stuck")
        for ek, attr in (("d", "sync"), ("t", "tensor"), ("s", "scalar"),
                         ("v", "vector"), ("g", "gpsimd")):
            oplist = emitted[ek]
            if not oplist:
                continue

            def section(eng, _ops=oplist):
                for (waits, buildfn, key) in _ops:
                    for (dk, val) in waits:
                        eng.wait_ge(sems[dk], val)
                    if buildfn is None:
                        continue
                    inst = buildfn(eng)
                    inst.then_inc(sems[key[0] if len(key) == 1 else key],
                                  16 if key.startswith("d") else 1)

            getattr(block, attr)(section)


def ln_stats(sc, stat, n, epsc=None):
    inv = 1.0 / n
    sc.add("v", lambda e: e.tensor_scalar(stat.ap[:, 2:3], stat.ap[:, 0:1],
                                          -inv, None, ALU.mult),
           reads=[stat], writes=[stat])
    sc.add("g", lambda e: e.tensor_tensor(stat.ap[:, 3:4], stat.ap[:, 2:3],
                                          stat.ap[:, 2:3], ALU.mult),
           reads=[stat], writes=[stat])
    sc.add("g", lambda e: e.tensor_scalar(stat.ap[:, 1:2], stat.ap[:, 1:2],
                                          inv, 0.0, ALU.mult, ALU.add),
           reads=[stat], writes=[stat])
    sc.add("g", lambda e: e.tensor_tensor(stat.ap[:, 3:4], stat.ap[:, 1:2],
                                          stat.ap[:, 3:4], ALU.subtract),
           reads=[stat], writes=[stat])
    sc.add("s", lambda e: e.activation(stat.ap[:, 3:4], stat.ap[:, 3:4], AF.Sqrt,
                                       bias=epsc.ap[:, 0:1], scale=1.0),
           reads=[stat, epsc], writes=[stat])
    sc.add("v", lambda e: e.reciprocal(stat.ap[:, 4:5], stat.ap[:, 3:4]),
           reads=[stat], writes=[stat])


def softplus(sc, xb, spb):
    w = xb.ap.shape[1]
    sp = spb.ap[:, 0:w]
    sc.add("s", lambda e: e.activation(sp, xb.ap, AF.Abs, bias=0.0, scale=1.0),
           reads=[xb], writes=[spb], cost=1050)
    sc.add("s", lambda e: e.activation(sp, sp, AF.Exp, bias=0.0, scale=-1.0),
           reads=[spb], writes=[spb], cost=1050)
    sc.add("s", lambda e: e.activation(sp, sp, AF.Ln, bias=1.0, scale=1.0),
           reads=[spb], writes=[spb], cost=1050)
    sc.add("s", lambda e: e.activation(xb.ap, xb.ap, AF.Relu, bias=0.0, scale=1.0),
           reads=[xb], writes=[xb], cost=1050)
    sc.add("g", lambda e: e.tensor_tensor(xb.ap, xb.ap, sp, ALU.add),
           reads=[xb, spb], writes=[xb], cost=2100)


def build_nc():
    nc = bass.Bass("TRN2", target_bir_lowering=False)
    h = {}
    h["x"] = nc.dram_tensor("x", [SPC, SEQ, DM], F32, kind="ExternalInput").ap()
    for nm, shape in [("wpT", [DM, DM]), ("wfT", [DM, DM]), ("wbT", [DM, DM]),
                      ("wdbcT", [DM, 96]), ("wdtT", [DTR, DM]),
                      ("wgT", [DM, INNER]), ("wuT", [DM, INNER]), ("wdT", [INNER, DM])]:
        h[nm] = nc.dram_tensor(nm, shape, F32, kind="ExternalInput").ap()
    for nm in ("A_log8", "gamma8", "beta8", "bp8", "bf8", "bb8", "dtb8", "D8"):
        cols = NDC * NST if nm == "A_log8" else NDC
        h[nm] = nc.dram_tensor(nm, [P, cols], F32, kind="ExternalInput").ap()
    h["ident"] = nc.dram_tensor("ident", [P, P], F32, kind="ExternalInput").ap()
    h["identb"] = nc.dram_tensor("identb", [P, P], BF16, kind="ExternalInput").ap()
    h["ones"] = nc.dram_tensor("ones", [P, P], F32, kind="ExternalInput").ap()
    h["out"] = nc.dram_tensor("out", [SPC * SEQ, DM], F32, kind="ExternalOutput").ap()
    h["x2scr"] = nc.dram_tensor("x2scr", [SPC * SEQ, DM], F32, kind="Internal").ap()
    h["xq1scr"] = nc.dram_tensor("xq1scr", [DM, SPC * SEQ], BF16, kind="Internal").ap()
    h["xq2scr"] = nc.dram_tensor("xq2scr", [INNER, SPC * SEQ], BF16, kind="Internal").ap()
    bscr = [Buf(nc.dram_tensor(f"bscr{i}", [NST, TS], F32, kind="Internal").ap())
            for i in range(2)]
    cscr = [Buf(nc.dram_tensor(f"cscr{i}", [NST, TS], F32, kind="Internal").ap())
            for i in range(2)]
    x2scrB = Buf(h["x2scr"])
    xq1scrB = Buf(h["xq1scr"])
    xq2scrB = Buf(h["xq2scr"])
    outB = Buf(h["out"])

    sc = Sched()

    with contextlib.ExitStack() as main_stk:
        sems = {ek: main_stk.enter_context(nc.semaphore(f"S{ek}")) for ek in Sched.ENG}
        for j in range(NDSEM):
            sems[f"d{j}"] = main_stk.enter_context(nc.semaphore(f"Sd{j}"))

        def sb(stk, name, shape, dt=F32):
            return Buf(stk.enter_context(nc.sbuf_tensor(name, shape, dt)).ap())

        def psb(stk, name, shape, dt=F32):
            return Buf(stk.enter_context(nc.psum_tensor(name, shape, dt)).ap())

        def dma(dst, src, reads=(), writes=(), xfer=2000.0):
            return sc.add_dma(lambda e, d=dst, s=src: e.dma_start(out=d, in_=s),
                              reads=reads, writes=writes, xfer=xfer)

        ident = sb(main_stk, "identt", [P, P])
        identb = sb(main_stk, "identbt", [P, P], BF16)
        ones = sb(main_stk, "onest", [P, P])
        A_sb = sb(main_stk, "A_sb", [P, NDC * NST])
        D_sb = sb(main_stk, "D_sb", [P, NDC])
        dtb = sb(main_stk, "dtb", [P, NDC])
        bp = sb(main_stk, "bp", [P, NDC])
        bfc = sb(main_stk, "bfc", [P, NDC])
        bbc = sb(main_stk, "bbc", [P, NDC])
        gam = sb(main_stk, "gam", [P, NDC])
        bet = sb(main_stk, "bet", [P, NDC])
        bfold = sb(main_stk, "bfold", [P, NDC])
        ceff = sb(main_stk, "ceff", [P, 2])
        tmpc = sb(main_stk, "tmpc", [P, 2])
        mx1c = sb(main_stk, "mx1c", [P, NTT])
        mx2c = sb(main_stk, "mx2c", [P, NTT])
        stat = sb(main_stk, "stat", [P, 8])
        sc1 = sb(main_stk, "sc1", [P, NTT])
        mgs = sb(main_stk, "mgs", [P, 4])
        colt = sb(main_stk, "colt", [P, 4])
        epsc = sb(main_stk, "epsc", [P, 1])

        def weight_quant(wh, wq, wtmp, wqp, mcol, nkc, width, psw):
            nelem = float(nkc * P * width)
            for kc in range(nkc):
                wt = wtmp[kc % 2]
                dma(wt.ap, wh[kc * P:(kc + 1) * P, :], writes=[wt])
                sc.add("v", lambda e, wt=wt: e.tensor_reduce(
                    colt.ap[:, 0:1], wt.ap, AX.X, ALU.add, apply_absolute_value=True),
                    reads=[wt], writes=[colt], cost=7800)
                if kc == 0:
                    sc.add("g", lambda e: e.tensor_copy(colt.ap[:, 1:2], colt.ap[:, 0:1]),
                           reads=[colt], writes=[colt])
                else:
                    sc.add("g", lambda e: e.tensor_tensor(
                        colt.ap[:, 1:2], colt.ap[:, 1:2], colt.ap[:, 0:1], ALU.add),
                        reads=[colt], writes=[colt])
            sc.add("t", lambda e: e.matmul(psw.ap[:, 0:1], ones.ap, colt.ap[:, 1:2],
                                           start=True, stop=True),
                   reads=[ones, colt], writes=[psw])
            sc.add("s", lambda e: e.copy(colt.ap[:, 2:3], psw.ap[:, 0:1]),
                   reads=[psw], writes=[colt])
            sc.add("g", lambda e: e.tensor_scalar(colt.ap[:, 2:3], colt.ap[:, 2:3],
                                                  1.0 / nelem, 0.0, ALU.mult, ALU.add),
                   reads=[colt], writes=[colt])
            sc.add("g", lambda e: e.tensor_scalar(colt.ap[:, 3:4], colt.ap[:, 2:3],
                                                  3.0e38, EPS, ALU.min, ALU.max),
                   reads=[colt], writes=[colt])
            sc.add("g", lambda e: e.tensor_scalar(
                mgs.ap[:, mcol:mcol + 1], colt.ap[:, 3:4], 1.0 / 127.0, 0.0,
                ALU.mult, ALU.add), reads=[colt], writes=[mgs])
            sc.add("v", lambda e: e.reciprocal(colt.ap[:, 0:1], colt.ap[:, 3:4]),
                   reads=[colt], writes=[colt])
            for kc in range(nkc):
                wt = wtmp[kc % 2]
                dma(wt.ap, wh[kc * P:(kc + 1) * P, :], writes=[wt])
                sc.add("s", lambda e, wt=wt: e.activation(
                    wqp.ap, wt.ap, AF.Identity, bias=0.0, scale=colt.ap[:, 0:1]),
                    reads=[wt, colt], writes=[wqp], cost=3600)
                sc.add("v", lambda e: e.tensor_scalar(
                    wqp.ap, wqp.ap, ceff.ap[:, 0:1], ceff.ap[:, 0:1],
                    ALU.add, ALU.subtract), reads=[wqp, ceff], writes=[wqp], cost=4400)
                sc.add("g", lambda e, kc=kc: e.tensor_scalar(
                    wq.ap[:, kc * width:(kc + 1) * width], wqp.ap, 1.0, -1.0,
                    ALU.min, ALU.max), reads=[wqp], writes=[wq], cost=7800)

        with contextlib.ExitStack() as stk0:
            wpT = sb(stk0, "wpT_s", [P, NDC * DM])
            wfT = sb(stk0, "wfT_s", [P, NDC * DM])
            wbT = sb(stk0, "wbT_s", [P, NDC * DM])
            wdbcT = sb(stk0, "wdbcT_s", [P, NDC * 96])
            wdtT = sb(stk0, "wdtT_s", [DTR, NDC * P])

            for nm, b in [("ident", ident), ("identb", identb), ("ones", ones),
                          ("A_log8", A_sb), ("D8", D_sb), ("dtb8", dtb),
                          ("bp8", bp), ("bf8", bfc), ("bb8", bbc),
                          ("gamma8", gam), ("beta8", bet)]:
                dma(b.ap, h[nm], writes=[b])
            for dc in range(NDC):
                dma(wpT.ap[:, dc * DM:(dc + 1) * DM], h["wpT"][dc * P:(dc + 1) * P, :], writes=[wpT])
                dma(wfT.ap[:, dc * DM:(dc + 1) * DM], h["wfT"][dc * P:(dc + 1) * P, :], writes=[wfT])
                dma(wbT.ap[:, dc * DM:(dc + 1) * DM], h["wbT"][dc * P:(dc + 1) * P, :], writes=[wbT])
                dma(wdbcT.ap[:, dc * 96:(dc + 1) * 96], h["wdbcT"][dc * P:(dc + 1) * P, :], writes=[wdbcT])
                dma(wdtT.ap[:, dc * P:(dc + 1) * P], h["wdtT"][:, dc * P:(dc + 1) * P], writes=[wdtT])

            sc.add("v", lambda e: e.tensor_scalar(tmpc.ap, ones.ap[:, 0:2], 1.0, None,
                                                  ALU.mult), reads=[ones], writes=[tmpc])
            sc.add("g", lambda e: e.tensor_scalar(tmpc.ap, tmpc.ap, 2.0**-40, 1.0,
                                                  ALU.add, ALU.mult),
                   reads=[tmpc], writes=[tmpc])
            sc.add("v", lambda e: e.tensor_scalar(tmpc.ap, tmpc.ap, -1.0, None, ALU.add),
                   reads=[tmpc], writes=[tmpc])
            sc.add("g", lambda e: e.tensor_scalar(ceff.ap, tmpc.ap, KMUL, C32,
                                                  ALU.mult, ALU.add),
                   reads=[tmpc], writes=[ceff])

            sc.add("g", lambda e: e.tensor_scalar(epsc.ap, ones.ap[:, 0:1], EPS, 0.0,
                                                  ALU.mult, ALU.add),
                   reads=[ones], writes=[epsc])
            sc.add("s", lambda e: e.activation(A_sb.ap, A_sb.ap, AF.Exp,
                                               bias=0.0, scale=1.0),
                   reads=[A_sb], writes=[A_sb])
            sc.add("g", lambda e: e.tensor_scalar(A_sb.ap, A_sb.ap, -1.0, 0.0,
                                                  ALU.mult, ALU.add),
                   reads=[A_sb], writes=[A_sb])

            with contextlib.ExitStack() as stkA:
                x_sbP = [sb(stkA, f"x_sb{i}", [P, DM]) for i in range(2)]
                xn_sb = sb(stkA, "xn_sb", [P, DM])
                xpTP = [sb(stkA, f"xpT{i}", [P, NDC * TA]) for i in range(2)]
                uT = sb(stkA, "uT", [P, NDC * TA])
                deltaTP = [sb(stkA, f"deltaT{i}", [P, NDC * TA]) for i in range(2)]
                duTP = [sb(stkA, f"duT{i}", [P, NDC * TA]) for i in range(2)]
                ysumP = [sb(stkA, f"ysum{i}", [P, NDC * TA]) for i in range(2)]
                spP = [sb(stkA, f"sp{i}", [P, DM]) for i in range(2)]
                dbcP = [Buf(stkA.enter_context(nc.sbuf_tensor(f"dbc{i}", [96, TA], F32)).ap())
                        for i in range(2)]
                brepP = [sb(stkA, f"brep{i}", [P, NST * TS]) for i in range(2)]
                crepP = [sb(stkA, f"crep{i}", [P, NST * TS]) for i in range(2)]
                ebuf = [sb(stkA, f"ebuf{i}", [P, NST * (TS + 1)]) for i in range(2)]
                bbuf = [sb(stkA, f"bbuf{i}", [P, NST * (TS + 1)]) for i in range(2)]
                hbuf = sb(stkA, "hbuf", [P, NST * (TS + 1)])
                hcar = sb(stkA, "hcar", [P, 2 * NDC * NST])
                yscP = [sb(stkA, f"ysc{i}", [P, TS]) for i in range(2)]
                statP = [sb(stkA, f"stat{i}", [P, 8]) for i in range(2)]
                xq1b = sb(stkA, "xq1b", [P, DM], BF16)
                xq1T = sb(stkA, "xq1T", [P, NDC * TA], BF16)
                psA = psb(stkA, "psA", [P, 512])
                psA2 = psb(stkA, "psA2", [P, 512])
                psDT = psb(stkA, "psDT", [P, 512])
                psT0 = psb(stkA, "psT0", [P, 512])
                psT1 = psb(stkA, "psT1", [P, 512])
                psDB = psb(stkA, "psDB", [96, TA])
                psTB0 = psb(stkA, "psTB0", [P, 512], BF16)
                psTB1 = psb(stkA, "psTB1", [P, 512], BF16)
                psmm = [psA, psA2]

                def e3(b):
                    return b.ap.rearrange("p (n t) -> p n t", n=NST)

                for eb in ebuf:
                    sc.add("g", lambda e, eb=eb: e.memset(e3(eb)[:, :, 0:1], 0.0),
                           writes=[eb])

                for oc in range(NDC):
                    for dc in range(NDC):
                        sc.add("t", lambda e, oc=oc, dc=dc: e.matmul(
                            psA.ap[:, 0:1],
                            wpT.ap[:, dc * DM + oc * P: dc * DM + (oc + 1) * P],
                            bet.ap[:, dc:dc + 1],
                            start=(dc == 0), stop=(dc == NDC - 1)),
                            reads=[wpT, bet], writes=[psA], inc=(dc == NDC - 1))
                    sc.add("s", lambda e, oc=oc: e.activation(
                        bfold.ap[:, oc:oc + 1], psA.ap[:, 0:1], AF.Identity,
                        bias=bp.ap[:, oc:oc + 1], scale=1.0),
                        reads=[psA, bp], writes=[bfold])
                for dc in range(NDC):
                    sc.add("v", lambda e, dc=dc: e.tensor_scalar(
                        wpT.ap[:, dc * DM:(dc + 1) * DM], wpT.ap[:, dc * DM:(dc + 1) * DM],
                        gam.ap[:, dc:dc + 1], 1.0, ALU.mult, ALU.mult),
                        reads=[wpT, gam], writes=[wpT])

                def do_head(s, c, ti, x_sb, xpT, deltaT, duT, ysum, sp, dbc, stat):
                    t0 = c * TA
                    dma(x_sb.ap, h["x"][s, t0:t0 + TA, :], writes=[x_sb])
                    sc.add("s", lambda e: e.activation(
                        x_sb.ap, x_sb.ap, AF.Identity, bias=0.0, scale=1.0,
                        accum_out=stat.ap[:, 0:1]), reads=[x_sb], writes=[x_sb, stat], cost=1100)
                    sc.add("s", lambda e: e.activation(
                        sp.ap, x_sb.ap, AF.Square, bias=0.0, scale=1.0,
                        accum_out=stat.ap[:, 1:2]), reads=[x_sb], writes=[sp, stat], cost=1100)
                    ln_stats(sc, stat, DM, epsc)
                    sc.add("g", lambda e: e.tensor_scalar(
                        xn_sb.ap, x_sb.ap, stat.ap[:, 2:3], stat.ap[:, 4:5],
                        ALU.add, ALU.mult), reads=[x_sb, stat], writes=[xn_sb], cost=1100)
                    for half in range(2):
                        pst = [psT0, psT1][half]
                        for j in range(4):
                            dc = half * 4 + j
                            sc.add("t", lambda e, dc=dc, j=j, pst=pst: e.transpose(
                                pst.ap[:, j * P:(j + 1) * P],
                                xn_sb.ap[:, dc * P:(dc + 1) * P], ident.ap),
                                reads=[xn_sb, ident], writes=[pst], inc=(j == 3))
                        sc.add("s", lambda e, half=half, pst=pst: e.copy(
                            sp.ap[:, half * 512:(half + 1) * 512], pst.ap),
                            reads=[pst], writes=[sp])
                    for oc in range(NDC):
                        pm = psmm[oc % 2]
                        for dc in range(NDC):
                            sc.add("t", lambda e, oc=oc, dc=dc, pm=pm: e.matmul(
                                pm.ap[:, 0:TA],
                                wpT.ap[:, dc * DM + oc * P: dc * DM + (oc + 1) * P],
                                sp.ap[:, dc * P:(dc + 1) * P],
                                start=(dc == 0), stop=(dc == NDC - 1)),
                                reads=[wpT, sp], writes=[pm], inc=(dc == NDC - 1))
                        sc.add("s", lambda e, oc=oc, pm=pm: e.activation(
                            xpT.ap[:, oc * TA:(oc + 1) * TA], pm.ap[:, 0:TA],
                            AF.Identity, bias=bfold.ap[:, oc:oc + 1], scale=1.0),
                            reads=[pm, bfold], writes=[xpT])

                    for br, (wb, bb_) in enumerate([(wfT, bfc), (wbT, bbc)]):
                        for oc in range(NDC):
                            pm = psmm[oc % 2]
                            for dc in range(NDC):
                                sc.add("t", lambda e, oc=oc, dc=dc, pm=pm, wb=wb: e.matmul(
                                    pm.ap[:, 0:TA],
                                    wb.ap[:, dc * DM + oc * P: dc * DM + (oc + 1) * P],
                                    xpT.ap[:, dc * TA:(dc + 1) * TA],
                                    start=(dc == 0), stop=(dc == NDC - 1)),
                                    reads=[wb, xpT], writes=[pm], inc=(dc == NDC - 1))
                            sc.add("s", lambda e, oc=oc, pm=pm, bb_=bb_: e.activation(
                                uT.ap[:, oc * TA:(oc + 1) * TA], pm.ap[:, 0:TA],
                                AF.Identity, bias=bb_.ap[:, oc:oc + 1], scale=1.0),
                                reads=[pm, bb_], writes=[uT])
                        softplus(sc, uT, sp)
                        for dc in range(NDC):
                            sc.add("t", lambda e, dc=dc: e.matmul(
                                psDB.ap,
                                wdbcT.ap[:, dc * 96:(dc + 1) * 96],
                                uT.ap[:, dc * TA:(dc + 1) * TA],
                                start=(dc == 0), stop=(dc == NDC - 1)),
                                reads=[wdbcT, uT], writes=[psDB], inc=(dc == NDC - 1))
                        sc.add("s", lambda e: e.copy(dbc.ap, psDB.ap),
                               reads=[psDB], writes=[dbc])
                        for g4 in range(2):
                            for j in range(4):
                                dc = g4 * 4 + j
                                sc.add("t", lambda e, dc=dc, j=j: e.matmul(
                                    psDT.ap[:, j * TA:(j + 1) * TA],
                                    wdtT.ap[:, dc * P:(dc + 1) * P],
                                    dbc.ap[0:DTR, :],
                                    start=True, stop=True),
                                    reads=[wdtT, dbc], writes=[psDT], inc=(j == 3))
                            for j in range(4):
                                dc = g4 * 4 + j
                                sc.add("s", lambda e, dc=dc, j=j: e.activation(
                                    deltaT.ap[:, dc * TA:(dc + 1) * TA],
                                    psDT.ap[:, j * TA:(j + 1) * TA],
                                    AF.Identity, bias=dtb.ap[:, dc:dc + 1], scale=1.0),
                                    reads=[psDT, dtb], writes=[deltaT])
                        softplus(sc, deltaT, sp)
                        sc.add("g", lambda e: e.tensor_tensor(
                            duT.ap, deltaT.ap, uT.ap, ALU.mult),
                            reads=[deltaT, uT], writes=[duT], cost=2100)
                        if br == 0:
                            for dc in range(NDC):
                                sc.add("g", lambda e, dc=dc: e.tensor_scalar(
                                    ysum.ap[:, dc * TA:(dc + 1) * TA],
                                    uT.ap[:, dc * TA:(dc + 1) * TA],
                                    D_sb.ap[:, dc:dc + 1], 0.0, ALU.mult, ALU.add),
                                    reads=[uT, D_sb], writes=[ysum])
                        else:
                            for dc in range(NDC):
                                sc.add("g", lambda e, dc=dc: e.tensor_scalar(
                                    sp.ap[:, dc * P:(dc + 1) * P],
                                    uT.ap[:, dc * TA:(dc + 1) * TA],
                                    D_sb.ap[:, dc:dc + 1], 0.0, ALU.mult, ALU.add),
                                    reads=[uT, D_sb], writes=[sp])
                            sc.add("g", lambda e: e.tensor_tensor(
                                ysum.ap, ysum.ap, sp.ap, ALU.add),
                                reads=[ysum, sp], writes=[ysum])
                        for sub in range(NSUB):
                            ts0 = sub * TS
                            pb = (c * NSUB + sub) % 2
                            dma(bscr[pb].ap, dbc.ap[DTR:DTR + NST, ts0:ts0 + TS],
                                reads=[dbc], writes=[bscr[pb]])
                            dma(cscr[pb].ap, dbc.ap[DTR + NST:96, ts0:ts0 + TS],
                                reads=[dbc], writes=[cscr[pb]])
                            bflat = bscr[pb].ap.rearrange("n t -> (n t)") \
                                .unsqueeze(0).broadcast_to([P, NST * TS])
                            cflat = cscr[pb].ap.rearrange("n t -> (n t)") \
                                .unsqueeze(0).broadcast_to([P, NST * TS])
                            dma(brepP[pb].ap, bflat, reads=[bscr[pb]], writes=[brepP[pb]])
                            dma(crepP[pb].ap, cflat, reads=[cscr[pb]], writes=[crepP[pb]])
                            brep, crep = brepP[pb], crepP[pb]

                            def emit_e(dc):
                                eb = ebuf[(dc + sub) % 2]
                                col0 = dc * TA + ts0
                                ek = "v" if dc % 2 == 0 else "g"
                                sc.add(ek, lambda e, eb=eb, dc=dc, col0=col0: e.tensor_tensor(
                                    e3(eb)[:, :, 1:],
                                    deltaT.ap[:, col0:col0 + TS]
                                    .unsqueeze(1).broadcast_to([P, NST, TS]),
                                    A_sb.ap[:, dc * NST:(dc + 1) * NST]
                                    .unsqueeze(2).broadcast_to([P, NST, TS]),
                                    ALU.mult), reads=[deltaT, A_sb], writes=[eb],
                                    cost=1300 if ek == "v" else 1900)

                            def emit_exp(dc):
                                eb = ebuf[(dc + sub) % 2]
                                sc.add("s", lambda e, eb=eb: e.activation(
                                    e3(eb)[:, :, 1:], e3(eb)[:, :, 1:], AF.Exp,
                                    bias=0.0, scale=1.0), reads=[eb], writes=[eb], cost=1100)

                            def emit_b(dc):
                                bbf = bbuf[(dc + sub) % 2]
                                col0 = dc * TA + ts0
                                hsl = slice((br * NDC + dc) * NST,
                                            (br * NDC + dc + 1) * NST)
                                sc.add("g", lambda e, bbf=bbf, hsl=hsl: e.tensor_copy(
                                    e3(bbf)[:, :, 0:1], hcar.ap[:, hsl].unsqueeze(2)),
                                    reads=[hcar], writes=[bbf])
                                sc.add("g", lambda e, bbf=bbf, col0=col0, brep=brep: e.tensor_tensor(
                                    e3(bbf)[:, :, 1:],
                                    duT.ap[:, col0:col0 + TS]
                                    .unsqueeze(1).broadcast_to([P, NST, TS]),
                                    brep.ap.rearrange("p (n t) -> p n t", n=NST),
                                    ALU.mult), reads=[duT, brep], writes=[bbf], cost=1950)

                            def emit_red(dc):
                                eb = ebuf[(dc + sub) % 2]
                                ysc = yscP[dc % 2]
                                col0 = dc * TA + ts0
                                sc.add("v", lambda e, eb=eb, ysc=ysc: e.tensor_reduce(
                                    ysc.ap, e3(eb)[:, :, 1:].transpose([0, 2, 1]),
                                    AX.X, ALU.add), reads=[eb], writes=[ysc], cost=1950)
                                sc.add("g", lambda e, col0=col0, ysc=ysc: e.tensor_tensor(
                                    ysum.ap[:, col0:col0 + TS],
                                    ysum.ap[:, col0:col0 + TS],
                                    ysc.ap, ALU.add), reads=[ysc, ysum], writes=[ysum], cost=300)

                            emit_e(0)
                            emit_exp(0)
                            emit_b(0)
                            for dc in range(NDC):
                                eb = ebuf[(dc + sub) % 2]
                                bbf = bbuf[(dc + sub) % 2]
                                hb = hbuf
                                hsl = slice((br * NDC + dc) * NST,
                                            (br * NDC + dc + 1) * NST)
                                sc.add("v", lambda e, hb=hb, eb=eb, bbf=bbf: e.tensor_tensor_scan(
                                    hb.ap, eb.ap, bbf.ap, 0.0, ALU.mult, ALU.add),
                                    reads=[eb, bbf], writes=[hb], cost=2250)
                                if dc > 0:
                                    emit_red(dc - 1)
                                if dc + 1 < NDC:
                                    emit_e(dc + 1)
                                    emit_b(dc + 1)
                                sc.add("g", lambda e, hb=hb, hsl=hsl: e.tensor_copy(
                                    hcar.ap[:, hsl], e3(hb)[:, :, TS:TS + 1].squeeze(2)),
                                    reads=[hb], writes=[hcar])
                                sc.add("g", lambda e, eb=eb, hb=hb, crep=crep: e.tensor_tensor(
                                    e3(eb)[:, :, 1:], e3(hb)[:, :, 1:],
                                    crep.ap.rearrange("p (n t) -> p n t", n=NST),
                                    ALU.mult), reads=[hb, crep], writes=[eb], cost=1900)
                                if dc + 1 < NDC:
                                    emit_exp(dc + 1)
                            emit_red(NDC - 1)

                def do_tail(s, c, ti, x_sb, xpT, deltaT, duT, ysum, sp, dbc, stat):
                    sc.add("s", lambda e: e.activation(xpT.ap, xpT.ap, AF.Silu,
                                                       bias=0.0, scale=1.0),
                           reads=[xpT], writes=[xpT], cost=1100)
                    sc.add("g", lambda e: e.tensor_tensor(
                        ysum.ap, ysum.ap, xpT.ap, ALU.mult),
                        reads=[ysum, xpT], writes=[ysum], cost=2100)
                    for half in range(2):
                        pst = [psT0, psT1][half]
                        for j in range(4):
                            dc = half * 4 + j
                            sc.add("t", lambda e, dc=dc, j=j, pst=pst: e.transpose(
                                pst.ap[:, j * P:(j + 1) * P],
                                ysum.ap[:, dc * TA:(dc + 1) * TA], ident.ap),
                                reads=[ysum, ident], writes=[pst], inc=(j == 3))
                        sc.add("v", lambda e, half=half, pst=pst: e.scalar_tensor_tensor(
                            x_sb.ap[:, half * 512:(half + 1) * 512],
                            x_sb.ap[:, half * 512:(half + 1) * 512], 2.0, pst.ap,
                            ALU.mult, ALU.add), reads=[x_sb, pst], writes=[x_sb])
                    sc.add("s", lambda e: e.activation(
                        x_sb.ap, x_sb.ap, AF.Identity, bias=0.0, scale=1.0,
                        accum_out=stat.ap[:, 0:1]), reads=[x_sb], writes=[x_sb, stat], cost=1100)
                    sc.add("s", lambda e: e.activation(
                        sp.ap, x_sb.ap, AF.Square, bias=0.0, scale=1.0,
                        accum_out=stat.ap[:, 1:2]), reads=[x_sb], writes=[sp, stat], cost=1100)
                    ln_stats(sc, stat, DM, epsc)
                    sc.add("g", lambda e: e.tensor_scalar(
                        x_sb.ap, x_sb.ap, stat.ap[:, 2:3], stat.ap[:, 4:5],
                        ALU.add, ALU.mult), reads=[x_sb, stat], writes=[x_sb], cost=1100)
                    dma(h["x2scr"][ti * TA:(ti + 1) * TA, :], x_sb.ap,
                        reads=[x_sb], writes=[x2scrB])
                    sc.add("v", lambda e: e.tensor_reduce(
                        stat.ap[:, 5:6], x_sb.ap, AX.X, ALU.max,
                        apply_absolute_value=True), reads=[x_sb], writes=[stat], cost=2000)
                    sc.add("g", lambda e: e.tensor_scalar(
                        mx1c.ap[:, ti:ti + 1], stat.ap[:, 5:6], 3.0e38, EPS,
                        ALU.min, ALU.max), reads=[stat], writes=[mx1c])
                    sc.add("v", lambda e: e.reciprocal(
                        stat.ap[:, 6:7], mx1c.ap[:, ti:ti + 1]),
                        reads=[mx1c], writes=[stat])
                    sc.add("g", lambda e: e.tensor_scalar(
                        sc1.ap[:, ti:ti + 1], stat.ap[:, 6:7], 127.0, 0.0,
                        ALU.mult, ALU.add), reads=[stat], writes=[sc1])
                    sc.add("s", lambda e: e.activation(
                        sp.ap, x_sb.ap, AF.Identity, bias=0.0,
                        scale=sc1.ap[:, ti:ti + 1]), reads=[x_sb, sc1], writes=[sp], cost=1100)
                    sc.add("v", lambda e: e.tensor_scalar(
                        sp.ap, sp.ap, ceff.ap[:, 0:1], ceff.ap[:, 0:1],
                        ALU.add, ALU.subtract), reads=[sp, ceff], writes=[sp], cost=1200)
                    sc.add("g", lambda e: e.tensor_scalar(
                        xq1b.ap, sp.ap, 127.0, -128.0, ALU.min, ALU.max),
                        reads=[sp], writes=[xq1b], cost=2000)
                    for half in range(2):
                        pst = [psTB0, psTB1][half]
                        for j in range(4):
                            dc = half * 4 + j
                            sc.add("t", lambda e, dc=dc, j=j, pst=pst: e.transpose(
                                pst.ap[:, j * P:(j + 1) * P],
                                xq1b.ap[:, dc * P:(dc + 1) * P], identb.ap),
                                reads=[xq1b, identb], writes=[pst], inc=(j == 3))
                        sc.add("s", lambda e, half=half, pst=pst: e.copy(
                            xq1T.ap[:, half * 512:(half + 1) * 512], pst.ap),
                            reads=[pst], writes=[xq1T])
                    dma(h["xq1scr"].rearrange("(dc p) t -> p dc t", p=P)
                        [:, :, ti * TA:(ti + 1) * TA],
                        xq1T.ap.rearrange("p (dc t) -> p dc t", dc=NDC),
                        reads=[xq1T], writes=[xq1scrB])

                def args_of(ti):
                    pi = ti % 2
                    return (ti // NCH, ti % NCH, ti, x_sbP[pi], xpTP[pi], deltaTP[pi],
                            duTP[pi], ysumP[pi], spP[pi], dbcP[pi], statP[pi])

                for ti in range(NTT):
                    if ti % NCH == 0:
                        sc.add("g", lambda e: e.memset(hcar.ap, 0.0), writes=[hcar])
                    do_head(*args_of(ti))
                    if ti > 0:
                        do_tail(*args_of(ti - 1))
                do_tail(*args_of(NTT - 1))

            sc.barrier()

        with contextlib.ExitStack() as stkB1:
            wgq = sb(stkB1, "wgq", [P, NDC * INNER], BF16)
            wuq = sb(stkB1, "wuq", [P, NDC * INNER], BF16)
            with contextlib.ExitStack() as stkQ:
                wtmp = [sb(stkQ, f"wtmp{i}", [P, INNER]) for i in range(2)]
                wqp = sb(stkQ, "wqp", [P, INNER])
                psW = psb(stkQ, "psW", [P, 8])
                weight_quant(h["wgT"], wgq, wtmp, wqp, 0, NDC, INNER, psW)
                weight_quant(h["wuT"], wuq, wtmp, wqp, 1, NDC, INNER, psW)
                sc.barrier()

            xq1c = [sb(stkB1, f"xq1c{i}", [P, NDC * TA], BF16) for i in range(2)]
            gate = sb(stkB1, "gate", [P, INNER])
            prod = sb(stkB1, "prod", [P, INNER])
            xq2b = sb(stkB1, "xq2b", [P, INNER], BF16)
            xq2T = sb(stkB1, "xq2T", [P, NIC * TA], BF16)
            psG = psb(stkB1, "psG", [P, 512])
            psU = psb(stkB1, "psU", [P, 512])
            psT0b = psb(stkB1, "psT0b", [P, 512], BF16)
            psT1b = psb(stkB1, "psT1b", [P, 512], BF16)
            sgc = sb(stkB1, "sgc", [P, 2])

            for ti in range(NTT):
                xc = xq1c[ti % 2]
                dma(xc.ap.rearrange("p (dc t) -> p dc t", dc=NDC),
                    h["xq1scr"].rearrange("(dc p) t -> p dc t", p=P)
                    [:, :, ti * TA:(ti + 1) * TA],
                    reads=[xq1scrB], writes=[xc])
                sc.add("g", lambda e, ti=ti: e.tensor_tensor(
                    sgc.ap[:, 0:1], mx1c.ap[:, ti:ti + 1], mgs.ap[:, 0:1], ALU.mult),
                    reads=[mx1c, mgs], writes=[sgc])
                sc.add("g", lambda e, ti=ti: e.tensor_tensor(
                    sgc.ap[:, 1:2], mx1c.ap[:, ti:ti + 1], mgs.ap[:, 1:2], ALU.mult),
                    reads=[mx1c, mgs], writes=[sgc])
                for oc in range(8):
                    for dc in range(NDC):
                        sc.add("t", lambda e, oc=oc, dc=dc, xc=xc: e.matmul(
                            psG.ap, xc.ap[:, dc * TA:(dc + 1) * TA],
                            wgq.ap[:, dc * INNER + oc * 512:dc * INNER + (oc + 1) * 512],
                            start=(dc == 0), stop=(dc == NDC - 1)),
                            reads=[xc, wgq], writes=[psG], inc=(dc == NDC - 1))
                    sc.add("s", lambda e, oc=oc: e.activation(
                        gate.ap[:, oc * 512:(oc + 1) * 512], psG.ap, AF.Silu,
                        bias=0.0, scale=sgc.ap[:, 0:1]),
                        reads=[psG, sgc], writes=[gate], cost=600)
                    for dc in range(NDC):
                        sc.add("t", lambda e, oc=oc, dc=dc, xc=xc: e.matmul(
                            psU.ap, xc.ap[:, dc * TA:(dc + 1) * TA],
                            wuq.ap[:, dc * INNER + oc * 512:dc * INNER + (oc + 1) * 512],
                            start=(dc == 0), stop=(dc == NDC - 1)),
                            reads=[xc, wuq], writes=[psU], inc=(dc == NDC - 1))
                    sc.add("v", lambda e, oc=oc: e.scalar_tensor_tensor(
                        prod.ap[:, oc * 512:(oc + 1) * 512], psU.ap, sgc.ap[:, 1:2],
                        gate.ap[:, oc * 512:(oc + 1) * 512], ALU.mult, ALU.mult),
                        reads=[psU, sgc, gate], writes=[prod], cost=700)
                sc.add("v", lambda e: e.tensor_reduce(
                    stat.ap[:, 5:6], prod.ap, AX.X, ALU.max,
                    apply_absolute_value=True), reads=[prod], writes=[stat], cost=7800)
                sc.add("g", lambda e, ti=ti: e.tensor_scalar(
                    mx2c.ap[:, ti:ti + 1], stat.ap[:, 5:6], 3.0e38, EPS,
                    ALU.min, ALU.max), reads=[stat], writes=[mx2c])
                sc.add("v", lambda e, ti=ti: e.reciprocal(
                    stat.ap[:, 6:7], mx2c.ap[:, ti:ti + 1]), reads=[mx2c], writes=[stat])
                sc.add("g", lambda e: e.tensor_scalar(
                    stat.ap[:, 7:8], stat.ap[:, 6:7], 127.0, 0.0, ALU.mult, ALU.add),
                    reads=[stat], writes=[stat])
                sc.add("s", lambda e: e.activation(
                    gate.ap, prod.ap, AF.Identity, bias=0.0, scale=stat.ap[:, 7:8]),
                    reads=[prod, stat], writes=[gate], cost=3600)
                sc.add("v", lambda e: e.tensor_scalar(
                    gate.ap, gate.ap, ceff.ap[:, 0:1], ceff.ap[:, 0:1],
                    ALU.add, ALU.subtract), reads=[gate, ceff], writes=[gate], cost=4400)
                sc.add("g", lambda e: e.tensor_scalar(
                    xq2b.ap, gate.ap, 127.0, -128.0, ALU.min, ALU.max),
                    reads=[gate], writes=[xq2b], cost=7800)
                for q8 in range(8):
                    pst = [psT0b, psT1b][q8 % 2]
                    for j in range(4):
                        ocb = q8 * 4 + j
                        sc.add("t", lambda e, ocb=ocb, j=j, pst=pst: e.transpose(
                            pst.ap[:, j * P:(j + 1) * P],
                            xq2b.ap[:, ocb * P:(ocb + 1) * P], identb.ap),
                            reads=[xq2b, identb], writes=[pst], inc=(j == 3))
                    sc.add("s", lambda e, q8=q8, pst=pst: e.copy(
                        xq2T.ap[:, q8 * 512:(q8 + 1) * 512], pst.ap),
                        reads=[pst], writes=[xq2T])
                dma(h["xq2scr"].rearrange("(kc p) t -> p kc t", p=P)
                    [:, :, ti * TA:(ti + 1) * TA],
                    xq2T.ap.rearrange("p (kc t) -> p kc t", kc=NIC),
                    reads=[xq2T], writes=[xq2scrB])
            sc.barrier()

        with contextlib.ExitStack() as stkB2:
            wdq = sb(stkB2, "wdq", [P, NIC * DM], BF16)
            with contextlib.ExitStack() as stkQ2:
                wtmp2 = [sb(stkQ2, f"wtmp2_{i}", [P, DM]) for i in range(2)]
                wqp2 = sb(stkQ2, "wqp2", [P, DM])
                psW2 = psb(stkQ2, "psW2", [P, 8])
                weight_quant(h["wdT"], wdq, wtmp2, wqp2, 2, NIC, DM, psW2)
                sc.barrier()

            xq2c = [sb(stkB2, f"xq2c{i}", [P, NIC * TA], BF16) for i in range(2)]
            x2c = [sb(stkB2, f"x2c{i}", [P, DM]) for i in range(2)]
            outb = sb(stkB2, "outb", [P, DM])
            psD0 = psb(stkB2, "psD0", [P, 512])
            psD1 = psb(stkB2, "psD1", [P, 512])
            sdc = sb(stkB2, "sdc", [P, 1])

            for ti in range(NTT):
                xc = xq2c[ti % 2]
                x2 = x2c[ti % 2]
                dma(xc.ap.rearrange("p (kc t) -> p kc t", kc=NIC),
                    h["xq2scr"].rearrange("(kc p) t -> p kc t", p=P)
                    [:, :, ti * TA:(ti + 1) * TA],
                    reads=[xq2scrB], writes=[xc])
                dma(x2.ap, h["x2scr"][ti * TA:(ti + 1) * TA, :],
                    reads=[x2scrB], writes=[x2])
                sc.add("g", lambda e, ti=ti: e.tensor_tensor(
                    sdc.ap, mx2c.ap[:, ti:ti + 1], mgs.ap[:, 2:3], ALU.mult),
                    reads=[mx2c, mgs], writes=[sdc])
                for oc2 in range(2):
                    pm = [psD0, psD1][oc2]
                    for kc in range(NIC):
                        sc.add("t", lambda e, oc2=oc2, kc=kc, xc=xc, pm=pm: e.matmul(
                            pm.ap, xc.ap[:, kc * TA:(kc + 1) * TA],
                            wdq.ap[:, kc * DM + oc2 * 512:kc * DM + (oc2 + 1) * 512],
                            start=(kc == 0), stop=(kc == NIC - 1)),
                            reads=[xc, wdq], writes=[pm], inc=(kc == NIC - 1))
                    sc.add("v", lambda e, oc2=oc2, x2=x2, pm=pm: e.scalar_tensor_tensor(
                        x2.ap[:, oc2 * 512:(oc2 + 1) * 512], pm.ap, sdc.ap[:, 0:1],
                        x2.ap[:, oc2 * 512:(oc2 + 1) * 512], ALU.mult, ALU.add),
                        reads=[pm, sdc, x2], writes=[x2])
                sc.add("s", lambda e, x2=x2: e.activation(
                    outb.ap, x2.ap, AF.Identity, bias=0.0, scale=1.0,
                    accum_out=stat.ap[:, 0:1]), reads=[x2], writes=[outb, stat], cost=1100)
                sc.add("s", lambda e, x2=x2: e.activation(
                    outb.ap, x2.ap, AF.Square, bias=0.0, scale=1.0,
                    accum_out=stat.ap[:, 1:2]), reads=[x2], writes=[outb, stat], cost=1100)
                ln_stats(sc, stat, DM, epsc)
                sc.add("g", lambda e, x2=x2: e.tensor_scalar(
                    outb.ap, x2.ap, stat.ap[:, 2:3], stat.ap[:, 4:5],
                    ALU.add, ALU.mult), reads=[x2, stat], writes=[outb], cost=1100)
                dma(h["out"][ti * TA:(ti + 1) * TA, :], outb.ap,
                    reads=[outb], writes=[outB])

        with nc.Block() as block:
            sc.emit(block, sems)
            def final_section(eng):
                for (dk, val) in sc.final_dma:
                    eng.wait_ge(sems[dk], val)
            block.sync(final_section)
    return nc


_NC_CACHE = [None]


def kernel(**inputs):
    import ml_dtypes
    f32 = np.float32
    x = np.ascontiguousarray(np.asarray(inputs["L0_attn_x"], dtype=f32))

    def t(a):
        return np.ascontiguousarray(np.asarray(a, dtype=f32).T)

    def col8(v):
        return np.ascontiguousarray(np.asarray(v, dtype=f32).reshape(NDC, P).T)

    A8 = np.ascontiguousarray(
        np.asarray(inputs["L0_attn_A_log"], dtype=f32).reshape(NDC, P, NST)
        .transpose(1, 0, 2).reshape(P, NDC * NST))
    base = dict(
        wpT=t(inputs["L0_attn_proj_W"]), wfT=t(inputs["L0_attn_fconv_W"]),
        wbT=t(inputs["L0_attn_bconv_W"]), wdbcT=t(inputs["L0_attn_deltaBC_W"]),
        wdtT=t(inputs["L0_attn_dt_W"]), wgT=t(inputs["L0_ffn_w_gate"]),
        wuT=t(inputs["L0_ffn_w_up"]), wdT=t(inputs["L0_ffn_w_down"]),
        A_log8=A8,
        gamma8=col8(inputs["L0_attn_ln_w"]), beta8=col8(inputs["L0_attn_ln_b"]),
        bp8=col8(inputs["L0_attn_proj_b"]), bf8=col8(inputs["L0_attn_fconv_b"]),
        bb8=col8(inputs["L0_attn_bconv_b"]), dtb8=col8(inputs["L0_attn_dt_b"]),
        D8=col8(inputs["L0_attn_D"]),
        ident=np.eye(P, dtype=f32),
        identb=np.eye(P, dtype=ml_dtypes.bfloat16),
        ones=np.ones((P, P), dtype=f32),
    )
    in_maps = []
    for core in range(8):
        m = dict(base)
        m["x"] = np.ascontiguousarray(x[core * SPC:(core + 1) * SPC])
        in_maps.append(m)

    if _NC_CACHE[0] is None:
        _NC_CACHE[0] = build_nc()
    res = run_bass_kernel_spmd(_NC_CACHE[0], in_maps, core_ids=list(range(8)))
    out = np.concatenate([r["out"].reshape(SPC, SEQ, DM) for r in res.results], axis=0)
    return np.ascontiguousarray(out.astype(np.float32))


# version 08
# speedup vs baseline: 1.5021x; 1.0500x over previous
import numpy as np
import contextlib
import concourse.bass as bass
import concourse.mybir as mybir
from concourse.bass_utils import run_bass_kernel_spmd

F32 = mybir.dt.float32
BF16 = mybir.dt.bfloat16
AF = mybir.ActivationFunctionType
ALU = mybir.AluOpType
AX = mybir.AxisListType

P = 128
SPC = 2
SEQ = 2048
DM = 1024
NDC = DM // P
INNER = 4096
NIC = INNER // P
TA = 128
NCH = SEQ // TA
TS = 64
NSUB = TA // TS
NST = 16
DTR = 64
EPS = 1e-5
C32 = 1.5 * 2.0**23
C64 = 1.5 * 2.0**52
KMUL = float(np.float32((C64 - C32) * (2.0**40)))
NTT = SPC * NCH


class Buf:
    __slots__ = ("ap", "writer", "readers")

    def __init__(self, ap):
        self.ap = ap
        self.writer = None
        self.readers = []


NDSEM = 32


class Sched:
    ENG = ("t", "s", "v", "g")
    STREAMS = ("d", "t", "s", "v", "g")

    def __init__(self):
        self.nodes = []
        self.frontier = None

    def _deps(self, reads, writes):
        deps = []
        for b in reads:
            if b.writer is not None:
                deps.append(b.writer)
        for b in writes:
            deps.extend(b.readers)
            if b.writer is not None:
                deps.append(b.writer)
        if self.frontier is not None:
            deps.append(self.frontier)
        return deps

    def add(self, ek, buildfn, reads=(), writes=(), inc=True, cost=300.0):
        deps = self._deps(reads, writes)
        nid = len(self.nodes)
        self.nodes.append((ek, buildfn, float(cost), deps, False, 0.0))
        for b in reads:
            b.readers.append(nid)
        for b in writes:
            b.writer = nid
            b.readers = []
        return nid

    def add_dma(self, buildfn, reads=(), writes=(), xfer=2000.0):
        deps = self._deps(reads, writes)
        nid = len(self.nodes)
        self.nodes.append(("d", buildfn, 150.0, deps, True, float(xfer)))
        for b in reads:
            b.readers.append(nid)
        for b in writes:
            b.writer = nid
            b.readers = []
        return nid

    def barrier(self):
        deps = list(range(self.frontier + 1 if self.frontier is not None else 0,
                          len(self.nodes)))
        nid = len(self.nodes)
        self.nodes.append((None, None, 0.0, deps, False, 0.0))
        self.frontier = nid
        return nid

    def _schedule(self):
        import heapq
        n = len(self.nodes)
        succ = [[] for _ in range(n)]
        indeg = [0] * n
        for i, (_, _, _, deps, _, _) in enumerate(self.nodes):
            for d in set(deps):
                succ[d].append(i)
                indeg[i] += 1
        prio = [0.0] * n
        for i in range(n - 1, -1, -1):
            ek, _, cost, _, is_dma, xfer = self.nodes[i]
            best = 0.0
            for s_ in succ[i]:
                if prio[s_] > best:
                    best = prio[s_]
            prio[i] = best + cost + (xfer if is_dma else 0.0)
        ready = {e: [] for e in self.STREAMS}
        free = {e: 0.0 for e in self.STREAMS}
        fin = [0.0] * n
        depcnt = list(indeg)
        rdytime = [0.0] * n
        barrier_ready = []
        order = {e: [] for e in self.STREAMS}
        def push(i):
            ek = self.nodes[i][0]
            if ek is None:
                barrier_ready.append(i)
            else:
                heapq.heappush(ready[ek], (-prio[i], i))
        for i in range(n):
            if depcnt[i] == 0:
                push(i)
        ndone = 0
        while ndone < n:
            while barrier_ready:
                i = barrier_ready.pop()
                fin[i] = rdytime[i]
                ndone += 1
                for s_ in succ[i]:
                    depcnt[s_] -= 1
                    rdytime[s_] = max(rdytime[s_], fin[i])
                    if depcnt[s_] == 0:
                        push(s_)
                continue
            bestpick = None
            for e in self.STREAMS:
                while ready[e]:
                    i = ready[e][0][1]
                    start = max(free[e], rdytime[i])
                    if bestpick is None or start < bestpick[0] or \
                       (start == bestpick[0] and -ready[e][0][0] > bestpick[3]):
                        bestpick = (start, e, i, -ready[e][0][0])
                    break
            if bestpick is None:
                if barrier_ready:
                    continue
                raise RuntimeError("deadlock in scheduler")
            start, e, i, _ = bestpick
            import heapq as hq
            hq.heappop(ready[e])
            ek, _, cost, _, is_dma, xfer = self.nodes[i]
            end_issue = start + cost
            free[e] = end_issue
            fin[i] = end_issue + (xfer if is_dma else 0.0)
            order[e].append(i)
            ndone += 1
            for s_ in succ[i]:
                depcnt[s_] -= 1
                rdytime[s_] = max(rdytime[s_], fin[i])
                if depcnt[s_] == 0:
                    push(s_)
        self.est_span = max(fin) if n else 0.0
        return order

    def emit(self, block, sems):
        order = self._schedule()
        semid = [None] * len(self.nodes)
        cnt = {e: 0 for e in self.ENG}
        dcnt = [0] * NDSEM
        dmagate = [None] * len(self.nodes)
        dnext = 0
        for e in self.ENG:
            for i in order[e]:
                cnt[e] += 1
                semid[i] = (e, cnt[e])
        for i in order["d"]:
            j = dnext
            dnext = (j + 1) % NDSEM
            if dcnt[j] > 0:
                dmagate[i] = (f"d{j}", 16 * dcnt[j])
            dcnt[j] += 1
            semid[i] = (f"d{j}", 16 * dcnt[j])
        self.final_dma = [(f"d{j}", 16 * dcnt[j]) for j in range(NDSEM) if dcnt[j]]
        vc = {e: {} for e in self.STREAMS}
        snap = {}
        emitted = {e: [] for e in self.STREAMS}
        pos = {e: 0 for e in self.STREAMS}
        nexti = {e: 0 for e in self.STREAMS}
        done = [False] * len(self.nodes)
        def deps_done(i):
            return all(done[d] or self.nodes[d][0] is None and _bar_done(d)
                       for d in self.nodes[i][3])
        def _bar_done(b):
            return all(done[d] for d in self.nodes[b][3])
        progress = True
        total = sum(len(order[e]) for e in self.STREAMS)
        processed = 0
        while processed < total:
            progress = False
            for e in self.STREAMS:
                while nexti[e] < len(order[e]):
                    i = order[e][nexti[e]]
                    if not deps_done(i):
                        break
                    waits = []
                    mydeps = []
                    for d in set(self.nodes[i][3]):
                        if self.nodes[d][0] is None:
                            mydeps.extend(self.nodes[d][3])
                        else:
                            mydeps.append(d)
                    for d in set(mydeps):
                        dk, val = semid[d]
                        if vc[e].get(dk, 0) >= val:
                            continue
                        waits.append((dk, val))
                        ps = snap.get((dk, val))
                        if ps:
                            for k2, v2 in ps.items():
                                if v2 > vc[e].get(k2, 0):
                                    vc[e][k2] = v2
                        if val > vc[e].get(dk, 0):
                            vc[e][dk] = val
                    if dmagate[i] is not None:
                        gk, gv = dmagate[i]
                        if vc[e].get(gk, 0) < gv:
                            waits.append((gk, gv))
                            vc[e][gk] = gv
                    key, val = semid[i]
                    s_ = dict(vc[e])
                    s_[key] = val
                    snap[(key, val)] = s_
                    emitted[e].append((waits, self.nodes[i][1], key))
                    done[i] = True
                    nexti[e] += 1
                    processed += 1
                    progress = True
            if not progress:
                raise RuntimeError("wait-computation stuck")
        for ek, attr in (("d", "sync"), ("t", "tensor"), ("s", "scalar"),
                         ("v", "vector"), ("g", "gpsimd")):
            oplist = emitted[ek]
            if not oplist:
                continue

            def section(eng, _ops=oplist):
                for (waits, buildfn, key) in _ops:
                    for (dk, val) in waits:
                        eng.wait_ge(sems[dk], val)
                    if buildfn is None:
                        continue
                    inst = buildfn(eng)
                    inst.then_inc(sems[key[0] if len(key) == 1 else key],
                                  16 if key.startswith("d") else 1)

            getattr(block, attr)(section)


def ln_stats(sc, stat, n, epsc=None):
    inv = 1.0 / n
    sc.add("v", lambda e: e.tensor_scalar(stat.ap[:, 2:3], stat.ap[:, 0:1],
                                          -inv, None, ALU.mult),
           reads=[stat], writes=[stat])
    sc.add("g", lambda e: e.tensor_tensor(stat.ap[:, 3:4], stat.ap[:, 2:3],
                                          stat.ap[:, 2:3], ALU.mult),
           reads=[stat], writes=[stat])
    sc.add("g", lambda e: e.tensor_scalar(stat.ap[:, 1:2], stat.ap[:, 1:2],
                                          inv, 0.0, ALU.mult, ALU.add),
           reads=[stat], writes=[stat])
    sc.add("g", lambda e: e.tensor_tensor(stat.ap[:, 3:4], stat.ap[:, 1:2],
                                          stat.ap[:, 3:4], ALU.subtract),
           reads=[stat], writes=[stat])
    sc.add("s", lambda e: e.activation(stat.ap[:, 3:4], stat.ap[:, 3:4], AF.Sqrt,
                                       bias=epsc.ap[:, 0:1], scale=1.0),
           reads=[stat, epsc], writes=[stat])
    sc.add("v", lambda e: e.reciprocal(stat.ap[:, 4:5], stat.ap[:, 3:4]),
           reads=[stat], writes=[stat])


def softplus(sc, xb, spb):
    w = xb.ap.shape[1]
    sp = spb.ap[:, 0:w]
    sc.add("s", lambda e: e.activation(sp, xb.ap, AF.Abs, bias=0.0, scale=1.0),
           reads=[xb], writes=[spb], cost=1050)
    sc.add("s", lambda e: e.activation(sp, sp, AF.Exp, bias=0.0, scale=-1.0),
           reads=[spb], writes=[spb], cost=1050)
    sc.add("s", lambda e: e.activation(sp, sp, AF.Ln, bias=1.0, scale=1.0),
           reads=[spb], writes=[spb], cost=1050)
    sc.add("s", lambda e: e.activation(xb.ap, xb.ap, AF.Relu, bias=0.0, scale=1.0),
           reads=[xb], writes=[xb], cost=1050)
    sc.add("g", lambda e: e.tensor_tensor(xb.ap, xb.ap, sp, ALU.add),
           reads=[xb, spb], writes=[xb], cost=2100)


def build_nc():
    nc = bass.Bass("TRN2", target_bir_lowering=False)
    h = {}
    h["x"] = nc.dram_tensor("x", [SPC, SEQ, DM], F32, kind="ExternalInput").ap()
    for nm, shape in [("wpT", [DM, DM]), ("wfT", [DM, DM]), ("wbT", [DM, DM]),
                      ("wdbcT", [DM, 96]), ("wdtT", [DTR, DM]),
                      ("wgT", [DM, INNER]), ("wuT", [DM, INNER]), ("wdT", [INNER, DM])]:
        h[nm] = nc.dram_tensor(nm, shape, F32, kind="ExternalInput").ap()
    for nm in ("A_log8", "gamma8", "beta8", "bp8", "bf8", "bb8", "dtb8", "D8"):
        cols = NDC * NST if nm == "A_log8" else NDC
        h[nm] = nc.dram_tensor(nm, [P, cols], F32, kind="ExternalInput").ap()
    h["ident"] = nc.dram_tensor("ident", [P, P], F32, kind="ExternalInput").ap()
    h["identb"] = nc.dram_tensor("identb", [P, P], BF16, kind="ExternalInput").ap()
    h["ones"] = nc.dram_tensor("ones", [P, P], F32, kind="ExternalInput").ap()
    h["out"] = nc.dram_tensor("out", [SPC * SEQ, DM], F32, kind="ExternalOutput").ap()
    h["x2scr"] = nc.dram_tensor("x2scr", [SPC * SEQ, DM], F32, kind="Internal").ap()
    h["xq1scr"] = nc.dram_tensor("xq1scr", [DM, SPC * SEQ], BF16, kind="Internal").ap()
    h["xq2scr"] = nc.dram_tensor("xq2scr", [INNER, SPC * SEQ], BF16, kind="Internal").ap()
    bscr = [Buf(nc.dram_tensor(f"bscr{i}", [NST, TS], F32, kind="Internal").ap())
            for i in range(2)]
    cscr = [Buf(nc.dram_tensor(f"cscr{i}", [NST, TS], F32, kind="Internal").ap())
            for i in range(2)]
    x2scrB = Buf(h["x2scr"])
    xq1scrB = Buf(h["xq1scr"])
    xq2scrB = Buf(h["xq2scr"])
    outB = Buf(h["out"])

    sc = Sched()

    with contextlib.ExitStack() as main_stk:
        sems = {ek: main_stk.enter_context(nc.semaphore(f"S{ek}")) for ek in Sched.ENG}
        for j in range(NDSEM):
            sems[f"d{j}"] = main_stk.enter_context(nc.semaphore(f"Sd{j}"))

        def sb(stk, name, shape, dt=F32):
            return Buf(stk.enter_context(nc.sbuf_tensor(name, shape, dt)).ap())

        def psb(stk, name, shape, dt=F32):
            return Buf(stk.enter_context(nc.psum_tensor(name, shape, dt)).ap())

        def dma(dst, src, reads=(), writes=(), xfer=2000.0):
            return sc.add_dma(lambda e, d=dst, s=src: e.dma_start(out=d, in_=s),
                              reads=reads, writes=writes, xfer=xfer)

        ident = sb(main_stk, "identt", [P, P])
        identb = sb(main_stk, "identbt", [P, P], BF16)
        ones = sb(main_stk, "onest", [P, P])
        A_sb = sb(main_stk, "A_sb", [P, NDC * NST])
        D_sb = sb(main_stk, "D_sb", [P, NDC])
        dtb = sb(main_stk, "dtb", [P, NDC])
        bp = sb(main_stk, "bp", [P, NDC])
        bfc = sb(main_stk, "bfc", [P, NDC])
        bbc = sb(main_stk, "bbc", [P, NDC])
        gam = sb(main_stk, "gam", [P, NDC])
        bet = sb(main_stk, "bet", [P, NDC])
        bfold = sb(main_stk, "bfold", [P, NDC])
        ceff = sb(main_stk, "ceff", [P, 2])
        tmpc = sb(main_stk, "tmpc", [P, 2])
        mx1c = sb(main_stk, "mx1c", [P, NTT])
        mx2c = sb(main_stk, "mx2c", [P, NTT])
        stat = sb(main_stk, "stat", [P, 8])
        sc1 = sb(main_stk, "sc1", [P, NTT])
        mgs = sb(main_stk, "mgs", [P, 4])
        colt = sb(main_stk, "colt", [P, 4])
        epsc = sb(main_stk, "epsc", [P, 1])

        def weight_quant(wh, wq, wtmp, wqp, mcol, nkc, width, psw):
            nelem = float(nkc * P * width)
            for kc in range(nkc):
                wt = wtmp[kc % 2]
                dma(wt.ap, wh[kc * P:(kc + 1) * P, :], writes=[wt])
                sc.add("v", lambda e, wt=wt: e.tensor_reduce(
                    colt.ap[:, 0:1], wt.ap, AX.X, ALU.add, apply_absolute_value=True),
                    reads=[wt], writes=[colt], cost=7800)
                if kc == 0:
                    sc.add("g", lambda e: e.tensor_copy(colt.ap[:, 1:2], colt.ap[:, 0:1]),
                           reads=[colt], writes=[colt])
                else:
                    sc.add("g", lambda e: e.tensor_tensor(
                        colt.ap[:, 1:2], colt.ap[:, 1:2], colt.ap[:, 0:1], ALU.add),
                        reads=[colt], writes=[colt])
            sc.add("t", lambda e: e.matmul(psw.ap[:, 0:1], ones.ap, colt.ap[:, 1:2],
                                           start=True, stop=True),
                   reads=[ones, colt], writes=[psw])
            sc.add("s", lambda e: e.copy(colt.ap[:, 2:3], psw.ap[:, 0:1]),
                   reads=[psw], writes=[colt])
            sc.add("g", lambda e: e.tensor_scalar(colt.ap[:, 2:3], colt.ap[:, 2:3],
                                                  1.0 / nelem, 0.0, ALU.mult, ALU.add),
                   reads=[colt], writes=[colt])
            sc.add("g", lambda e: e.tensor_scalar(colt.ap[:, 3:4], colt.ap[:, 2:3],
                                                  3.0e38, EPS, ALU.min, ALU.max),
                   reads=[colt], writes=[colt])
            sc.add("g", lambda e: e.tensor_scalar(
                mgs.ap[:, mcol:mcol + 1], colt.ap[:, 3:4], 1.0 / 127.0, 0.0,
                ALU.mult, ALU.add), reads=[colt], writes=[mgs])
            sc.add("v", lambda e: e.reciprocal(colt.ap[:, 0:1], colt.ap[:, 3:4]),
                   reads=[colt], writes=[colt])
            for kc in range(nkc):
                wt = wtmp[kc % 2]
                dma(wt.ap, wh[kc * P:(kc + 1) * P, :], writes=[wt])
                sc.add("s", lambda e, wt=wt: e.activation(
                    wqp.ap, wt.ap, AF.Identity, bias=0.0, scale=colt.ap[:, 0:1]),
                    reads=[wt, colt], writes=[wqp], cost=3600)
                sc.add("v", lambda e: e.tensor_scalar(
                    wqp.ap, wqp.ap, ceff.ap[:, 0:1], ceff.ap[:, 0:1],
                    ALU.add, ALU.subtract), reads=[wqp, ceff], writes=[wqp], cost=4400)
                sc.add("g", lambda e, kc=kc: e.tensor_scalar(
                    wq.ap[:, kc * width:(kc + 1) * width], wqp.ap, 1.0, -1.0,
                    ALU.min, ALU.max), reads=[wqp], writes=[wq], cost=7800)

        with contextlib.ExitStack() as stk0:
            wpT = sb(stk0, "wpT_s", [P, NDC * DM])
            wfT = sb(stk0, "wfT_s", [P, NDC * DM])
            wbT = sb(stk0, "wbT_s", [P, NDC * DM])
            wdbcT = sb(stk0, "wdbcT_s", [P, NDC * 96])
            wdtT = sb(stk0, "wdtT_s", [DTR, NDC * P])

            for nm, b in [("ident", ident), ("identb", identb), ("ones", ones),
                          ("A_log8", A_sb), ("D8", D_sb), ("dtb8", dtb),
                          ("bp8", bp), ("bf8", bfc), ("bb8", bbc),
                          ("gamma8", gam), ("beta8", bet)]:
                dma(b.ap, h[nm], writes=[b])
            for dc in range(NDC):
                dma(wpT.ap[:, dc * DM:(dc + 1) * DM], h["wpT"][dc * P:(dc + 1) * P, :], writes=[wpT])
                dma(wfT.ap[:, dc * DM:(dc + 1) * DM], h["wfT"][dc * P:(dc + 1) * P, :], writes=[wfT])
                dma(wbT.ap[:, dc * DM:(dc + 1) * DM], h["wbT"][dc * P:(dc + 1) * P, :], writes=[wbT])
                dma(wdbcT.ap[:, dc * 96:(dc + 1) * 96], h["wdbcT"][dc * P:(dc + 1) * P, :], writes=[wdbcT])
                dma(wdtT.ap[:, dc * P:(dc + 1) * P], h["wdtT"][:, dc * P:(dc + 1) * P], writes=[wdtT])

            sc.add("v", lambda e: e.tensor_scalar(tmpc.ap, ones.ap[:, 0:2], 1.0, None,
                                                  ALU.mult), reads=[ones], writes=[tmpc])
            sc.add("g", lambda e: e.tensor_scalar(tmpc.ap, tmpc.ap, 2.0**-40, 1.0,
                                                  ALU.add, ALU.mult),
                   reads=[tmpc], writes=[tmpc])
            sc.add("v", lambda e: e.tensor_scalar(tmpc.ap, tmpc.ap, -1.0, None, ALU.add),
                   reads=[tmpc], writes=[tmpc])
            sc.add("g", lambda e: e.tensor_scalar(ceff.ap, tmpc.ap, KMUL, C32,
                                                  ALU.mult, ALU.add),
                   reads=[tmpc], writes=[ceff])

            sc.add("g", lambda e: e.tensor_scalar(epsc.ap, ones.ap[:, 0:1], EPS, 0.0,
                                                  ALU.mult, ALU.add),
                   reads=[ones], writes=[epsc])
            sc.add("s", lambda e: e.activation(A_sb.ap, A_sb.ap, AF.Exp,
                                               bias=0.0, scale=1.0),
                   reads=[A_sb], writes=[A_sb])
            sc.add("g", lambda e: e.tensor_scalar(A_sb.ap, A_sb.ap, -1.0, 0.0,
                                                  ALU.mult, ALU.add),
                   reads=[A_sb], writes=[A_sb])

            with contextlib.ExitStack() as stkA:
                x_sbP = [sb(stkA, f"x_sb{i}", [P, DM]) for i in range(2)]
                xn_sb = sb(stkA, "xn_sb", [P, DM])
                xpTP = [sb(stkA, f"xpT{i}", [P, NDC * TA]) for i in range(2)]
                uT = sb(stkA, "uT", [P, NDC * TA])
                deltaTP = [sb(stkA, f"deltaT{i}", [P, NDC * TA]) for i in range(2)]
                duTP = [sb(stkA, f"duT{i}", [P, NDC * TA]) for i in range(2)]
                ysumP = [sb(stkA, f"ysum{i}", [P, NDC * TA]) for i in range(2)]
                spP = [sb(stkA, f"sp{i}", [P, DM]) for i in range(2)]
                dbcP = [Buf(stkA.enter_context(nc.sbuf_tensor(f"dbc{i}", [96, TA], F32)).ap())
                        for i in range(2)]
                brepP = [sb(stkA, f"brep{i}", [P, NST * TS]) for i in range(2)]
                crepP = [sb(stkA, f"crep{i}", [P, NST * TS]) for i in range(2)]
                ebuf = [sb(stkA, f"ebuf{i}", [P, NST * (TS + 1)]) for i in range(2)]
                bbuf = [sb(stkA, f"bbuf{i}", [P, NST * (TS + 1)]) for i in range(2)]
                hbuf = sb(stkA, "hbuf", [P, NST * (TS + 1)])
                hcar = sb(stkA, "hcar", [P, 2 * NDC * NST])
                yscP = [sb(stkA, f"ysc{i}", [P, TS]) for i in range(2)]
                statP = [sb(stkA, f"stat{i}", [P, 8]) for i in range(2)]
                xq1b = sb(stkA, "xq1b", [P, DM], BF16)
                xq1T = sb(stkA, "xq1T", [P, NDC * TA], BF16)
                psA = psb(stkA, "psA", [P, 512])
                psA2 = psb(stkA, "psA2", [P, 512])
                psDT = psb(stkA, "psDT", [P, 512])
                psT0 = psb(stkA, "psT0", [P, 512])
                psT1 = psb(stkA, "psT1", [P, 512])
                psDB = psb(stkA, "psDB", [96, TA])
                psTB0 = psb(stkA, "psTB0", [P, 512], BF16)
                psTB1 = psb(stkA, "psTB1", [P, 512], BF16)
                psmm = [psA, psA2]

                def e3(b):
                    return b.ap.rearrange("p (n t) -> p n t", n=NST)

                for eb in ebuf:
                    sc.add("g", lambda e, eb=eb: e.memset(e3(eb)[:, :, 0:1], 0.0),
                           writes=[eb])

                for oc in range(NDC):
                    for dc in range(NDC):
                        sc.add("t", lambda e, oc=oc, dc=dc: e.matmul(
                            psA.ap[:, 0:1],
                            wpT.ap[:, dc * DM + oc * P: dc * DM + (oc + 1) * P],
                            bet.ap[:, dc:dc + 1],
                            start=(dc == 0), stop=(dc == NDC - 1)),
                            reads=[wpT, bet], writes=[psA], inc=(dc == NDC - 1))
                    sc.add("s", lambda e, oc=oc: e.activation(
                        bfold.ap[:, oc:oc + 1], psA.ap[:, 0:1], AF.Identity,
                        bias=bp.ap[:, oc:oc + 1], scale=1.0),
                        reads=[psA, bp], writes=[bfold])
                for dc in range(NDC):
                    sc.add("v", lambda e, dc=dc: e.tensor_scalar(
                        wpT.ap[:, dc * DM:(dc + 1) * DM], wpT.ap[:, dc * DM:(dc + 1) * DM],
                        gam.ap[:, dc:dc + 1], 1.0, ALU.mult, ALU.mult),
                        reads=[wpT, gam], writes=[wpT])

                def do_head(s, c, ti, x_sb, xpT, deltaT, duT, ysum, sp, dbc, stat):
                    t0 = c * TA
                    dma(x_sb.ap, h["x"][s, t0:t0 + TA, :], writes=[x_sb])
                    sc.add("s", lambda e: e.activation(
                        x_sb.ap, x_sb.ap, AF.Identity, bias=0.0, scale=1.0,
                        accum_out=stat.ap[:, 0:1]), reads=[x_sb], writes=[x_sb, stat], cost=1100)
                    sc.add("s", lambda e: e.activation(
                        sp.ap, x_sb.ap, AF.Square, bias=0.0, scale=1.0,
                        accum_out=stat.ap[:, 1:2]), reads=[x_sb], writes=[sp, stat], cost=1100)
                    ln_stats(sc, stat, DM, epsc)
                    sc.add("g", lambda e: e.tensor_scalar(
                        xn_sb.ap, x_sb.ap, stat.ap[:, 2:3], stat.ap[:, 4:5],
                        ALU.add, ALU.mult), reads=[x_sb, stat], writes=[xn_sb], cost=1100)
                    for half in range(2):
                        pst = [psT0, psT1][half]
                        for j in range(4):
                            dc = half * 4 + j
                            sc.add("t", lambda e, dc=dc, j=j, pst=pst: e.transpose(
                                pst.ap[:, j * P:(j + 1) * P],
                                xn_sb.ap[:, dc * P:(dc + 1) * P], ident.ap),
                                reads=[xn_sb, ident], writes=[pst], inc=(j == 3))
                        sc.add("s", lambda e, half=half, pst=pst: e.copy(
                            sp.ap[:, half * 512:(half + 1) * 512], pst.ap),
                            reads=[pst], writes=[sp])
                    for oc in range(NDC):
                        pm = psmm[oc % 2]
                        for dc in range(NDC):
                            sc.add("t", lambda e, oc=oc, dc=dc, pm=pm: e.matmul(
                                pm.ap[:, 0:TA],
                                wpT.ap[:, dc * DM + oc * P: dc * DM + (oc + 1) * P],
                                sp.ap[:, dc * P:(dc + 1) * P],
                                start=(dc == 0), stop=(dc == NDC - 1)),
                                reads=[wpT, sp], writes=[pm], inc=(dc == NDC - 1))
                        sc.add("s", lambda e, oc=oc, pm=pm: e.activation(
                            xpT.ap[:, oc * TA:(oc + 1) * TA], pm.ap[:, 0:TA],
                            AF.Identity, bias=bfold.ap[:, oc:oc + 1], scale=1.0),
                            reads=[pm, bfold], writes=[xpT])

                    for br, (wb, bb_) in enumerate([(wfT, bfc), (wbT, bbc)]):
                        for oc in range(NDC):
                            pm = psmm[oc % 2]
                            for dc in range(NDC):
                                sc.add("t", lambda e, oc=oc, dc=dc, pm=pm, wb=wb: e.matmul(
                                    pm.ap[:, 0:TA],
                                    wb.ap[:, dc * DM + oc * P: dc * DM + (oc + 1) * P],
                                    xpT.ap[:, dc * TA:(dc + 1) * TA],
                                    start=(dc == 0), stop=(dc == NDC - 1)),
                                    reads=[wb, xpT], writes=[pm], inc=(dc == NDC - 1))
                            sc.add("s", lambda e, oc=oc, pm=pm, bb_=bb_: e.activation(
                                uT.ap[:, oc * TA:(oc + 1) * TA], pm.ap[:, 0:TA],
                                AF.Identity, bias=bb_.ap[:, oc:oc + 1], scale=1.0),
                                reads=[pm, bb_], writes=[uT])
                        softplus(sc, uT, sp)
                        for dc in range(NDC):
                            sc.add("t", lambda e, dc=dc: e.matmul(
                                psDB.ap,
                                wdbcT.ap[:, dc * 96:(dc + 1) * 96],
                                uT.ap[:, dc * TA:(dc + 1) * TA],
                                start=(dc == 0), stop=(dc == NDC - 1)),
                                reads=[wdbcT, uT], writes=[psDB], inc=(dc == NDC - 1))
                        sc.add("s", lambda e: e.copy(dbc.ap, psDB.ap),
                               reads=[psDB], writes=[dbc])
                        for g4 in range(2):
                            for j in range(4):
                                dc = g4 * 4 + j
                                sc.add("t", lambda e, dc=dc, j=j: e.matmul(
                                    psDT.ap[:, j * TA:(j + 1) * TA],
                                    wdtT.ap[:, dc * P:(dc + 1) * P],
                                    dbc.ap[0:DTR, :],
                                    start=True, stop=True),
                                    reads=[wdtT, dbc], writes=[psDT], inc=(j == 3))
                            for j in range(4):
                                dc = g4 * 4 + j
                                sc.add("s", lambda e, dc=dc, j=j: e.activation(
                                    deltaT.ap[:, dc * TA:(dc + 1) * TA],
                                    psDT.ap[:, j * TA:(j + 1) * TA],
                                    AF.Identity, bias=dtb.ap[:, dc:dc + 1], scale=1.0),
                                    reads=[psDT, dtb], writes=[deltaT])
                        softplus(sc, deltaT, sp)
                        sc.add("v", lambda e: e.tensor_tensor(
                            duT.ap, deltaT.ap, uT.ap, ALU.mult),
                            reads=[deltaT, uT], writes=[duT], cost=1200)
                        if br == 0:
                            for dc in range(NDC):
                                sc.add("g", lambda e, dc=dc: e.tensor_scalar(
                                    ysum.ap[:, dc * TA:(dc + 1) * TA],
                                    uT.ap[:, dc * TA:(dc + 1) * TA],
                                    D_sb.ap[:, dc:dc + 1], 0.0, ALU.mult, ALU.add),
                                    reads=[uT, D_sb], writes=[ysum])
                        else:
                            for dc in range(NDC):
                                sc.add("g", lambda e, dc=dc: e.tensor_scalar(
                                    sp.ap[:, dc * P:(dc + 1) * P],
                                    uT.ap[:, dc * TA:(dc + 1) * TA],
                                    D_sb.ap[:, dc:dc + 1], 0.0, ALU.mult, ALU.add),
                                    reads=[uT, D_sb], writes=[sp])
                            sc.add("g", lambda e: e.tensor_tensor(
                                ysum.ap, ysum.ap, sp.ap, ALU.add),
                                reads=[ysum, sp], writes=[ysum])
                        for sub in range(NSUB):
                            ts0 = sub * TS
                            pb = (c * NSUB + sub) % 2
                            dma(bscr[pb].ap, dbc.ap[DTR:DTR + NST, ts0:ts0 + TS],
                                reads=[dbc], writes=[bscr[pb]])
                            dma(cscr[pb].ap, dbc.ap[DTR + NST:96, ts0:ts0 + TS],
                                reads=[dbc], writes=[cscr[pb]])
                            bflat = bscr[pb].ap.rearrange("n t -> (n t)") \
                                .unsqueeze(0).broadcast_to([P, NST * TS])
                            cflat = cscr[pb].ap.rearrange("n t -> (n t)") \
                                .unsqueeze(0).broadcast_to([P, NST * TS])
                            dma(brepP[pb].ap, bflat, reads=[bscr[pb]], writes=[brepP[pb]])
                            dma(crepP[pb].ap, cflat, reads=[cscr[pb]], writes=[crepP[pb]])
                            brep, crep = brepP[pb], crepP[pb]

                            def emit_e(dc):
                                eb = ebuf[(dc + sub) % 2]
                                col0 = dc * TA + ts0
                                ek = "g"
                                sc.add(ek, lambda e, eb=eb, dc=dc, col0=col0: e.tensor_tensor(
                                    e3(eb)[:, :, 1:],
                                    deltaT.ap[:, col0:col0 + TS]
                                    .unsqueeze(1).broadcast_to([P, NST, TS]),
                                    A_sb.ap[:, dc * NST:(dc + 1) * NST]
                                    .unsqueeze(2).broadcast_to([P, NST, TS]),
                                    ALU.mult), reads=[deltaT, A_sb], writes=[eb],
                                    cost=1950)

                            def emit_exp(dc):
                                eb = ebuf[(dc + sub) % 2]
                                sc.add("s", lambda e, eb=eb: e.activation(
                                    e3(eb)[:, :, 1:], e3(eb)[:, :, 1:], AF.Exp,
                                    bias=0.0, scale=1.0), reads=[eb], writes=[eb], cost=1100)

                            def emit_b(dc):
                                bbf = bbuf[(dc + sub) % 2]
                                col0 = dc * TA + ts0
                                hsl = slice((br * NDC + dc) * NST,
                                            (br * NDC + dc + 1) * NST)
                                sc.add("s", lambda e, bbf=bbf, hsl=hsl: e.copy(
                                    e3(bbf)[:, :, 0:1], hcar.ap[:, hsl].unsqueeze(2)),
                                    reads=[hcar], writes=[bbf], cost=250)
                                sc.add("g", lambda e, bbf=bbf, col0=col0, brep=brep: e.tensor_tensor(
                                    e3(bbf)[:, :, 1:],
                                    duT.ap[:, col0:col0 + TS]
                                    .unsqueeze(1).broadcast_to([P, NST, TS]),
                                    brep.ap.rearrange("p (n t) -> p n t", n=NST),
                                    ALU.mult), reads=[duT, brep], writes=[bbf], cost=1950)

                            def emit_red(dc):
                                eb = ebuf[(dc + sub) % 2]
                                ysc = yscP[dc % 2]
                                col0 = dc * TA + ts0
                                sc.add("v", lambda e, eb=eb, ysc=ysc: e.tensor_reduce(
                                    ysc.ap, e3(eb)[:, :, 1:].transpose([0, 2, 1]),
                                    AX.X, ALU.add), reads=[eb], writes=[ysc], cost=1950)
                                sc.add("g", lambda e, col0=col0, ysc=ysc: e.tensor_tensor(
                                    ysum.ap[:, col0:col0 + TS],
                                    ysum.ap[:, col0:col0 + TS],
                                    ysc.ap, ALU.add), reads=[ysc, ysum], writes=[ysum], cost=300)

                            emit_e(0)
                            emit_exp(0)
                            emit_b(0)
                            for dc in range(NDC):
                                eb = ebuf[(dc + sub) % 2]
                                bbf = bbuf[(dc + sub) % 2]
                                hb = hbuf
                                hsl = slice((br * NDC + dc) * NST,
                                            (br * NDC + dc + 1) * NST)
                                sc.add("v", lambda e, hb=hb, eb=eb, bbf=bbf: e.tensor_tensor_scan(
                                    hb.ap, eb.ap, bbf.ap, 0.0, ALU.mult, ALU.add),
                                    reads=[eb, bbf], writes=[hb], cost=2250)
                                if dc > 0:
                                    emit_red(dc - 1)
                                if dc + 1 < NDC:
                                    emit_e(dc + 1)
                                    emit_b(dc + 1)
                                sc.add("s", lambda e, hb=hb, hsl=hsl: e.copy(
                                    hcar.ap[:, hsl], e3(hb)[:, :, TS:TS + 1].squeeze(2)),
                                    reads=[hb], writes=[hcar], cost=250)
                                sc.add("v" if dc % 4 != 3 else "g",
                                    lambda e, eb=eb, hb=hb, crep=crep: e.tensor_tensor(
                                    e3(eb)[:, :, 1:], e3(hb)[:, :, 1:],
                                    crep.ap.rearrange("p (n t) -> p n t", n=NST),
                                    ALU.mult), reads=[hb, crep], writes=[eb],
                                    cost=1550 if dc % 4 != 3 else 1900)
                                if dc + 1 < NDC:
                                    emit_exp(dc + 1)
                            emit_red(NDC - 1)

                def do_tail(s, c, ti, x_sb, xpT, deltaT, duT, ysum, sp, dbc, stat):
                    sc.add("s", lambda e: e.activation(xpT.ap, xpT.ap, AF.Silu,
                                                       bias=0.0, scale=1.0),
                           reads=[xpT], writes=[xpT], cost=1100)
                    sc.add("v", lambda e: e.tensor_tensor(
                        ysum.ap, ysum.ap, xpT.ap, ALU.mult),
                        reads=[ysum, xpT], writes=[ysum], cost=1200)
                    for half in range(2):
                        pst = [psT0, psT1][half]
                        for j in range(4):
                            dc = half * 4 + j
                            sc.add("t", lambda e, dc=dc, j=j, pst=pst: e.transpose(
                                pst.ap[:, j * P:(j + 1) * P],
                                ysum.ap[:, dc * TA:(dc + 1) * TA], ident.ap),
                                reads=[ysum, ident], writes=[pst], inc=(j == 3))
                        sc.add("v", lambda e, half=half, pst=pst: e.scalar_tensor_tensor(
                            x_sb.ap[:, half * 512:(half + 1) * 512],
                            x_sb.ap[:, half * 512:(half + 1) * 512], 2.0, pst.ap,
                            ALU.mult, ALU.add), reads=[x_sb, pst], writes=[x_sb])
                    sc.add("s", lambda e: e.activation(
                        x_sb.ap, x_sb.ap, AF.Identity, bias=0.0, scale=1.0,
                        accum_out=stat.ap[:, 0:1]), reads=[x_sb], writes=[x_sb, stat], cost=1100)
                    sc.add("s", lambda e: e.activation(
                        sp.ap, x_sb.ap, AF.Square, bias=0.0, scale=1.0,
                        accum_out=stat.ap[:, 1:2]), reads=[x_sb], writes=[sp, stat], cost=1100)
                    ln_stats(sc, stat, DM, epsc)
                    sc.add("g", lambda e: e.tensor_scalar(
                        x_sb.ap, x_sb.ap, stat.ap[:, 2:3], stat.ap[:, 4:5],
                        ALU.add, ALU.mult), reads=[x_sb, stat], writes=[x_sb], cost=1100)
                    dma(h["x2scr"][ti * TA:(ti + 1) * TA, :], x_sb.ap,
                        reads=[x_sb], writes=[x2scrB])
                    sc.add("v", lambda e: e.tensor_reduce(
                        stat.ap[:, 5:6], x_sb.ap, AX.X, ALU.max,
                        apply_absolute_value=True), reads=[x_sb], writes=[stat], cost=2000)
                    sc.add("g", lambda e: e.tensor_scalar(
                        mx1c.ap[:, ti:ti + 1], stat.ap[:, 5:6], 3.0e38, EPS,
                        ALU.min, ALU.max), reads=[stat], writes=[mx1c])
                    sc.add("v", lambda e: e.reciprocal(
                        stat.ap[:, 6:7], mx1c.ap[:, ti:ti + 1]),
                        reads=[mx1c], writes=[stat])
                    sc.add("g", lambda e: e.tensor_scalar(
                        sc1.ap[:, ti:ti + 1], stat.ap[:, 6:7], 127.0, 0.0,
                        ALU.mult, ALU.add), reads=[stat], writes=[sc1])
                    sc.add("s", lambda e: e.activation(
                        sp.ap, x_sb.ap, AF.Identity, bias=0.0,
                        scale=sc1.ap[:, ti:ti + 1]), reads=[x_sb, sc1], writes=[sp], cost=1100)
                    sc.add("v", lambda e: e.tensor_scalar(
                        sp.ap, sp.ap, ceff.ap[:, 0:1], ceff.ap[:, 0:1],
                        ALU.add, ALU.subtract), reads=[sp, ceff], writes=[sp], cost=1200)
                    sc.add("g", lambda e: e.tensor_scalar(
                        xq1b.ap, sp.ap, 127.0, -128.0, ALU.min, ALU.max),
                        reads=[sp], writes=[xq1b], cost=2000)
                    for half in range(2):
                        pst = [psTB0, psTB1][half]
                        for j in range(4):
                            dc = half * 4 + j
                            sc.add("t", lambda e, dc=dc, j=j, pst=pst: e.transpose(
                                pst.ap[:, j * P:(j + 1) * P],
                                xq1b.ap[:, dc * P:(dc + 1) * P], identb.ap),
                                reads=[xq1b, identb], writes=[pst], inc=(j == 3))
                        sc.add("s", lambda e, half=half, pst=pst: e.copy(
                            xq1T.ap[:, half * 512:(half + 1) * 512], pst.ap),
                            reads=[pst], writes=[xq1T])
                    dma(h["xq1scr"].rearrange("(dc p) t -> p dc t", p=P)
                        [:, :, ti * TA:(ti + 1) * TA],
                        xq1T.ap.rearrange("p (dc t) -> p dc t", dc=NDC),
                        reads=[xq1T], writes=[xq1scrB])

                def args_of(ti):
                    pi = ti % 2
                    return (ti // NCH, ti % NCH, ti, x_sbP[pi], xpTP[pi], deltaTP[pi],
                            duTP[pi], ysumP[pi], spP[pi], dbcP[pi], statP[pi])

                for ti in range(NTT):
                    if ti % NCH == 0:
                        sc.add("g", lambda e: e.memset(hcar.ap, 0.0), writes=[hcar])
                    do_head(*args_of(ti))
                    if ti > 0:
                        do_tail(*args_of(ti - 1))
                do_tail(*args_of(NTT - 1))

            sc.barrier()

        with contextlib.ExitStack() as stkB1:
            wgq = sb(stkB1, "wgq", [P, NDC * INNER], BF16)
            wuq = sb(stkB1, "wuq", [P, NDC * INNER], BF16)
            with contextlib.ExitStack() as stkQ:
                wtmp = [sb(stkQ, f"wtmp{i}", [P, INNER]) for i in range(2)]
                wqp = sb(stkQ, "wqp", [P, INNER])
                psW = psb(stkQ, "psW", [P, 8])
                weight_quant(h["wgT"], wgq, wtmp, wqp, 0, NDC, INNER, psW)
                weight_quant(h["wuT"], wuq, wtmp, wqp, 1, NDC, INNER, psW)
                sc.barrier()

            xq1c = [sb(stkB1, f"xq1c{i}", [P, NDC * TA], BF16) for i in range(2)]
            gate = sb(stkB1, "gate", [P, INNER])
            prod = sb(stkB1, "prod", [P, INNER])
            xq2b = sb(stkB1, "xq2b", [P, INNER], BF16)
            xq2T = sb(stkB1, "xq2T", [P, NIC * TA], BF16)
            psG = psb(stkB1, "psG", [P, 512])
            psU = psb(stkB1, "psU", [P, 512])
            psT0b = psb(stkB1, "psT0b", [P, 512], BF16)
            psT1b = psb(stkB1, "psT1b", [P, 512], BF16)
            sgc = sb(stkB1, "sgc", [P, 2])

            for ti in range(NTT):
                xc = xq1c[ti % 2]
                dma(xc.ap.rearrange("p (dc t) -> p dc t", dc=NDC),
                    h["xq1scr"].rearrange("(dc p) t -> p dc t", p=P)
                    [:, :, ti * TA:(ti + 1) * TA],
                    reads=[xq1scrB], writes=[xc])
                sc.add("g", lambda e, ti=ti: e.tensor_tensor(
                    sgc.ap[:, 0:1], mx1c.ap[:, ti:ti + 1], mgs.ap[:, 0:1], ALU.mult),
                    reads=[mx1c, mgs], writes=[sgc])
                sc.add("g", lambda e, ti=ti: e.tensor_tensor(
                    sgc.ap[:, 1:2], mx1c.ap[:, ti:ti + 1], mgs.ap[:, 1:2], ALU.mult),
                    reads=[mx1c, mgs], writes=[sgc])
                for oc in range(8):
                    for dc in range(NDC):
                        sc.add("t", lambda e, oc=oc, dc=dc, xc=xc: e.matmul(
                            psG.ap, xc.ap[:, dc * TA:(dc + 1) * TA],
                            wgq.ap[:, dc * INNER + oc * 512:dc * INNER + (oc + 1) * 512],
                            start=(dc == 0), stop=(dc == NDC - 1)),
                            reads=[xc, wgq], writes=[psG], inc=(dc == NDC - 1))
                    sc.add("s", lambda e, oc=oc: e.activation(
                        gate.ap[:, oc * 512:(oc + 1) * 512], psG.ap, AF.Silu,
                        bias=0.0, scale=sgc.ap[:, 0:1]),
                        reads=[psG, sgc], writes=[gate], cost=600)
                    for dc in range(NDC):
                        sc.add("t", lambda e, oc=oc, dc=dc, xc=xc: e.matmul(
                            psU.ap, xc.ap[:, dc * TA:(dc + 1) * TA],
                            wuq.ap[:, dc * INNER + oc * 512:dc * INNER + (oc + 1) * 512],
                            start=(dc == 0), stop=(dc == NDC - 1)),
                            reads=[xc, wuq], writes=[psU], inc=(dc == NDC - 1))
                    sc.add("v", lambda e, oc=oc: e.scalar_tensor_tensor(
                        prod.ap[:, oc * 512:(oc + 1) * 512], psU.ap, sgc.ap[:, 1:2],
                        gate.ap[:, oc * 512:(oc + 1) * 512], ALU.mult, ALU.mult),
                        reads=[psU, sgc, gate], writes=[prod], cost=700)
                sc.add("v", lambda e: e.tensor_reduce(
                    stat.ap[:, 5:6], prod.ap, AX.X, ALU.max,
                    apply_absolute_value=True), reads=[prod], writes=[stat], cost=7800)
                sc.add("g", lambda e, ti=ti: e.tensor_scalar(
                    mx2c.ap[:, ti:ti + 1], stat.ap[:, 5:6], 3.0e38, EPS,
                    ALU.min, ALU.max), reads=[stat], writes=[mx2c])
                sc.add("v", lambda e, ti=ti: e.reciprocal(
                    stat.ap[:, 6:7], mx2c.ap[:, ti:ti + 1]), reads=[mx2c], writes=[stat])
                sc.add("g", lambda e: e.tensor_scalar(
                    stat.ap[:, 7:8], stat.ap[:, 6:7], 127.0, 0.0, ALU.mult, ALU.add),
                    reads=[stat], writes=[stat])
                sc.add("s", lambda e: e.activation(
                    gate.ap, prod.ap, AF.Identity, bias=0.0, scale=stat.ap[:, 7:8]),
                    reads=[prod, stat], writes=[gate], cost=3600)
                sc.add("v", lambda e: e.tensor_scalar(
                    gate.ap, gate.ap, ceff.ap[:, 0:1], ceff.ap[:, 0:1],
                    ALU.add, ALU.subtract), reads=[gate, ceff], writes=[gate], cost=4400)
                sc.add("g", lambda e: e.tensor_scalar(
                    xq2b.ap, gate.ap, 127.0, -128.0, ALU.min, ALU.max),
                    reads=[gate], writes=[xq2b], cost=7800)
                for q8 in range(8):
                    pst = [psT0b, psT1b][q8 % 2]
                    for j in range(4):
                        ocb = q8 * 4 + j
                        sc.add("t", lambda e, ocb=ocb, j=j, pst=pst: e.transpose(
                            pst.ap[:, j * P:(j + 1) * P],
                            xq2b.ap[:, ocb * P:(ocb + 1) * P], identb.ap),
                            reads=[xq2b, identb], writes=[pst], inc=(j == 3))
                    sc.add("s", lambda e, q8=q8, pst=pst: e.copy(
                        xq2T.ap[:, q8 * 512:(q8 + 1) * 512], pst.ap),
                        reads=[pst], writes=[xq2T])
                dma(h["xq2scr"].rearrange("(kc p) t -> p kc t", p=P)
                    [:, :, ti * TA:(ti + 1) * TA],
                    xq2T.ap.rearrange("p (kc t) -> p kc t", kc=NIC),
                    reads=[xq2T], writes=[xq2scrB])
            sc.barrier()

        with contextlib.ExitStack() as stkB2:
            wdq = sb(stkB2, "wdq", [P, NIC * DM], BF16)
            with contextlib.ExitStack() as stkQ2:
                wtmp2 = [sb(stkQ2, f"wtmp2_{i}", [P, DM]) for i in range(2)]
                wqp2 = sb(stkQ2, "wqp2", [P, DM])
                psW2 = psb(stkQ2, "psW2", [P, 8])
                weight_quant(h["wdT"], wdq, wtmp2, wqp2, 2, NIC, DM, psW2)
                sc.barrier()

            xq2c = [sb(stkB2, f"xq2c{i}", [P, NIC * TA], BF16) for i in range(2)]
            x2c = [sb(stkB2, f"x2c{i}", [P, DM]) for i in range(2)]
            outb = sb(stkB2, "outb", [P, DM])
            psD0 = psb(stkB2, "psD0", [P, 512])
            psD1 = psb(stkB2, "psD1", [P, 512])
            sdc = sb(stkB2, "sdc", [P, 1])

            for ti in range(NTT):
                xc = xq2c[ti % 2]
                x2 = x2c[ti % 2]
                dma(xc.ap.rearrange("p (kc t) -> p kc t", kc=NIC),
                    h["xq2scr"].rearrange("(kc p) t -> p kc t", p=P)
                    [:, :, ti * TA:(ti + 1) * TA],
                    reads=[xq2scrB], writes=[xc])
                dma(x2.ap, h["x2scr"][ti * TA:(ti + 1) * TA, :],
                    reads=[x2scrB], writes=[x2])
                sc.add("g", lambda e, ti=ti: e.tensor_tensor(
                    sdc.ap, mx2c.ap[:, ti:ti + 1], mgs.ap[:, 2:3], ALU.mult),
                    reads=[mx2c, mgs], writes=[sdc])
                for oc2 in range(2):
                    pm = [psD0, psD1][oc2]
                    for kc in range(NIC):
                        sc.add("t", lambda e, oc2=oc2, kc=kc, xc=xc, pm=pm: e.matmul(
                            pm.ap, xc.ap[:, kc * TA:(kc + 1) * TA],
                            wdq.ap[:, kc * DM + oc2 * 512:kc * DM + (oc2 + 1) * 512],
                            start=(kc == 0), stop=(kc == NIC - 1)),
                            reads=[xc, wdq], writes=[pm], inc=(kc == NIC - 1))
                    sc.add("v", lambda e, oc2=oc2, x2=x2, pm=pm: e.scalar_tensor_tensor(
                        x2.ap[:, oc2 * 512:(oc2 + 1) * 512], pm.ap, sdc.ap[:, 0:1],
                        x2.ap[:, oc2 * 512:(oc2 + 1) * 512], ALU.mult, ALU.add),
                        reads=[pm, sdc, x2], writes=[x2])
                sc.add("s", lambda e, x2=x2: e.activation(
                    outb.ap, x2.ap, AF.Identity, bias=0.0, scale=1.0,
                    accum_out=stat.ap[:, 0:1]), reads=[x2], writes=[outb, stat], cost=1100)
                sc.add("s", lambda e, x2=x2: e.activation(
                    outb.ap, x2.ap, AF.Square, bias=0.0, scale=1.0,
                    accum_out=stat.ap[:, 1:2]), reads=[x2], writes=[outb, stat], cost=1100)
                ln_stats(sc, stat, DM, epsc)
                sc.add("g", lambda e, x2=x2: e.tensor_scalar(
                    outb.ap, x2.ap, stat.ap[:, 2:3], stat.ap[:, 4:5],
                    ALU.add, ALU.mult), reads=[x2, stat], writes=[outb], cost=1100)
                dma(h["out"][ti * TA:(ti + 1) * TA, :], outb.ap,
                    reads=[outb], writes=[outB])

        import os
        with nc.Block() as block:
            sc.emit(block, sems)
            if os.environ.get("K_DEBUG"):
                print("sched est span: %.2f ms" % (sc.est_span / 1e6))
            def final_section(eng):
                for (dk, val) in sc.final_dma:
                    eng.wait_ge(sems[dk], val)
            block.sync(final_section)
    return nc


_NC_CACHE = [None]


def kernel(**inputs):
    import ml_dtypes
    f32 = np.float32
    x = np.ascontiguousarray(np.asarray(inputs["L0_attn_x"], dtype=f32))

    def t(a):
        return np.ascontiguousarray(np.asarray(a, dtype=f32).T)

    def col8(v):
        return np.ascontiguousarray(np.asarray(v, dtype=f32).reshape(NDC, P).T)

    A8 = np.ascontiguousarray(
        np.asarray(inputs["L0_attn_A_log"], dtype=f32).reshape(NDC, P, NST)
        .transpose(1, 0, 2).reshape(P, NDC * NST))
    base = dict(
        wpT=t(inputs["L0_attn_proj_W"]), wfT=t(inputs["L0_attn_fconv_W"]),
        wbT=t(inputs["L0_attn_bconv_W"]), wdbcT=t(inputs["L0_attn_deltaBC_W"]),
        wdtT=t(inputs["L0_attn_dt_W"]), wgT=t(inputs["L0_ffn_w_gate"]),
        wuT=t(inputs["L0_ffn_w_up"]), wdT=t(inputs["L0_ffn_w_down"]),
        A_log8=A8,
        gamma8=col8(inputs["L0_attn_ln_w"]), beta8=col8(inputs["L0_attn_ln_b"]),
        bp8=col8(inputs["L0_attn_proj_b"]), bf8=col8(inputs["L0_attn_fconv_b"]),
        bb8=col8(inputs["L0_attn_bconv_b"]), dtb8=col8(inputs["L0_attn_dt_b"]),
        D8=col8(inputs["L0_attn_D"]),
        ident=np.eye(P, dtype=f32),
        identb=np.eye(P, dtype=ml_dtypes.bfloat16),
        ones=np.ones((P, P), dtype=f32),
    )
    in_maps = []
    for core in range(8):
        m = dict(base)
        m["x"] = np.ascontiguousarray(x[core * SPC:(core + 1) * SPC])
        in_maps.append(m)

    if _NC_CACHE[0] is None:
        _NC_CACHE[0] = build_nc()
    res = run_bass_kernel_spmd(_NC_CACHE[0], in_maps, core_ids=list(range(8)))
    out = np.concatenate([r["out"].reshape(SPC, SEQ, DM) for r in res.results], axis=0)
    return np.ascontiguousarray(out.astype(np.float32))
